# version 06
# baseline (speedup 1.0000x reference)
import numpy as np
import ml_dtypes
from contextlib import ExitStack

import concourse.bass as bass
import concourse.mybir as mybir
from concourse.bass_utils import run_bass_kernel_spmd

F32 = mybir.dt.float32
BF16 = mybir.dt.bfloat16
I32 = mybir.dt.int32
AF = mybir.ActivationFunctionType
ALU = mybir.AluOpType

B, T, E = 4, 4096, 1024
NH, D, HP, HC = 16, 64, 8, 8
NTO = 16
NQ = NTO * 128
NE = 18
FF = 512
CSHIFT = -20.0
I1024 = 1.0 / 1024.0


class Sem:
    def __init__(self, es, nc, name):
        self.h = es.enter_context(nc.semaphore(name))
        self.n = 0


def build_nc(debug=False, phases=5, qtiles=NTO, quarters=4, do_att=True, kvp='tmve'):
    nc = bass.Bass("TRN2", target_bir_lowering=False, detect_race_conditions=False)
    es = ExitStack()

    d_xseq = nc.dram_tensor("xseq", [T, E], F32, kind="ExternalInput").ap()
    d_xown = nc.dram_tensor("xown", [NQ, E], F32, kind="ExternalInput").ap()
    d_posq = nc.dram_tensor("posq", [128, NTO], F32, kind="ExternalInput").ap()
    d_masks = nc.dram_tensor("masks", [8, 128, 128], F32, kind="ExternalInput").ap()
    d_w1t = nc.dram_tensor("w1t", [E, 3 * E], F32, kind="ExternalInput").ap()
    d_w2t = nc.dram_tensor("w2t", [E, E], F32, kind="ExternalInput").ap()
    d_wgt = nc.dram_tensor("wgt", [E, 16], F32, kind="ExternalInput").ap()
    d_w1a = nc.dram_tensor("w1a", [NE, E, FF], F32, kind="ExternalInput").ap()
    d_w3a = nc.dram_tensor("w3a", [NE, E, FF], F32, kind="ExternalInput").ap()
    d_w2a = nc.dram_tensor("w2a", [NE, FF, E], F32, kind="ExternalInput").ap()
    d_y = nc.dram_tensor("y", [NQ, E], F32, kind="ExternalOutput").ap()
    dbg = {}
    if debug:
        for name, shape in [
            ("dbg_qt", [128, HP * 2048]), ("dbg_kt", [128, HP * 1024]),
            ("dbg_v65", [128, 8 * 1040]), ("dbg_attn", [128, NTO * 1024]),
            ("dbg_x1", [128, 8 * 1024]), ("dbg_comb", [128, 8 * NE]),
            ("dbg_qtab", [128, NTO * 128]), ("dbg_x2t", [128, 8 * 1024]),
            ("dbg_cst", [128, 8]), ("dbg_invf", [128, 64]),
            ("dbg_invf2", [128, 64]), ("dbg_tgi", [128, 128]),
        ]:
            dbg[name] = nc.dram_tensor(name, shape, F32, kind="ExternalOutput").ap()

    def sb(name, shape, dt):
        return es.enter_context(nc.sbuf_tensor(name, shape, dt))

    cst = sb("cst", [128, 8], F32)
    C_ONE, C_2PI, C_I2PI, C_LN1E4, C_NEGC, C_E4, C_EPS, C_PI = 0, 1, 2, 3, 4, 5, 6, 7
    invf = sb("invf", [128, 64], F32)
    id_f32 = sb("id_f32", [128, 128], F32)
    id_bf = sb("id_bf", [128, 128], BF16)
    masks = sb("masksb", [128, 8 * 128], BF16)
    qtab = sb("qtab", [128, NTO * 128], F32)
    ktab = sb("ktab", [128, 8 * 128], F32)
    comb = sb("comb", [128, NTO * NE], F32)
    pden = sb("pden", [128, 3 * 16 * 4], F32)
    xbt = sb("xbt", [128, 4 * 1024], BF16)
    xk_bf = xbt[:, 0:2048]
    xkT = xbt[:, 2048:4096]
    kro = sb("kro", [128, 2 * 1024], BF16)
    ptb = sb("ptb", [128, 4 * 512], BF16)
    rope_t1 = sb("rope_t1", [128, 1024], BF16)
    rope_t2 = sb("rope_t2", [128, 1024], BF16)
    tgen = sb("tgen", [128, 8 * 64], F32)
    tgi = sb("tgi", [128, 4 * 64], I32)
    idsc = tgi[:, 0:128]
    tposi = sb("tposi", [128, 8], I32)
    poscol = sb("poscol", [128, 16], F32)
    fnum = sb("fnum", [128, 2 * 256], F32)
    dcol = sb("dcol", [128, 32], F32)
    sq = sb("sq", [128, 1024], F32)
    lncol = sb("lncol", [128, 16], F32)
    gsc = sb("gsc", [128, 8 * 16], F32)
    gw = None
    G_PS, G_GS, G_GS2, G_CNT, G_ALW, G_MSK, G_RANK, G_SEL, G_WR, G_DEN = \
        0, 24, 28, 36, 40, 44, 60, 76, 92, 108
    gw = ptb[:].bitcast(F32)
    gcmp = rope_t1[:].bitcast(F32)[:, 0:256]
    silu_t = kro[:, 0:1024]
    xow = xbt[:].bitcast(F32)
    r1b = sb("r1b", [128, 2 * 1024], F32)
    yb = r1b
    wgt_f = sb("wgt_f", [128, HC * 16], F32)

    wkv = sb("wkv", [128, HC * 2048], BF16)
    ktq = sb("ktq", [128, HP * 1024], BF16)
    v65 = sb("v65", [128, 8 * 1040], BF16)
    qt = sb("qt", [128, HP * 2048], BF16)
    pnum = sb("pnum", [128, 3 * 16 * 256], BF16)
    attn = sb("attn", [128, NTO * 1024], BF16)

    x1res = wkv[:].bitcast(F32)
    x2t = ktq[:]
    w2bf = v65[:, 0:8192]
    w13 = qt[:]
    wq_bf = pnum[:, 0:8192]
    w2t_bf = pnum[:, 0:8192]
    atT = pnum[:, 8192:10240]
    x1t_f = pnum[:, 10240:12288].bitcast(F32)

    ps01 = es.enter_context(nc.psum_tensor("ps01", [128, 1024], F32))
    ps23 = es.enter_context(nc.psum_tensor("ps23", [128, 1024], F32))
    ps45 = es.enter_context(nc.psum_tensor("ps45", [128, 1024], F32))
    ps6 = es.enter_context(nc.psum_tensor("ps6", [128, 512], F32))
    ps7 = es.enter_context(nc.psum_tensor("ps7", [128, 512], F32))
    ps45b = ps45[:].bitcast(BF16)

    sS = Sem(es, nc, "sS")
    sG = Sem(es, nc, "sG")
    sT = Sem(es, nc, "sT")
    sV = Sem(es, nc, "sV")
    sC = Sem(es, nc, "sC")
    sGc = Sem(es, nc, "sGc")
    sOut = Sem(es, nc, "sOut")

    stream = {e: [] for e in ("sync", "gpsimd", "tensor", "vector", "scalar")}

    def emit(eng, fn, waits=(), incs=(), dma=False):
        for s, v in waits:
            assert v <= s.n, f"wait on future event: {v} > {s.n} ({eng})"
        for s, k in incs:
            s.n += k
        w = tuple((s.h, v) for s, v in waits if v > 0)
        i = tuple((s.h, k) for s, k in incs)
        drain_incs = (not dma) and eng in ("vector", "scalar", "gpsimd") and i

        def thunk(h, w=w, i=i, fn=fn, drain_incs=drain_incs):
            for sem, v in w:
                h.wait_ge(sem, v)
            inst = fn(h)
            if drain_incs:
                d = h.drain(fusable=False)
                for sem, k in i:
                    d.then_inc(sem, k)
            else:
                for sem, k in i:
                    inst.then_inc(sem, k)
        stream[eng].append(thunk)

    def T1(fn, waits=()):
        emit("tensor", fn, waits, [(sT, 1)])
        return sT.n

    def V1(fn, waits=()):
        emit("vector", fn, waits, [(sV, 1)])
        return sV.n

    def V0(fn, waits=()):
        emit("vector", fn, waits)

    def S1(fn, waits=()):
        emit("scalar", fn, waits, [(sC, 1)])
        return sC.n

    def S0(fn, waits=()):
        emit("scalar", fn, waits)

    def G1(fn, waits=()):
        emit("gpsimd", fn, waits, [(sGc, 1)])
        return sGc.n

    def G0(fn, waits=()):
        emit("gpsimd", fn, waits)

    def dmaY(out, in_, waits=(), store=False):
        emit("sync", lambda h: h.dma_start(out=out, in_=in_), waits,
             [(sOut if store else sS, 16)], dma=True)
        return (sOut if store else sS).n

    def dmaG(out, in_, waits=()):
        emit("gpsimd", lambda h: h.dma_start(out=out, in_=in_), waits, [(sG, 16)],
             dma=True)
        return sG.n

    G0(lambda h: h.memset(cst[:, C_ONE:C_ONE + 1], 1.0))
    G0(lambda h: h.memset(cst[:, C_NEGC:C_NEGC + 1], CSHIFT))
    G0(lambda h: h.memset(cst[:, C_EPS:C_EPS + 1], 1e-5))
    e0 = G1(lambda h: h.memset(cst[:, C_E4:C_E4 + 1], 10000.0))
    S0(lambda h: h.activation(cst[:, C_2PI:C_2PI + 1], cst[:, C_ONE:C_ONE + 1],
                              AF.Arctan), waits=[(sGc, e0)])
    e1 = S1(lambda h: h.activation(cst[:, C_LN1E4:C_LN1E4 + 1],
                                   cst[:, C_E4:C_E4 + 1], AF.Ln))
    e_rc0 = V1(lambda h: h.reciprocal(cst[:, C_I2PI:C_I2PI + 1],
                                      cst[:, C_2PI:C_2PI + 1]),
               waits=[(sC, e1)])
    V0(lambda h: h.tensor_scalar(cst[:, C_2PI:C_2PI + 1], cst[:, C_2PI:C_2PI + 1],
                                 8.0, None, ALU.mult), waits=[(sV, e_rc0)])
    V0(lambda h: h.tensor_scalar(cst[:, C_I2PI:C_I2PI + 1],
                                 cst[:, C_I2PI:C_I2PI + 1], 0.125, None,
                                 ALU.mult), waits=[(sV, e_rc0)])
    e_cst = V1(lambda h: h.tensor_scalar(cst[:, C_PI:C_PI + 1],
                                         cst[:, C_2PI:C_2PI + 1], 0.5, None,
                                         ALU.mult))

    e2 = G1(lambda h: h.iota(tgi[:, 0:32], pattern=[[1, 32]], base=0,
                             channel_multiplier=0))
    V0(lambda h: h.tensor_copy(invf[:, 0:32], tgi[:, 0:32]), waits=[(sGc, e2)])
    V0(lambda h: h.tensor_scalar(invf[:, 0:32], invf[:, 0:32], -0.03125, None,
                                 ALU.mult))
    e3 = V1(lambda h: h.tensor_scalar(invf[:, 0:32], invf[:, 0:32],
                                      cst[:, C_LN1E4:C_LN1E4 + 1], None, ALU.mult),
            waits=[(sC, e1)])
    e4 = S1(lambda h: h.activation(invf[:, 0:32], invf[:, 0:32], AF.Exp),
            waits=[(sV, e3)])
    e_invf = V1(lambda h: h.tensor_copy(invf[:, 32:64], invf[:, 0:32]),
                waits=[(sC, e4)])

    if debug:
        dmaG(dbg["dbg_invf2"], invf[:], waits=[(sV, e_invf)])
    e5 = G1(lambda h: h.iota(idsc[:], pattern=[[1, 128]], base=0,
                             channel_multiplier=-1), waits=[(sV, e3)])
    V0(lambda h: h.tensor_copy(id_f32[:], idsc[:]), waits=[(sGc, e5)])
    e_idf = V1(lambda h: h.tensor_scalar(id_f32[:], id_f32[:], 0.0, None,
                                         ALU.is_equal))
    e_id = V1(lambda h: h.tensor_copy(id_bf[:], id_f32[:]), waits=[(sV, e_idf)])

    e_masks = dmaG(masks[:].rearrange("p (i c) -> p i c", i=8),
                   d_masks[:, :, :].transpose([1, 0, 2]))
    e_comb1 = G1(lambda h: h.memset(
        comb[:].rearrange("p (t e) -> p t e", e=NE)[:, :, 16:18], 1.0))

    if phases < 1:
        if debug:
            G0(lambda h: h.tensor_copy(tgen[:, 0:128].bitcast(I32) if False else tgen[:, 0:128], tgi[:].bitcast(F32)) if False else h.nop())
            dmaG(dbg["dbg_tgi"], tgi[:].bitcast(F32), waits=[(sV, e_invf)])
            dmaG(dbg["dbg_invf"], invf[:], waits=[(sV, e_invf)])
            dmaG(dbg["dbg_cst"], cst[:], waits=[(sV, e_invf)])
        emit("sync", lambda h: h.nop(), waits=[(sS, sS.n)])
        emit("gpsimd", lambda h: h.nop(), waits=[(sG, sG.n)])
        with nc.Block() as block:
            for name in ("sync", "gpsimd", "tensor", "vector", "scalar"):
                thunks = stream[name]
                def runner(h, thunks=thunks):
                    for t in thunks:
                        t(h)
                getattr(block, name)(runner)
        es.close()
        return nc

    def gen_tables_batch(tab, tiles, pcols, extra_waits, prev_events):
        n = len(tiles)
        ang = [tab[:, j * 128:j * 128 + 64] for j in tiles]
        scr = [tab[:, j * 128 + 64:j * 128 + 128] for j in tiles]
        rr = [tgen[:, i * 128:i * 128 + 64] for i in range(n)]
        sh = [tgen[:, i * 128 + 64:i * 128 + 128] for i in range(n)]
        ii = [tgi[:, i * 64:(i + 1) * 64] for i in range(n)]
        ev = 0

        def layer(fn_per_tile, waits=()):
            for i in range(n):
                (V1 if i == n - 1 else V0)(fn_per_tile(i),
                                           waits=list(waits) if i == 0 else ())
            return sV.n

        w0 = [(sV, e_invf), (sV, e_cst)] + list(extra_waits) + list(prev_events)
        ev = layer(lambda i: (lambda h, i=i: h.tensor_scalar(
            ang[i], invf[:, 0:64], pcols[i], None, ALU.mult)), w0)
        ev = layer(lambda i: (lambda h, i=i: h.tensor_scalar(
            scr[i], ang[i], cst[:, C_I2PI:C_I2PI + 1], None, ALU.mult)),
            [(sV, ev)])
        ev = layer(lambda i: (lambda h, i=i: h.tensor_copy(ii[i], scr[i])),
                   [(sV, ev)])
        ev = layer(lambda i: (lambda h, i=i: h.tensor_copy(scr[i], ii[i])),
                   [(sV, ev)])
        ev = layer(lambda i: (lambda h, i=i: h.tensor_scalar(
            scr[i], scr[i], cst[:, C_2PI:C_2PI + 1], None, ALU.mult)), [(sV, ev)])
        ev = layer(lambda i: (lambda h, i=i: h.tensor_sub(rr[i], ang[i], scr[i])),
                   [(sV, ev)])
        ev = layer(lambda i: (lambda h, i=i: h.tensor_scalar(
            sh[i], rr[i], cst[:, C_PI:C_PI + 1], None, ALU.is_gt)), [(sV, ev)])
        ev = layer(lambda i: (lambda h, i=i: h.tensor_scalar(
            sh[i], sh[i], cst[:, C_2PI:C_2PI + 1], None, ALU.mult)), [(sV, ev)])
        ev = layer(lambda i: (lambda h, i=i: h.tensor_sub(rr[i], rr[i], sh[i])),
                   [(sV, ev)])
        ev = layer(lambda i: (lambda h, i=i: h.scalar_tensor_tensor(
            sh[i], rr[i], -1.0,
            cst[:, C_PI:C_PI + 1].broadcast_to([128, 64]), ALU.mult, ALU.is_gt)),
            [(sV, ev)])
        ev = layer(lambda i: (lambda h, i=i: h.tensor_scalar(
            sh[i], sh[i], cst[:, C_2PI:C_2PI + 1], None, ALU.mult)), [(sV, ev)])
        ev_r = layer(lambda i: (lambda h, i=i: h.tensor_add(rr[i], rr[i], sh[i])),
                     [(sV, ev)])
        for i in range(n):
            S0(lambda h, i=i: h.activation(scr[i], rr[i], AF.Sin),
               waits=[(sV, ev_r)] if i == 0 else ())
        for i in range(n):
            (S1 if i == n - 1 else S0)(lambda h, i=i: h.activation(
                sh[i], rr[i], AF.Sin, scale=0.5))
        ev_s = sC.n
        ev = layer(lambda i: (lambda h, i=i: h.tensor_mul(sh[i], sh[i], sh[i])),
                   [(sC, ev_s)])
        ev = layer(lambda i: (lambda h, i=i: h.tensor_scalar(
            ang[i], sh[i], -2.0, 1.0, ALU.mult, ALU.add)), [(sV, ev)])
        ev = layer(lambda i: (lambda h, i=i: h.tensor_scalar(
            tab[:, tiles[i] * 128 + 64:tiles[i] * 128 + 96],
            tab[:, tiles[i] * 128 + 64:tiles[i] * 128 + 96],
            -1.0, None, ALU.mult)), [(sV, ev)])
        return ev, ev_s

    def rope_emit(psrc, tab, off, waits, out_bf):
        ps3 = psrc.rearrange("p (h d) -> p h d", d=64)
        t13 = rope_t1[:, 0:1024].rearrange("p (h d) -> p h d", d=64)
        V0(lambda h: h.tensor_tensor(
            t13[:, :, 0:32], ps3[:, :, 32:64],
            tab[:, off + 64:off + 96].unsqueeze(1).broadcast_to([128, 16, 32]),
            ALU.mult), waits=waits)
        V0(lambda h: h.tensor_tensor(
            t13[:, :, 32:64], ps3[:, :, 0:32],
            tab[:, off + 96:off + 128].unsqueeze(1).broadcast_to([128, 16, 32]),
            ALU.mult))
        ev_m = V1(lambda h: h.tensor_tensor(
            rope_t2[:, 0:1024].rearrange("p (h d) -> p h d", d=64), ps3,
            tab[:, off:off + 64].unsqueeze(1).broadcast_to([128, 16, 64]),
            ALU.mult))
        return V1(lambda h: h.tensor_add(out_bf, rope_t1[:, 0:1024],
                                         rope_t2[:, 0:1024]),
                  waits=[(sV, ev_m)])

    def transpose8(src, isbf, waits_first, evac_aps, ev_prev_reads,
                   extra_evac=None):
        ident = id_bf if isbf else id_f32
        psv = ps45b if isbf else ps45[:]
        bank_off = 1024 if isbf else 512
        evts = [0, 0]
        for c in range(8):
            off = (c % 4) * 128 + (0 if c < 4 else bank_off)
            w = list(waits_first) if c == 0 else []
            if c == 0 and ev_prev_reads[0]:
                w.append((sV, ev_prev_reads[0]))
            if c == 4 and ev_prev_reads[1]:
                w.append((sV, ev_prev_reads[1]))
            ev = T1(lambda h, off=off, c=c: h.transpose(
                psv[:, off:off + 128], src[:, c * 128:(c + 1) * 128], ident[:]), w)
            if c == 3:
                evts[0] = ev
            if c == 7:
                evts[1] = ev
        out = []
        for half in range(2):
            pslice = psv[:, half * bank_off:half * bank_off + 512]
            if extra_evac is None:
                out.append(V1(lambda h, pslice=pslice, half=half: h.tensor_copy(
                    evac_aps[half], pslice), waits=[(sT, evts[half])]))
            else:
                V0(lambda h, pslice=pslice, half=half: h.tensor_copy(
                    evac_aps[half], pslice), waits=[(sT, evts[half])])
                out.append(V1(lambda h, pslice=pslice, half=half: h.tensor_copy(
                    extra_evac[half], pslice)))
        return out

    e_wq = dmaG(wq_bf.rearrange("p (c n) -> p c n", c=HC),
                d_w1t[:, 0:E].rearrange("(c p) n -> p c n", p=128))

    qt3 = qt[:].rearrange("p (hp q) -> p hp q", hp=HP)
    e_posq = dmaY(poscol[:, 0:16], d_posq[:])
    gen_ev, gen_sc = 0, 0
    for bb in range(4):
        tiles = list(range(bb * 4, (bb + 1) * 4))
        pcols = [poscol[:, j:j + 1] for j in tiles]
        gen_ev, gen_sc = gen_tables_batch(
            qtab, tiles, pcols, [(sS, e_posq)],
            [(sV, gen_ev), (sC, gen_sc)] if bb else [(sV, e_id)])
    e_qtab = gen_ev
    e_qtab_t = [e_qtab] * NTO

    ev_xqT = [[0, 0], [0, 0]]
    ev_qt_ev = [0, 0]
    ev_rope_prev = 0
    for j in range(qtiles):
        r = j % 2
        ev_x = dmaG(xk_bf[:, r * 1024:(r + 1) * 1024],
                    d_xown[j * 128:(j + 1) * 128, :],
                    waits=[(sV, ev_xqT[r][1])])
        ev_xqT[r] = transpose8(
            xk_bf[:, r * 1024:(r + 1) * 1024], True,
            [(sG, ev_x), (sV, e_id)],
            (xkT[:, r * 1024:r * 1024 + 512], xkT[:, r * 1024 + 512:r * 1024 + 1024]),
            (ev_qt_ev[0], ev_qt_ev[1]))
        ev_mm = 0
        w = [(sV, ev_xqT[r][1]), (sG, e_wq), (sV, ev_rope_prev)]
        for c in range(HC):
            for half in range(2):
                ev_mm = T1(lambda h, r=r, c=c, half=half: h.matmul(
                    ps01[:, half * 512:(half + 1) * 512],
                    xkT[:, r * 1024 + c * 128:r * 1024 + (c + 1) * 128],
                    wq_bf[:, c * 1024 + half * 512:c * 1024 + half * 512 + 512],
                    start=(c == 0), stop=(c == HC - 1)), w)
                w = []
        ev_rope_prev = rope_emit(ps01[:], qtab, j * 128,
                                 [(sT, ev_mm), (sV, e_qtab_t[j])],
                                 kro[:, r * 1024:(r + 1) * 1024])
        ev_qt_ev = transpose8(
            kro[:, r * 1024:(r + 1) * 1024], True, [(sV, ev_rope_prev)],
            (qt3[:, 0:4, j * 128:(j + 1) * 128], qt3[:, 4:8, j * 128:(j + 1) * 128]),
            (ev_qt_ev[0], ev_qt_ev[1]))
    e_qt_done = max(ev_qt_ev)
    e_q_rope_last = sV.n
    if debug:
        dmaG(dbg["dbg_qt"], qt[:], waits=[(sV, e_qt_done)])
        dmaG(dbg["dbg_qtab"], qtab[:], waits=[(sV, e_qt_done)])
        dmaG(dbg["dbg_cst"], cst[:], waits=[(sV, e_qt_done)])
        dmaG(dbg["dbg_invf"], invf[:], waits=[(sV, e_qt_done)])

    if phases < 3:
        emit("sync", lambda h: h.nop(), waits=[(sS, sS.n)])
        emit("gpsimd", lambda h: h.nop(), waits=[(sG, sG.n)])
        with nc.Block() as block:
            for name in ("sync", "gpsimd", "tensor", "vector", "scalar"):
                thunks = stream[name]
                def runner(h, thunks=thunks):
                    for t in thunks:
                        t(h)
                getattr(block, name)(runner)
        es.close()
        return nc
    import os
    e_wkv = (dmaG(wkv[:].rearrange("p (c n) -> p c n", c=HC),
                  d_w1t[:, E:3 * E].rearrange("(c p) n -> p c n", p=128))
             if os.environ.get("K_NOWKV") != "1" else 0)

    pnum3 = pnum[:].rearrange("p (s n) -> p s n", n=256)
    pden3 = pden[:].rearrange("p (s n) -> p s n", n=4)
    pn_first = {}
    ev_slot = {}
    v65_4 = v65[:].rearrange("p (t h o) -> p t h o", t=8, o=65)
    ktq3 = ktq[:].rearrange("p (hp q) -> p hp q", hp=HP)

    e_attn_done = [0] * NTO
    ev_av_evac = [0, 0]
    ev_dcol_free = [0, 0]
    e_kv_evac = [e_q_rope_last, e_q_rope_last]
    ev_kT_ev = [0, 0]
    ev_xsT = [[0, 0], [0, 0]]
    ev_exp_bank = [0, 0]
    ev_pt_used = [0, 0, 0, 0]
    last_att_T = 0

    for qi, qr in enumerate([3, 2, 1, 0][:quarters]):
        ev_v65_ones_v = V1(lambda h: h.memset(v65_4[:, :, :, 64:65], 1.0),
                           waits=[(sT, last_att_T)])
        ev_v65_ones = G1(lambda h: h.nop(), waits=[(sV, ev_v65_ones_v)])
        if 't' in kvp:
            for sl in range(8):
                gt = qr * 8 + sl
                (G1 if sl == 7 else G0)(lambda h, gt=gt, sl=sl: h.iota(
                    tposi[:, sl:sl + 1], pattern=[[1, 1]], base=gt * 128,
                    channel_multiplier=1), waits=[(sV, gen_ev)] if sl == 0 else ())
            e6 = sGc.n
            evp = V1(lambda h: h.tensor_copy(poscol[:, 0:8], tposi[:, 0:8]),
                     waits=[(sGc, e6), (sV, gen_ev)])
            for bb in range(2):
                tiles = list(range(bb * 4, (bb + 1) * 4))
                pcols = [poscol[:, i:i + 1] for i in tiles]
                gen_ev, gen_sc = gen_tables_batch(
                    ktab, tiles, pcols, [(sV, evp)],
                    [(sV, gen_ev), (sC, gen_sc)])
            ev_ktab_done = gen_ev

        for sl in range(8):
            gt = qr * 8 + sl
            r = sl % 2
            ev_x = dmaG(xk_bf[:, r * 1024:(r + 1) * 1024],
                        d_xseq[gt * 128:(gt + 1) * 128, :],
                        waits=[(sV, ev_xsT[r][1]), (sV, ev_xqT[r][1])])
            ev_xsT[r] = transpose8(
                xk_bf[:, r * 1024:(r + 1) * 1024], True, [(sG, ev_x)],
                (xkT[:, r * 1024:r * 1024 + 512],
                 xkT[:, r * 1024 + 512:r * 1024 + 1024]),
                (ev_kT_ev[0], ev_kT_ev[1]))
            ev_mm = 0
            w = [(sV, ev_xsT[r][1]), (sG, e_wkv), (sV, e_kv_evac[0]),
                 (sV, e_kv_evac[1]), (sV, ev_av_evac[0]), (sV, ev_av_evac[1])]
            if 'm' not in kvp:
                continue
            for c in range(HC):
                for half in range(2):
                    T1(lambda h, r=r, c=c, half=half: h.matmul(
                        ps01[:, half * 512:(half + 1) * 512],
                        xkT[:, r * 1024 + c * 128:r * 1024 + (c + 1) * 128],
                        wkv[:, c * 2048 + half * 512:c * 2048 + half * 512 + 512],
                        start=(c == 0), stop=(c == HC - 1)), w)
                    w = []
                    ev_mm = T1(lambda h, r=r, c=c, half=half: h.matmul(
                        ps23[:, half * 512:(half + 1) * 512],
                        xkT[:, r * 1024 + c * 128:r * 1024 + (c + 1) * 128],
                        wkv[:, c * 2048 + 1024 + half * 512:
                            c * 2048 + 1024 + half * 512 + 512],
                        start=(c == 0), stop=(c == HC - 1)))
            if 'v' in kvp:
              e_kv_evac[0] = V1(lambda h, sl=sl: h.tensor_copy(
                v65_4[:, sl, :, 0:64],
                ps23[:].rearrange("p (h o) -> p h o", o=64)),
                waits=[(sT, ev_mm), (sGc, ev_v65_ones)])
            ev_ro = rope_emit(ps01[:], ktab, sl * 128,
                              [(sT, ev_mm), (sV, ev_ktab_done)],
                              kro[:, r * 1024:(r + 1) * 1024])
            e_kv_evac[1] = ev_ro
            ev_kT_ev = transpose8(
                kro[:, r * 1024:(r + 1) * 1024], True, [(sV, ev_ro)],
                (ktq3[:, 0:4, sl * 128:(sl + 1) * 128],
                 ktq3[:, 4:8, sl * 128:(sl + 1) * 128]),
                (ev_kT_ev[0], ev_kT_ev[1]))
        e_ktq_done = max(ev_kT_ev)
        e_v65_done = e_kv_evac[0]
        if debug and qr == 0:
            dmaG(dbg["dbg_kt"], ktq[:], waits=[(sV, e_ktq_done)])
            dmaG(dbg["dbg_v65"], v65[:], waits=[(sV, e_v65_done)])

        for J in range(qr, 4):
            for hp in range(HP):
                for sub in range(2):
                    hh = hp * 2 + sub
                    avp = ps01 if hp % 2 == 0 else ps23
                    av_off = sub * 512
                    pt_base = 2 * sub
                    work = [(i, (i // 2 if J == qr else 0)) for i in range(8)]
                    n_av = sum(4 - cu for _, cu in work)
                    k_av = 0
                    ev_avmm = 0
                    for i, cu in work:
                        N = 512 - cu * 128
                        diag = (J == qr)
                        sps = ps6 if i % 2 == 0 else ps7
                        w = [(sC, ev_exp_bank[i % 2])]
                        if i == 0:
                            w += [(sV, e_ktq_done), (sV, e_v65_done),
                                  (sV, e_qt_done), (sG, e_masks)]
                        ev_smm = T1(lambda h, hp=hp, sub=sub, i=i, J=J, cu=cu,
                                    N=N, sps=sps: h.matmul(
                            sps[:, cu * 128:cu * 128 + N],
                            ktq[64 * sub:64 * (sub + 1),
                                hp * 1024 + i * 128:hp * 1024 + (i + 1) * 128],
                            qt[64 * sub:64 * (sub + 1),
                               hp * 2048 + J * 512 + cu * 128:
                               hp * 2048 + J * 512 + 512],
                            start=True, stop=True, tile_position=(64 * sub, 0)), w)
                        ptq = pt_base + i % 2
                        ptv = ptb[:, ptq * 512:(ptq + 1) * 512]
                        ev_exp = S1(lambda h, ptv=ptv, sps=sps, cu=cu, N=N:
                                    h.activation(
                            ptv[:, cu * 128:cu * 128 + N],
                            sps[:, cu * 128:cu * 128 + N], AF.Exp,
                            bias=cst[:, C_NEGC:C_NEGC + 1], scale=0.125),
                            waits=[(sT, ev_smm), (sT, ev_pt_used[ptq])])
                        ev_exp_bank[i % 2] = ev_exp
                        ev_msk = 0
                        if diag:
                            ev_msk = V1(lambda h, ptv=ptv, cu=cu, i=i:
                                        h.tensor_tensor(
                                ptv[:, cu * 128:cu * 128 + 128],
                                ptv[:, cu * 128:cu * 128 + 128],
                                masks[:, i * 128:(i + 1) * 128], ALU.mult),
                                waits=[(sC, ev_exp), (sG, e_masks)])
                        for c in range(cu, 4):
                            w2 = [(sC, ev_exp)]
                            if diag and c == cu:
                                w2.append((sV, ev_msk))
                            if k_av == 0:
                                w2.append((sV, ev_av_evac[hp % 2]))
                            st_flag = (k_av == 0)
                            sp_flag = (k_av == n_av - 1)
                            ev_avmm = T1(lambda h, avp=avp, av_off=av_off, c=c,
                                         ptv=ptv, i=i, hh=hh, st=st_flag,
                                         sp=sp_flag: h.matmul(
                                avp[:, av_off + c * 65:av_off + c * 65 + 65],
                                ptv[:, c * 128:(c + 1) * 128],
                                v65[:, i * 1040 + hh * 65:i * 1040 + hh * 65 + 65],
                                start=st, stop=sp, skip_group_check=True), w2)
                            ev_pt_used[ptq] = ev_avmm
                            k_av += 1
                    av4 = avp[:, av_off:av_off + 260].rearrange(
                        "p (c o) -> p c o", o=65)
                    avden = av4[:, :, 64:65].rearrange("p c o -> p (c o)")
                    avnum = av4[:, :, 0:64]
                    slot = (J - 1) * 16 + hh
                    ev_sl = ev_slot.get(slot, 0)
                    if qr > 0:
                        first = pn_first.setdefault(slot, True)
                        pn_first[slot] = False
                        pslot = pnum3[:, slot, :].rearrange("p (c o) -> p c o", o=64)
                        if first:
                            ev1 = V1(lambda h, pslot=pslot, avnum=avnum:
                                     h.tensor_copy(pslot, avnum),
                                     waits=[(sT, ev_avmm), (sV, ev_sl)])
                            ev2 = V1(lambda h, slot=slot, avden=avden:
                                     h.tensor_copy(pden3[:, slot, :], avden))
                        else:
                            ev1 = V1(lambda h, pslot=pslot, avnum=avnum:
                                     h.tensor_tensor(pslot, avnum, pslot, ALU.add),
                                     waits=[(sT, ev_avmm), (sV, ev_sl)])
                            ev2 = V1(lambda h, slot=slot, avden=avden:
                                     h.tensor_tensor(pden3[:, slot, :], avden,
                                                     pden3[:, slot, :], ALU.add))
                        ev_av_evac[hp % 2] = max(ev1, ev2)
                        ev_slot[slot] = max(ev1, ev2)
                    else:
                        dbase = sub * 16
                        if J > 0:
                            evd = V1(lambda h, slot=slot, avden=avden, dbase=dbase:
                                     h.tensor_tensor(dcol[:, dbase:dbase + 4],
                                                     avden, pden3[:, slot, :],
                                                     ALU.add),
                                     waits=[(sT, ev_avmm),
                                            (sC, ev_dcol_free[sub])])
                            evn = V1(lambda h, slot=slot, avnum=avnum, sub=sub:
                                     h.tensor_tensor(
                                fnum[:, sub * 256:(sub + 1) * 256].rearrange(
                                    "p (c o) -> p c o", o=64), avnum,
                                pnum3[:, slot, :].rearrange("p (c o) -> p c o",
                                                            o=64), ALU.add))
                        else:
                            evd = V1(lambda h, avden=avden, dbase=dbase:
                                     h.tensor_copy(dcol[:, dbase:dbase + 4], avden),
                                     waits=[(sT, ev_avmm),
                                            (sC, ev_dcol_free[sub])])
                            evn = V1(lambda h, avnum=avnum, sub=sub: h.tensor_copy(
                                fnum[:, sub * 256:(sub + 1) * 256].rearrange(
                                    "p (c o) -> p c o", o=64), avnum))
                        evr = 0
                        for c in range(4):
                            evr = V1(lambda h, dbase=dbase, c=c: h.reciprocal(
                                dcol[:, dbase + 8 + c:dbase + 9 + c],
                                dcol[:, dbase + c:dbase + c + 1]),
                                waits=[(sV, evn)])
                        ev_av_evac[hp % 2] = max(evn, evr)
                        for c in range(4):
                            tt = J * 4 + c
                            e_attn_done[tt] = S1(lambda h, tt=tt, hh=hh, sub=sub,
                                                 c=c, dbase=dbase: h.activation(
                                attn[:, tt * 1024 + hh * 64:
                                     tt * 1024 + hh * 64 + 64],
                                fnum[:, sub * 256 + c * 64:
                                     sub * 256 + (c + 1) * 64],
                                AF.Identity,
                                scale=dcol[:, dbase + 8 + c:dbase + 9 + c]),
                                waits=[(sV, evr)])
                        ev_dcol_free[sub] = sC.n
        last_att_T = sT.n
    ev_pnum_done = sV.n
    e_attn_all = max(e_attn_done)
    if debug:
        dmaG(dbg["dbg_attn"], attn[:], waits=[(sC, e_attn_all)])

    if phases < 4:
        emit("sync", lambda h: h.nop(), waits=[(sS, sS.n)])
        emit("gpsimd", lambda h: h.nop(), waits=[(sG, sG.n)])
        emit("scalar", lambda h: h.nop(), waits=[(sC, sC.n)])
        emit("vector", lambda h: h.nop(), waits=[(sV, sV.n)])
        with nc.Block() as block:
            for name in ("sync", "gpsimd", "tensor", "vector", "scalar"):
                thunks = stream[name]
                def runner(h, thunks=thunks):
                    for t in thunks:
                        t(h)
                getattr(block, name)(runner)
        es.close()
        return nc
    e_w2t = dmaG(w2t_bf.rearrange("p (c n) -> p c n", c=HC),
                 d_w2t[:].rearrange("(c p) n -> p c n", p=128),
                 waits=[(sV, ev_pnum_done), (sC, e_attn_all)])
    e_wgt = dmaY(wgt_f[:].rearrange("p (c n) -> p c n", c=HC),
                 d_wgt[:].rearrange("(c p) n -> p c n", p=128))

    x2t3 = x2t.rearrange("p (c t) -> p c t", c=HC)
    gu_bufs = (attn[:, 0:4096], attn[:, 4096:8192])

    ev_ln_free = [0, 0]
    ev_yb_free = [0, 0]

    def ln_emit(r1v, ev_r1_v, dst, cbase, waits_dst=()):
        evs1 = V1(lambda h: h.reduce_sum(lncol[:, cbase:cbase + 1], r1v,
                                         mybir.AxisListType.X),
                  waits=[(sV, ev_r1_v), (sC, ev_ln_free[cbase // 8])])
        evs2 = S1(lambda h: h.activation(sq[:], r1v, AF.Square,
                                         accum_out=lncol[:, cbase + 1:cbase + 2]),
                  waits=[(sV, ev_r1_v), (sC, ev_ln_free[cbase // 8])])
        V0(lambda h: h.tensor_scalar(lncol[:, cbase + 2:cbase + 3],
                                     lncol[:, cbase:cbase + 1], I1024, None,
                                     ALU.mult), waits=[(sV, evs1), (sC, evs2)])
        ev_m = V1(lambda h: h.tensor_scalar(lncol[:, cbase + 3:cbase + 4],
                                            lncol[:, cbase + 1:cbase + 2], I1024,
                                            None, ALU.mult))
        V0(lambda h: h.tensor_mul(lncol[:, cbase + 4:cbase + 5],
                                  lncol[:, cbase + 2:cbase + 3],
                                  lncol[:, cbase + 2:cbase + 3]),
           waits=[(sV, ev_m)])
        ev_mq = V1(lambda h: h.tensor_scalar(lncol[:, cbase + 5:cbase + 6],
                                             lncol[:, cbase + 2:cbase + 3], -1.0,
                                             None, ALU.mult))
        ev_var = V1(lambda h: h.tensor_sub(lncol[:, cbase + 3:cbase + 4],
                                           lncol[:, cbase + 3:cbase + 4],
                                           lncol[:, cbase + 4:cbase + 5]),
                    waits=[(sV, ev_mq)])
        ev_std = S1(lambda h: h.activation(lncol[:, cbase + 4:cbase + 5],
                                           lncol[:, cbase + 3:cbase + 4], AF.Sqrt,
                                           bias=cst[:, C_EPS:C_EPS + 1]),
                    waits=[(sV, ev_var)])
        ev_rstd = V1(lambda h: h.reciprocal(lncol[:, cbase + 6:cbase + 7],
                                            lncol[:, cbase + 4:cbase + 5]),
                     waits=[(sC, ev_std)])
        ev_p1 = S1(lambda h: h.activation(sq[:], r1v, AF.Identity,
                                          bias=lncol[:, cbase + 5:cbase + 6]),
                   waits=[(sV, ev_rstd)])
        ev = S1(lambda h: h.activation(dst, sq[:], AF.Identity,
                                       scale=lncol[:, cbase + 6:cbase + 7]),
                waits=[(sC, ev_p1)] + list(waits_dst))
        ev_ln_free[cbase // 8] = ev
        return ev

    def gate_batch(half, ev_sig_last, prev_gate_ev):
        gv = [gsc[:, t * 16:(t + 1) * 16] for t in range(8)]
        wv = [gw[:, t * 128:(t + 1) * 128] for t in range(8)]
        P_, GS, GS2, CNT, ALW, MSK, RNK, SEL, WR, DEN = \
            0, 24, 28, 36, 40, 44, 60, 76, 92, 108
        pairs = [(0, 1), (0, 2), (0, 3), (1, 2), (1, 3), (2, 3)]

        def layer(fn, waits=()):
            for t in range(8):
                (V1 if t == 7 else V0)(fn(t), waits=list(waits) if t == 0 else ())
            return sV.n

        def l1(t):
            def f(h, t=t):
                for k, (a, b) in enumerate(pairs):
                    h.tensor_tensor(wv[t][:, P_ + k * 4:P_ + (k + 1) * 4],
                                    gv[t][:, a:16:4], gv[t][:, b:16:4], ALU.add)
            return f
        ev = layer(l1, [(sC, ev_sig_last), (sV, prev_gate_ev)])
        ev = layer(lambda t: (lambda h, t=t: h.tensor_copy(
            wv[t][:, GS:GS + 4], wv[t][:, P_:P_ + 4])), [(sV, ev)])
        for k in range(1, 6):
            ev = layer(lambda t, k=k: (lambda h, t=t, k=k: h.tensor_max(
                wv[t][:, GS:GS + 4], wv[t][:, GS:GS + 4],
                wv[t][:, P_ + k * 4:P_ + (k + 1) * 4])), [(sV, ev)])
        ev = layer(lambda t: (lambda h, t=t: h.tensor_copy(
            wv[t][:, GS2:GS2 + 4], wv[t][:, GS:GS + 4])), [(sV, ev)])
        ev = layer(lambda t: (lambda h, t=t: h.tensor_copy(
            wv[t][:, GS2 + 4:GS2 + 8], wv[t][:, GS:GS + 4])), [(sV, ev)])
        ev = layer(lambda t: (lambda h, t=t: h.tensor_tensor(
            wv[t][:, CNT:CNT + 4], wv[t][:, GS2 + 1:GS2 + 5],
            wv[t][:, GS:GS + 4], ALU.is_gt)), [(sV, ev)])
        for k in (2, 3):
            ev = layer(lambda t, k=k: (lambda h, t=t, k=k: h.tensor_tensor(
                wv[t][:, ALW:ALW + 4], wv[t][:, GS2 + k:GS2 + k + 4],
                wv[t][:, GS:GS + 4], ALU.is_gt)), [(sV, ev)])
            ev = layer(lambda t: (lambda h, t=t: h.tensor_add(
                wv[t][:, CNT:CNT + 4], wv[t][:, CNT:CNT + 4],
                wv[t][:, ALW:ALW + 4])), [(sV, ev)])
        ev = layer(lambda t: (lambda h, t=t: h.tensor_scalar(
            wv[t][:, ALW:ALW + 4], wv[t][:, CNT:CNT + 4], 1.5, None, ALU.is_le)),
            [(sV, ev)])
        ev = layer(lambda t: (lambda h, t=t: h.scalar_tensor_tensor(
            wv[t][:, MSK:MSK + 16].rearrange("p (a b) -> p a b", a=4),
            wv[t][:, ALW:ALW + 4].unsqueeze(2).broadcast_to([128, 4, 4]),
            4.0, gv[t][:].rearrange("p (a b) -> p a b", a=4),
            ALU.mult, ALU.add)), [(sV, ev)])
        ev = layer(lambda t: (lambda h, t=t: h.tensor_scalar(
            wv[t][:, MSK:MSK + 16], wv[t][:, MSK:MSK + 16], -4.0, None, ALU.add)),
            [(sV, ev)])
        for t in range(8):
            ev_c = V1(lambda h, t=t: h.tensor_tensor(
                gcmp[:].rearrange("p (a b) -> p a b", a=16),
                wv[t][:, MSK:MSK + 16].unsqueeze(1).broadcast_to([128, 16, 16]),
                wv[t][:, MSK:MSK + 16].unsqueeze(2).broadcast_to([128, 16, 16]),
                ALU.is_gt), waits=[(sV, ev)])
            ev = V1(lambda h, t=t: h.reduce_sum(
                wv[t][:, RNK:RNK + 16],
                gcmp[:].rearrange("p (a b) -> p a b", a=16),
                mybir.AxisListType.X), waits=[(sV, ev_c)])
        ev = layer(lambda t: (lambda h, t=t: h.tensor_scalar(
            wv[t][:, SEL:SEL + 16], wv[t][:, RNK:RNK + 16], 3.5, None, ALU.is_le)),
            [(sV, ev)])
        ev = layer(lambda t: (lambda h, t=t: h.tensor_mul(
            wv[t][:, WR:WR + 16], wv[t][:, SEL:SEL + 16], gv[t][:])), [(sV, ev)])
        ev = layer(lambda t: (lambda h, t=t: h.reduce_sum(
            wv[t][:, DEN:DEN + 1], wv[t][:, WR:WR + 16],
            mybir.AxisListType.X)), [(sV, ev)])
        ev = layer(lambda t: (lambda h, t=t: h.tensor_scalar(
            wv[t][:, DEN + 1:DEN + 2], wv[t][:, DEN:DEN + 1], 1e-6, None,
            ALU.add)), [(sV, ev)])
        ev_rc = layer(lambda t: (lambda h, t=t: h.reciprocal(
            wv[t][:, DEN + 2:DEN + 3], wv[t][:, DEN + 1:DEN + 2])), [(sV, ev)])
        ev = layer(lambda t: (lambda h, t=t, half=half: h.tensor_copy(
            comb[:, (half * 8 + t) * NE:(half * 8 + t) * NE + 16],
            wv[t][:, WR:WR + 16])), [(sV, ev_rc)])
        ev = layer(lambda t: (lambda h, t=t, half=half: h.tensor_scalar(
            comb[:, (half * 8 + t) * NE:(half * 8 + t) * NE + 16],
            comb[:, (half * 8 + t) * NE:(half * 8 + t) * NE + 16],
            wv[t][:, DEN + 2:DEN + 3], None, ALU.mult)), [(sV, ev)])
        return ev

    ev_xow = [0, 0]
    ev_xow = [0, 0]
    ev_r1_used = [0, 0]
    ev_r1_scal = [0, 0]
    ev_gate_all = 0
    e_ffn_prev_half_v = 0
    e_ffn_prev_half_T = last_att_T
    ev_wuse = [last_att_T, last_att_T]
    ev_gu_used = [0, 0]

    for half in range(2):
        ev_atT = [[0, 0], [0, 0]]
        ev_op_evac = [0, 0]
        e_x2t_done = [0, 0]
        e_x1_done = [0] * 8
        e_sig = [0] * 8
        e_sig_last = 0
        for tl in range(8):
            tt = half * 8 + tl
            r = tl % 2
            ev_xow[r] = dmaY(xow[:, r * 1024:(r + 1) * 1024],
                             d_xown[tt * 128:(tt + 1) * 128, :],
                             waits=[(sV, ev_r1_used[r])] +
                                   ([(sT, last_att_T)] if (half == 0 and tl < 2)
                                    else []))
            ev_atT[r] = transpose8(
                attn[:, tt * 1024:(tt + 1) * 1024], True,
                [(sC, e_attn_done[tt]), (sV, ev_pnum_done), (sV, ev_atT[r][1])],
                (atT[:, r * 1024:r * 1024 + 512],
                 atT[:, r * 1024 + 512:r * 1024 + 1024]),
                (e_x2t_done[0], e_x2t_done[1]))
            opp = ps01 if r == 0 else ps23
            ev_mm = 0
            w = [(sV, ev_atT[r][1]), (sG, e_w2t), (sV, ev_op_evac[r])]
            for c in range(HC):
                for hf in range(2):
                    ev_mm = T1(lambda h, r=r, c=c, hf=hf, opp=opp: h.matmul(
                        opp[:, hf * 512:(hf + 1) * 512],
                        atT[:, r * 1024 + c * 128:r * 1024 + (c + 1) * 128],
                        w2t_bf[:, c * 1024 + hf * 512:c * 1024 + hf * 512 + 512],
                        start=(c == 0), stop=(c == HC - 1)), w)
                    w = []
            ev_r1 = V1(lambda h, r=r, opp=opp: h.tensor_add(
                r1b[:, r * 1024:(r + 1) * 1024], opp[:],
                xow[:, r * 1024:(r + 1) * 1024]),
                waits=[(sT, ev_mm), (sS, ev_xow[r]), (sC, ev_r1_scal[r]),
                       (sOut, ev_yb_free[r])])
            ev_op_evac[r] = ev_r1
            wdst = []
            if tl == 0:
                wdst = [(sT, e_ffn_prev_half_T), (sV, e_ffn_prev_half_v)]
            ev_x1 = ln_emit(r1b[:, r * 1024:(r + 1) * 1024], ev_r1,
                            x1res[:, tl * 1024:(tl + 1) * 1024], r * 8,
                            waits_dst=wdst)
            e_x1_done[tl] = ev_x1
            ev_r1_scal[r] = ev_x1
            ev_r1_used[r] = ev_r1
            ev_tx = transpose8(
                x1res[:, tl * 1024:(tl + 1) * 1024], False,
                [(sC, ev_x1)],
                (x2t3[:, 0:4, tl * 128:(tl + 1) * 128],
                 x2t3[:, 4:8, tl * 128:(tl + 1) * 128]),
                (e_x2t_done[0], e_x2t_done[1]),
                extra_evac=(x1t_f[:, 0:512], x1t_f[:, 512:1024]))
            e_x2t_done = ev_tx
            ev_gmm = 0
            w = [(sV, ev_tx[1]), (sS, e_wgt), (sC, e_sig_last)]
            for c in range(HC):
                ev_gmm = T1(lambda h, c=c: h.matmul(
                    ps6[:, 0:16], x1t_f[:, c * 128:(c + 1) * 128],
                    wgt_f[:, c * 16:(c + 1) * 16], start=(c == 0),
                    stop=(c == HC - 1)), w)
                w = []
            e_sig[tl] = S1(lambda h, tl=tl: h.activation(
                gsc[:, tl * 16:(tl + 1) * 16], ps6[:, 0:16], AF.Sigmoid),
                waits=[(sT, ev_gmm), (sV, ev_gate_all)])
            e_sig_last = e_sig[tl]
        ev_gate_all = gate_batch(half, e_sig_last, ev_gate_all)
        e_comb_all = ev_gate_all
        if debug and half == 0:
            dmaG(dbg["dbg_x1"], x1res[:, 0:8192], waits=[(sC, max(e_x1_done))])
            dmaG(dbg["dbg_x2t"], x2t[:, 0:8192], waits=[(sV, max(e_x2t_done))])

        e_wld = [0, 0]
        ev_dn_evac = [ev_op_evac[0], ev_op_evac[1]]
        ev_x1_acc = list(e_x1_done)
        ev_guw = 0
        for e in range(min(2, NE)):
            pe = e % 2
            dmaG(w13[:, pe * 4096:(pe + 1) * 4096]
                 .rearrange("p (c n) -> p c n", c=HC),
                 d_w1a[e].rearrange("(c p) n -> p c n", p=128),
                 waits=[(sT, ev_wuse[pe])])
            dmaG(w13[:, 8192 + pe * 4096:8192 + (pe + 1) * 4096]
                 .rearrange("p (c n) -> p c n", c=HC),
                 d_w3a[e].rearrange("(c p) n -> p c n", p=128))
            e_wld[pe] = dmaG(
                w2bf[:, pe * 4096:(pe + 1) * 4096]
                .rearrange("p (c n) -> p c n", c=4),
                d_w2a[e].rearrange("(c p) n -> p c n", p=128))
        for e in range(NE):
            pe = e % 2
            guv = gu_bufs[pe]
            ev_sil = 0
            for fc in range(4):
                for tc in range(2):
                    w = [(sG, e_wld[pe]), (sC, ev_sil), (sV, ev_guw)]
                    if fc == 0 and tc == 0 and e < 2:
                        w += [(sV, max(e_x2t_done)), (sC, e_sig_last)]
                    ev_g = 0
                    for c in range(HC):
                        ev_g = T1(lambda h, pe=pe, fc=fc, tc=tc, c=c: h.matmul(
                            ps6[:], w13[:, pe * 4096 + c * 512 + fc * 128:
                                        pe * 4096 + c * 512 + (fc + 1) * 128],
                            x2t[:, c * 1024 + tc * 512:c * 1024 + (tc + 1) * 512],
                            start=(c == 0), stop=(c == HC - 1)), w)
                        w = []
                    ev_u = 0
                    for c in range(HC):
                        ev_u = T1(lambda h, pe=pe, fc=fc, tc=tc, c=c: h.matmul(
                            ps7[:], w13[:, 8192 + pe * 4096 + c * 512 + fc * 128:
                                        8192 + pe * 4096 + c * 512 + (fc + 1) * 128],
                            x2t[:, c * 1024 + tc * 512:c * 1024 + (tc + 1) * 512],
                            start=(c == 0), stop=(c == HC - 1)))
                    ev_sil = S1(lambda h, tc=tc: h.activation(
                        silu_t[:, tc * 512:(tc + 1) * 512], ps6[:], AF.Silu),
                        waits=[(sT, ev_g), (sV, ev_guw)])
                    ev_guw = V1(lambda h, guv=guv, fc=fc, tc=tc: h.tensor_tensor(
                        guv[:, fc * 1024 + tc * 512:fc * 1024 + (tc + 1) * 512],
                        silu_t[:, tc * 512:(tc + 1) * 512], ps7[:], ALU.mult),
                        waits=[(sC, ev_sil), (sT, ev_u)])
            for tl in range(8):
                tt = half * 8 + tl
                dnp = ps01 if tl % 2 == 0 else ps23
                ev_mm = 0
                w = [(sV, ev_guw), (sV, ev_dn_evac[tl % 2]), (sG, e_wld[pe])]
                for fc in range(4):
                    for hf in range(2):
                        ev_mm = T1(lambda h, pe=pe, fc=fc, hf=hf, tl=tl, dnp=dnp:
                                   h.matmul(
                            dnp[:, hf * 512:(hf + 1) * 512],
                            gu_bufs[pe][:, fc * 1024 + tl * 128:
                                        fc * 1024 + (tl + 1) * 128],
                            w2bf[:, pe * 4096 + fc * 1024 + hf * 512:
                                 pe * 4096 + fc * 1024 + hf * 512 + 512],
                            start=(fc == 0), stop=(fc == 3)), w)
                        w = []
                ev_acc = V1(lambda h, tl=tl, tt=tt, e=e, dnp=dnp:
                            h.scalar_tensor_tensor(
                    x1res[:, tl * 1024:(tl + 1) * 1024], dnp[:],
                    comb[:, tt * NE + e:tt * NE + e + 1],
                    x1res[:, tl * 1024:(tl + 1) * 1024], ALU.mult, ALU.add),
                    waits=[(sT, ev_mm), (sC, e_x1_done[tl]),
                           (sV, e_comb_all),
                           (sV, ev_x1_acc[tl])] +
                          ([(sGc, e_comb1)] if e >= 16 else []))
                ev_dn_evac[tl % 2] = ev_acc
                ev_x1_acc[tl] = ev_acc
            ev_wuse[pe] = sT.n
            ev_gu_used[pe] = sT.n
            if e + 2 < NE:
                dmaG(w13[:, pe * 4096:(pe + 1) * 4096]
                     .rearrange("p (c n) -> p c n", c=HC),
                     d_w1a[e + 2].rearrange("(c p) n -> p c n", p=128),
                     waits=[(sT, ev_wuse[pe])])
                dmaG(w13[:, 8192 + pe * 4096:8192 + (pe + 1) * 4096]
                     .rearrange("p (c n) -> p c n", c=HC),
                     d_w3a[e + 2].rearrange("(c p) n -> p c n", p=128))
                e_wld[pe] = dmaG(
                    w2bf[:, pe * 4096:(pe + 1) * 4096]
                    .rearrange("p (c n) -> p c n", c=4),
                    d_w2a[e + 2].rearrange("(c p) n -> p c n", p=128))
        for tl in range(8):
            tt = half * 8 + tl
            r = tl % 2
            ev_y = ln_emit(x1res[:, tl * 1024:(tl + 1) * 1024], ev_x1_acc[tl],
                           yb[:, r * 1024:(r + 1) * 1024], r * 8,
                           waits_dst=[(sOut, ev_yb_free[r])])
            ev_yb_free[r] = dmaY(d_y[tt * 128:(tt + 1) * 128, :],
                                 yb[:, r * 1024:(r + 1) * 1024],
                                 waits=[(sC, ev_y)], store=True)
        if debug and half == 0:
            dmaG(dbg["dbg_comb"], comb[:, 0:8 * NE], waits=[(sV, e_comb_all)])
        e_ffn_prev_half_v = sV.n
        e_ffn_prev_half_T = sT.n

    emit("sync", lambda h: h.nop(), waits=[(sOut, sOut.n)])

    with nc.Block() as block:
        for name in ("sync", "gpsimd", "tensor", "vector", "scalar"):
            thunks = stream[name]

            def runner(h, thunks=thunks):
                for t in thunks:
                    t(h)
            getattr(block, name)(runner)
    es.close()
    return nc


_CACHE = {}


def _prep_host(inputs):
    f32 = np.float32
    x = np.ascontiguousarray(inputs["L0_attn_x"], dtype=f32)
    w1t = np.ascontiguousarray(np.asarray(inputs["L0_attn_W1"], f32).T)
    w2t = np.ascontiguousarray(np.asarray(inputs["L0_attn_W2"], f32).T)
    wgt = np.ascontiguousarray(np.asarray(inputs["L0_ffn_w_gate"], f32).T)
    w1e = np.asarray(inputs["L0_ffn_w1_e"], f32)
    w3e = np.asarray(inputs["L0_ffn_w3_e"], f32)
    w2e = np.asarray(inputs["L0_ffn_w2_e"], f32)
    w1s = np.asarray(inputs["L0_ffn_w1_s"], f32)
    w3s = np.asarray(inputs["L0_ffn_w3_s"], f32)
    w2s = np.asarray(inputs["L0_ffn_w2_s"], f32)
    w1a = np.ascontiguousarray(np.concatenate(
        [w1e.transpose(0, 2, 1), w1s.T.reshape(E, 2, FF).transpose(1, 0, 2)], 0))
    w3a = np.ascontiguousarray(np.concatenate(
        [w3e.transpose(0, 2, 1), w3s.T.reshape(E, 2, FF).transpose(1, 0, 2)], 0))
    w2a = np.ascontiguousarray(np.concatenate(
        [w2e.transpose(0, 2, 1), w2s.T.reshape(2, FF, E)], 0))
    tri = np.triu(np.ones((128, 128), f32))
    ones = np.ones((128, 128), f32)
    zeros = np.zeros((128, 128), f32)
    masks_p = [
        np.ascontiguousarray(np.stack([tri if i % 2 == 0 else zeros
                                       for i in range(8)])),
        np.ascontiguousarray(np.stack([ones if i % 2 == 0 else tri
                                       for i in range(8)])),
    ]
    in_maps, metas = [], []
    for core in range(8):
        b, p = core // 2, core % 2
        blocks = np.arange(NTO) * 2 + p
        rows = (blocks[:, None] * 128 + np.arange(128)[None, :]).reshape(-1)
        in_maps.append({
            "xseq": np.ascontiguousarray(x[b]),
            "xown": np.ascontiguousarray(x[b][rows]),
            "posq": np.ascontiguousarray(rows.astype(f32).reshape(NTO, 128).T),
            "masks": masks_p[p],
            "w1t": w1t, "w2t": w2t, "wgt": wgt,
            "w1a": w1a, "w3a": w3a, "w2a": w2a,
        })
        metas.append((b, blocks))
    return in_maps, metas


def kernel(L0_attn_x, L0_attn_W1, L0_attn_W2, L0_ffn_w_gate, L0_ffn_w1_e,
           L0_ffn_w3_e, L0_ffn_w2_e, L0_ffn_w1_s, L0_ffn_w3_s, L0_ffn_w2_s):
    inputs = dict(L0_attn_x=L0_attn_x, L0_attn_W1=L0_attn_W1,
                  L0_attn_W2=L0_attn_W2, L0_ffn_w_gate=L0_ffn_w_gate,
                  L0_ffn_w1_e=L0_ffn_w1_e, L0_ffn_w3_e=L0_ffn_w3_e,
                  L0_ffn_w2_e=L0_ffn_w2_e, L0_ffn_w1_s=L0_ffn_w1_s,
                  L0_ffn_w3_s=L0_ffn_w3_s, L0_ffn_w2_s=L0_ffn_w2_s)
    if "nc" not in _CACHE:
        _CACHE["nc"] = build_nc(debug=False)
    nc = _CACHE["nc"]
    in_maps, metas = _prep_host(inputs)
    res = run_bass_kernel_spmd(nc, in_maps, core_ids=list(range(8)))
    out = np.empty((B, T, E), dtype=np.float32)
    for core in range(8):
        b, blocks = metas[core]
        y = np.asarray(res.results[core]["y"]).reshape(NTO, 128, E)
        out.reshape(B, 32, 128, E)[b, blocks] = y
    return out


# version 09
# speedup vs baseline: 1.3864x; 1.3864x over previous
import numpy as np
import ml_dtypes
from contextlib import ExitStack

import concourse.bass as bass
import concourse.mybir as mybir
from concourse.bass_utils import run_bass_kernel_spmd

F32 = mybir.dt.float32
BF16 = mybir.dt.bfloat16
I32 = mybir.dt.int32
AF = mybir.ActivationFunctionType
ALU = mybir.AluOpType

B, T, E = 4, 4096, 1024
NH, D, HP, HC = 16, 64, 8, 8
NTO = 16
NQ = NTO * 128
NE = 18
FF = 512
CSHIFT = -20.0
I1024 = 1.0 / 1024.0


class Sem:
    def __init__(self, es, nc, name):
        self.h = es.enter_context(nc.semaphore(name))
        self.n = 0


def build_nc(debug=False, phases=5, qtiles=NTO, quarters=4, do_att=True, kvp='tmve'):
    nc = bass.Bass("TRN2", target_bir_lowering=False, detect_race_conditions=False)
    es = ExitStack()

    d_xseq = nc.dram_tensor("xseq", [T, E], F32, kind="ExternalInput").ap()
    d_xown = nc.dram_tensor("xown", [NQ, E], F32, kind="ExternalInput").ap()
    d_posq = nc.dram_tensor("posq", [128, NTO], F32, kind="ExternalInput").ap()
    d_masks = nc.dram_tensor("masks", [8, 128, 128], F32, kind="ExternalInput").ap()
    d_w1t = nc.dram_tensor("w1t", [E, 3 * E], F32, kind="ExternalInput").ap()
    d_w2t = nc.dram_tensor("w2t", [E, E], F32, kind="ExternalInput").ap()
    d_wgt = nc.dram_tensor("wgt", [E, 16], F32, kind="ExternalInput").ap()
    d_w1a = nc.dram_tensor("w1a", [NE, E, FF], F32, kind="ExternalInput").ap()
    d_w3a = nc.dram_tensor("w3a", [NE, E, FF], F32, kind="ExternalInput").ap()
    d_w2a = nc.dram_tensor("w2a", [NE, FF, E], F32, kind="ExternalInput").ap()
    d_y = nc.dram_tensor("y", [NQ, E], F32, kind="ExternalOutput").ap()
    dbg = {}
    if debug:
        for name, shape in [
            ("dbg_qt", [128, HP * 2048]), ("dbg_kt", [128, HP * 1024]),
            ("dbg_v65", [128, 8 * 1040]), ("dbg_attn", [128, NTO * 1024]),
            ("dbg_x1", [128, 8 * 1024]), ("dbg_comb", [128, 8 * NE]),
            ("dbg_qtab", [128, NTO * 128]), ("dbg_x2t", [128, 8 * 1024]),
            ("dbg_cst", [128, 8]), ("dbg_invf", [128, 64]),
            ("dbg_invf2", [128, 64]), ("dbg_tgi", [128, 128]),
        ]:
            dbg[name] = nc.dram_tensor(name, shape, F32, kind="ExternalOutput").ap()

    def sb(name, shape, dt):
        return es.enter_context(nc.sbuf_tensor(name, shape, dt))

    cst = sb("cst", [128, 8], F32)
    C_ONE, C_2PI, C_I2PI, C_LN1E4, C_NEGC, C_E4, C_EPS, C_PI = 0, 1, 2, 3, 4, 5, 6, 7
    invf = sb("invf", [128, 64], F32)
    id_f32 = sb("id_f32", [128, 128], F32)
    id_bf = sb("id_bf", [128, 128], BF16)
    masks = sb("masksb", [128, 8 * 128], BF16)
    qtab = sb("qtab", [128, NTO * 128], F32)
    ktab = qtab[:, 0:1024]
    comb = sb("comb", [128, NTO * NE], F32)
    pden = sb("pden", [128, 3 * 16 * 4], F32)
    xbt = sb("xbt", [128, 4 * 1024], BF16)
    xk_bf = xbt[:, 0:2048]
    xkT = xbt[:, 2048:4096]
    kro = sb("kro", [128, 2 * 1024], BF16)
    ptb = sb("ptb", [128, 6 * 512], BF16)
    rope_t1 = sb("rope_t1", [128, 1024], BF16)
    rope_t2 = sb("rope_t2", [128, 1024], BF16)
    tgen = sb("tgen", [128, 8 * 64], F32)
    tgi = sb("tgi", [128, 4 * 64], I32)
    idsc = tgi[:, 0:128]
    tposi = sb("tposi", [128, 8], I32)
    poscol = sb("poscol", [128, 16], F32)
    fnum = sb("fnum", [128, 2 * 256], F32)
    dcol = sb("dcol", [128, 32], F32)
    sq = sb("sq", [128, 1024], F32)
    lncol = sb("lncol", [128, 16], F32)
    gsc = sb("gsc", [128, 8 * 16], F32)
    gw = None
    G_PS, G_GS, G_GS2, G_CNT, G_ALW, G_MSK, G_RANK, G_SEL, G_WR, G_DEN = \
        0, 24, 28, 36, 40, 44, 60, 76, 92, 108
    gw = ptb[:].bitcast(F32)
    gcmp = rope_t1[:].bitcast(F32)[:, 0:256]
    silu_t = kro[:, 0:1024]
    xow = xbt[:].bitcast(F32)
    r1b = sb("r1b", [128, 2 * 1024], F32)
    yb = r1b
    wgt_f = sb("wgt_f", [128, HC * 16], F32)

    wkv = sb("wkv", [128, HC * 2048], BF16)
    ktq = sb("ktq", [128, HP * 1024], BF16)
    v65 = sb("v65", [128, 8 * 1040], BF16)
    qt = sb("qt", [128, HP * 2048], BF16)
    pnum = sb("pnum", [128, 3 * 16 * 256], BF16)
    attn = sb("attn", [128, NTO * 1024], BF16)

    x1res = wkv[:].bitcast(F32)
    x2t = ktq[:]
    w2bf = v65[:, 0:8192]
    w13 = qt[:]
    wq_bf = pnum[:, 0:8192]
    w2t_bf = pnum[:, 0:8192]
    atT = pnum[:, 8192:10240]
    x1t_f = pnum[:, 10240:12288].bitcast(F32)

    ps01 = es.enter_context(nc.psum_tensor("ps01", [128, 1024], F32))
    ps23 = es.enter_context(nc.psum_tensor("ps23", [128, 1024], F32))
    ps45 = es.enter_context(nc.psum_tensor("ps45", [128, 1024], F32))
    psS = es.enter_context(nc.psum_tensor("psS", [128, 1024], F32))
    ps6 = psS[:, 0:512]
    ps7 = psS[:, 512:1024]
    ps45b = ps45[:].bitcast(BF16)

    sS = Sem(es, nc, "sS")
    sG = Sem(es, nc, "sG")
    sT = Sem(es, nc, "sT")
    sV = Sem(es, nc, "sV")
    sC = Sem(es, nc, "sC")
    sGc = Sem(es, nc, "sGc")
    sOut = Sem(es, nc, "sOut")

    stream = {e: [] for e in ("sync", "gpsimd", "tensor", "vector", "scalar")}

    def emit(eng, fn, waits=(), incs=(), dma=False):
        for s, v in waits:
            assert v <= s.n, f"wait on future event: {v} > {s.n} ({eng})"
        for s, k in incs:
            s.n += k
        w = tuple((s.h, v) for s, v in waits if v > 0)
        i = tuple((s.h, k) for s, k in incs)
        drain_incs = (not dma) and eng in ("vector", "scalar", "gpsimd") and i

        def thunk(h, w=w, i=i, fn=fn, drain_incs=drain_incs):
            for sem, v in w:
                h.wait_ge(sem, v)
            inst = fn(h)
            if drain_incs:
                d = h.drain(fusable=False)
                for sem, k in i:
                    d.then_inc(sem, k)
            else:
                for sem, k in i:
                    inst.then_inc(sem, k)
        stream[eng].append(thunk)

    def T1(fn, waits=()):
        emit("tensor", fn, waits, [(sT, 1)])
        return sT.n

    def V1(fn, waits=()):
        emit("vector", fn, waits, [(sV, 1)])
        return sV.n

    def V0(fn, waits=()):
        emit("vector", fn, waits)

    def S1(fn, waits=()):
        emit("scalar", fn, waits, [(sC, 1)])
        return sC.n

    def S0(fn, waits=()):
        emit("scalar", fn, waits)

    def G1(fn, waits=()):
        emit("gpsimd", fn, waits, [(sGc, 1)])
        return sGc.n

    def G0(fn, waits=()):
        emit("gpsimd", fn, waits)

    def dmaY(out, in_, waits=(), store=False):
        emit("sync", lambda h: h.dma_start(out=out, in_=in_), waits,
             [(sOut if store else sS, 16)], dma=True)
        return (sOut if store else sS).n

    def dmaG(out, in_, waits=()):
        emit("gpsimd", lambda h: h.dma_start(out=out, in_=in_), waits, [(sG, 16)],
             dma=True)
        return sG.n

    G0(lambda h: h.memset(cst[:, C_ONE:C_ONE + 1], 1.0))
    G0(lambda h: h.memset(cst[:, C_NEGC:C_NEGC + 1], CSHIFT))
    G0(lambda h: h.memset(cst[:, C_EPS:C_EPS + 1], 1e-5))
    e0 = G1(lambda h: h.memset(cst[:, C_E4:C_E4 + 1], 10000.0))
    S0(lambda h: h.activation(cst[:, C_2PI:C_2PI + 1], cst[:, C_ONE:C_ONE + 1],
                              AF.Arctan), waits=[(sGc, e0)])
    e1 = S1(lambda h: h.activation(cst[:, C_LN1E4:C_LN1E4 + 1],
                                   cst[:, C_E4:C_E4 + 1], AF.Ln))
    e_rc0 = V1(lambda h: h.reciprocal(cst[:, C_I2PI:C_I2PI + 1],
                                      cst[:, C_2PI:C_2PI + 1]),
               waits=[(sC, e1)])
    V0(lambda h: h.tensor_scalar(cst[:, C_2PI:C_2PI + 1], cst[:, C_2PI:C_2PI + 1],
                                 8.0, None, ALU.mult), waits=[(sV, e_rc0)])
    V0(lambda h: h.tensor_scalar(cst[:, C_I2PI:C_I2PI + 1],
                                 cst[:, C_I2PI:C_I2PI + 1], 0.125, None,
                                 ALU.mult), waits=[(sV, e_rc0)])
    e_cst = V1(lambda h: h.tensor_scalar(cst[:, C_PI:C_PI + 1],
                                         cst[:, C_2PI:C_2PI + 1], 0.5, None,
                                         ALU.mult))

    e2 = G1(lambda h: h.iota(tgi[:, 0:32], pattern=[[1, 32]], base=0,
                             channel_multiplier=0))
    V0(lambda h: h.tensor_copy(invf[:, 0:32], tgi[:, 0:32]), waits=[(sGc, e2)])
    V0(lambda h: h.tensor_scalar(invf[:, 0:32], invf[:, 0:32], -0.03125, None,
                                 ALU.mult))
    e3 = V1(lambda h: h.tensor_scalar(invf[:, 0:32], invf[:, 0:32],
                                      cst[:, C_LN1E4:C_LN1E4 + 1], None, ALU.mult),
            waits=[(sC, e1)])
    e4 = S1(lambda h: h.activation(invf[:, 0:32], invf[:, 0:32], AF.Exp),
            waits=[(sV, e3)])
    e_invf = V1(lambda h: h.tensor_copy(invf[:, 32:64], invf[:, 0:32]),
                waits=[(sC, e4)])

    if debug:
        dmaG(dbg["dbg_invf2"], invf[:], waits=[(sV, e_invf)])
    e5 = G1(lambda h: h.iota(idsc[:], pattern=[[1, 128]], base=0,
                             channel_multiplier=-1), waits=[(sV, e3)])
    V0(lambda h: h.tensor_copy(id_f32[:], idsc[:]), waits=[(sGc, e5)])
    e_idf = V1(lambda h: h.tensor_scalar(id_f32[:], id_f32[:], 0.0, None,
                                         ALU.is_equal))
    e_id = V1(lambda h: h.tensor_copy(id_bf[:], id_f32[:]), waits=[(sV, e_idf)])

    e_masks = dmaG(masks[:].rearrange("p (i c) -> p i c", i=8),
                   d_masks[:, :, :].transpose([1, 0, 2]))
    e_comb1 = G1(lambda h: h.memset(
        comb[:].rearrange("p (t e) -> p t e", e=NE)[:, :, 16:18], 1.0))

    if phases < 1:
        if debug:
            G0(lambda h: h.tensor_copy(tgen[:, 0:128].bitcast(I32) if False else tgen[:, 0:128], tgi[:].bitcast(F32)) if False else h.nop())
            dmaG(dbg["dbg_tgi"], tgi[:].bitcast(F32), waits=[(sV, e_invf)])
            dmaG(dbg["dbg_invf"], invf[:], waits=[(sV, e_invf)])
            dmaG(dbg["dbg_cst"], cst[:], waits=[(sV, e_invf)])
        emit("sync", lambda h: h.nop(), waits=[(sS, sS.n)])
        emit("gpsimd", lambda h: h.nop(), waits=[(sG, sG.n)])
        with nc.Block() as block:
            for name in ("sync", "gpsimd", "tensor", "vector", "scalar"):
                thunks = stream[name]
                def runner(h, thunks=thunks):
                    for t in thunks:
                        t(h)
                getattr(block, name)(runner)
        es.close()
        return nc

    def gen_tables_batch(tab, tiles, pcols, extra_waits, prev_events):
        n = len(tiles)
        ang = [tab[:, j * 128:j * 128 + 64] for j in tiles]
        scr = [tab[:, j * 128 + 64:j * 128 + 128] for j in tiles]
        rr = [tgen[:, i * 128:i * 128 + 64] for i in range(n)]
        sh = [tgen[:, i * 128 + 64:i * 128 + 128] for i in range(n)]
        ii = [tgi[:, i * 64:(i + 1) * 64] for i in range(n)]
        ev = 0

        def layer(fn_per_tile, waits=()):
            for i in range(n):
                (V1 if i == n - 1 else V0)(fn_per_tile(i),
                                           waits=list(waits) if i == 0 else ())
            return sV.n

        w0 = [(sV, e_invf), (sV, e_cst)] + list(extra_waits) + list(prev_events)
        ev = layer(lambda i: (lambda h, i=i: h.tensor_scalar(
            ang[i], invf[:, 0:64], pcols[i], None, ALU.mult)), w0)
        ev = layer(lambda i: (lambda h, i=i: h.tensor_scalar(
            scr[i], ang[i], cst[:, C_I2PI:C_I2PI + 1], None, ALU.mult)),
            [(sV, ev)])
        ev = layer(lambda i: (lambda h, i=i: h.tensor_copy(ii[i], scr[i])),
                   [(sV, ev)])
        ev = layer(lambda i: (lambda h, i=i: h.tensor_copy(scr[i], ii[i])),
                   [(sV, ev)])
        ev = layer(lambda i: (lambda h, i=i: h.tensor_scalar(
            scr[i], scr[i], cst[:, C_2PI:C_2PI + 1], None, ALU.mult)), [(sV, ev)])
        ev = layer(lambda i: (lambda h, i=i: h.tensor_sub(rr[i], ang[i], scr[i])),
                   [(sV, ev)])
        ev = layer(lambda i: (lambda h, i=i: h.tensor_scalar(
            sh[i], rr[i], cst[:, C_PI:C_PI + 1], None, ALU.is_gt)), [(sV, ev)])
        ev = layer(lambda i: (lambda h, i=i: h.tensor_scalar(
            sh[i], sh[i], cst[:, C_2PI:C_2PI + 1], None, ALU.mult)), [(sV, ev)])
        ev = layer(lambda i: (lambda h, i=i: h.tensor_sub(rr[i], rr[i], sh[i])),
                   [(sV, ev)])
        ev = layer(lambda i: (lambda h, i=i: h.scalar_tensor_tensor(
            sh[i], rr[i], -1.0,
            cst[:, C_PI:C_PI + 1].broadcast_to([128, 64]), ALU.mult, ALU.is_gt)),
            [(sV, ev)])
        ev = layer(lambda i: (lambda h, i=i: h.tensor_scalar(
            sh[i], sh[i], cst[:, C_2PI:C_2PI + 1], None, ALU.mult)), [(sV, ev)])
        ev_r = layer(lambda i: (lambda h, i=i: h.tensor_add(rr[i], rr[i], sh[i])),
                     [(sV, ev)])
        for i in range(n):
            S0(lambda h, i=i: h.activation(scr[i], rr[i], AF.Sin),
               waits=[(sV, ev_r)] if i == 0 else ())
        for i in range(n):
            (S1 if i == n - 1 else S0)(lambda h, i=i: h.activation(
                sh[i], rr[i], AF.Sin, scale=0.5))
        ev_s = sC.n
        ev = layer(lambda i: (lambda h, i=i: h.tensor_mul(sh[i], sh[i], sh[i])),
                   [(sC, ev_s)])
        ev = layer(lambda i: (lambda h, i=i: h.tensor_scalar(
            ang[i], sh[i], -2.0, 1.0, ALU.mult, ALU.add)), [(sV, ev)])
        ev = layer(lambda i: (lambda h, i=i: h.tensor_scalar(
            tab[:, tiles[i] * 128 + 64:tiles[i] * 128 + 96],
            tab[:, tiles[i] * 128 + 64:tiles[i] * 128 + 96],
            -1.0, None, ALU.mult)), [(sV, ev)])
        return ev, ev_s

    def rope_emit(psrc, tab, off, waits, out_bf):
        ps3 = psrc.rearrange("p (h d) -> p h d", d=64)
        t13 = rope_t1[:, 0:1024].rearrange("p (h d) -> p h d", d=64)
        V0(lambda h: h.tensor_tensor(
            t13[:, :, 0:32], ps3[:, :, 32:64],
            tab[:, off + 64:off + 96].unsqueeze(1).broadcast_to([128, 16, 32]),
            ALU.mult), waits=waits)
        V0(lambda h: h.tensor_tensor(
            t13[:, :, 32:64], ps3[:, :, 0:32],
            tab[:, off + 96:off + 128].unsqueeze(1).broadcast_to([128, 16, 32]),
            ALU.mult))
        ev_m = V1(lambda h: h.tensor_tensor(
            rope_t2[:, 0:1024].rearrange("p (h d) -> p h d", d=64), ps3,
            tab[:, off:off + 64].unsqueeze(1).broadcast_to([128, 16, 64]),
            ALU.mult))
        return V1(lambda h: h.tensor_add(out_bf, rope_t1[:, 0:1024],
                                         rope_t2[:, 0:1024]),
                  waits=[(sV, ev_m)])

    def transpose8(src, isbf, waits_first, evac_aps, ev_prev_reads,
                   extra_evac=None):
        ident = id_bf if isbf else id_f32
        psv = ps45b if isbf else ps45[:]
        bank_off = 1024 if isbf else 512
        evts = [0, 0]
        for c in range(8):
            off = (c % 4) * 128 + (0 if c < 4 else bank_off)
            w = list(waits_first) if c == 0 else []
            if c == 0 and ev_prev_reads[0]:
                w.append((sV, ev_prev_reads[0]))
            if c == 4 and ev_prev_reads[1]:
                w.append((sV, ev_prev_reads[1]))
            ev = T1(lambda h, off=off, c=c: h.transpose(
                psv[:, off:off + 128], src[:, c * 128:(c + 1) * 128], ident[:]), w)
            if c == 3:
                evts[0] = ev
            if c == 7:
                evts[1] = ev
        out = []
        for half in range(2):
            pslice = psv[:, half * bank_off:half * bank_off + 512]
            if extra_evac is None:
                out.append(V1(lambda h, pslice=pslice, half=half: h.tensor_copy(
                    evac_aps[half], pslice), waits=[(sT, evts[half])]))
            else:
                V0(lambda h, pslice=pslice, half=half: h.tensor_copy(
                    evac_aps[half], pslice), waits=[(sT, evts[half])])
                out.append(V1(lambda h, pslice=pslice, half=half: h.tensor_copy(
                    extra_evac[half], pslice)))
        return out

    e_wq = dmaG(wq_bf.rearrange("p (c n) -> p c n", c=HC),
                d_w1t[:, 0:E].rearrange("(c p) n -> p c n", p=128))

    qt3 = qt[:].rearrange("p (hp q) -> p hp q", hp=HP)
    e_posq = dmaY(poscol[:, 0:16], d_posq[:])
    gen_ev, gen_sc = 0, 0
    for bb in range(4):
        tiles = list(range(bb * 4, (bb + 1) * 4))
        pcols = [poscol[:, j:j + 1] for j in tiles]
        gen_ev, gen_sc = gen_tables_batch(
            qtab, tiles, pcols, [(sS, e_posq)],
            [(sV, gen_ev), (sC, gen_sc)] if bb else [(sV, e_id)])
    e_qtab = gen_ev
    e_qtab_t = [e_qtab] * NTO

    ev_xqT = [[0, 0], [0, 0]]
    ev_qt_ev = [0, 0]
    ev_rope_prev = 0
    for j in range(qtiles):
        r = j % 2
        ev_x = dmaG(xk_bf[:, r * 1024:(r + 1) * 1024],
                    d_xown[j * 128:(j + 1) * 128, :],
                    waits=[(sV, ev_xqT[r][1])])
        ev_xqT[r] = transpose8(
            xk_bf[:, r * 1024:(r + 1) * 1024], True,
            [(sG, ev_x), (sV, e_id)],
            (xkT[:, r * 1024:r * 1024 + 512], xkT[:, r * 1024 + 512:r * 1024 + 1024]),
            (ev_qt_ev[0], ev_qt_ev[1]))
        ev_mm = 0
        w = [(sV, ev_xqT[r][1]), (sG, e_wq), (sV, ev_rope_prev)]
        for c in range(HC):
            for half in range(2):
                ev_mm = T1(lambda h, r=r, c=c, half=half: h.matmul(
                    ps01[:, half * 512:(half + 1) * 512],
                    xkT[:, r * 1024 + c * 128:r * 1024 + (c + 1) * 128],
                    wq_bf[:, c * 1024 + half * 512:c * 1024 + half * 512 + 512],
                    start=(c == 0), stop=(c == HC - 1)), w)
                w = []
        ev_rope_prev = rope_emit(ps01[:], qtab, j * 128,
                                 [(sT, ev_mm), (sV, e_qtab_t[j])],
                                 kro[:, r * 1024:(r + 1) * 1024])
        ev_qt_ev = transpose8(
            kro[:, r * 1024:(r + 1) * 1024], True, [(sV, ev_rope_prev)],
            (qt3[:, 0:4, j * 128:(j + 1) * 128], qt3[:, 4:8, j * 128:(j + 1) * 128]),
            (ev_qt_ev[0], ev_qt_ev[1]))
    e_qt_done = max(ev_qt_ev)
    e_q_rope_last = sV.n
    if debug:
        dmaG(dbg["dbg_qt"], qt[:], waits=[(sV, e_qt_done)])
        dmaG(dbg["dbg_qtab"], qtab[:], waits=[(sV, e_qt_done)])
        dmaG(dbg["dbg_cst"], cst[:], waits=[(sV, e_qt_done)])
        dmaG(dbg["dbg_invf"], invf[:], waits=[(sV, e_qt_done)])

    if phases < 3:
        emit("sync", lambda h: h.nop(), waits=[(sS, sS.n)])
        emit("gpsimd", lambda h: h.nop(), waits=[(sG, sG.n)])
        with nc.Block() as block:
            for name in ("sync", "gpsimd", "tensor", "vector", "scalar"):
                thunks = stream[name]
                def runner(h, thunks=thunks):
                    for t in thunks:
                        t(h)
                getattr(block, name)(runner)
        es.close()
        return nc
    import os
    e_wkv = (dmaG(wkv[:].rearrange("p (c n) -> p c n", c=HC),
                  d_w1t[:, E:3 * E].rearrange("(c p) n -> p c n", p=128))
             if os.environ.get("K_NOWKV") != "1" else 0)

    pnum3 = pnum[:].rearrange("p (s n) -> p s n", n=256)
    pden3 = pden[:].rearrange("p (s n) -> p s n", n=4)
    pn_first = {}
    ev_slot = {}
    v65_4 = v65[:].rearrange("p (t h o) -> p t h o", t=8, o=65)
    ktq3 = ktq[:].rearrange("p (hp q) -> p hp q", hp=HP)

    e_attn_done = [0] * NTO
    ev_av_evac = [0]
    ev_dcol_free = [0, 0]
    e_kv_evac = [e_q_rope_last, e_q_rope_last]
    ev_kT_ev = [0, 0]
    ev_xsT = [[0, 0], [0, 0]]
    ev_pair_rd = [0, 0, 0]
    ev_ptb_war = [0, 0, 0]
    ev_ptb_msk = [0, 0, 0]
    last_att_T = 0
    attn3 = attn[:].rearrange("p (t o) -> p t o", o=1024)
    st_pairs = (psS, ps45, ps23)
    gpair = [0]
    pend_pairs = []

    for qi, qr in enumerate([3, 2, 1, 0][:quarters]):
        ev_v65_ones_v = V1(lambda h: h.memset(v65_4[:, :, :, 64:65], 1.0),
                           waits=[(sT, last_att_T)])
        ev_v65_ones = G1(lambda h: h.nop(), waits=[(sV, ev_v65_ones_v)])
        if 't' in kvp:
            for sl in range(8):
                gt = qr * 8 + sl
                (G1 if sl == 7 else G0)(lambda h, gt=gt, sl=sl: h.iota(
                    tposi[:, sl:sl + 1], pattern=[[1, 1]], base=gt * 128,
                    channel_multiplier=1), waits=[(sV, gen_ev)] if sl == 0 else ())
            e6 = sGc.n
            evp = V1(lambda h: h.tensor_copy(poscol[:, 0:8], tposi[:, 0:8]),
                     waits=[(sGc, e6), (sV, gen_ev)])
            for bb in range(2):
                tiles = list(range(bb * 4, (bb + 1) * 4))
                pcols = [poscol[:, i:i + 1] for i in tiles]
                gen_ev, gen_sc = gen_tables_batch(
                    ktab, tiles, pcols, [(sV, evp)],
                    [(sV, gen_ev), (sC, gen_sc)])
            ev_ktab_done = gen_ev

        for sl in range(8):
            gt = qr * 8 + sl
            r = sl % 2
            ev_x = dmaG(xk_bf[:, r * 1024:(r + 1) * 1024],
                        d_xseq[gt * 128:(gt + 1) * 128, :],
                        waits=[(sV, ev_xsT[r][1]), (sV, ev_xqT[r][1])])
            ev_xsT[r] = transpose8(
                xk_bf[:, r * 1024:(r + 1) * 1024], True,
                [(sG, ev_x)] + ([(sC, ev_pair_rd[1])] if sl == 0 else []),
                (xkT[:, r * 1024:r * 1024 + 512],
                 xkT[:, r * 1024 + 512:r * 1024 + 1024]),
                (ev_kT_ev[0], ev_kT_ev[1]))
            ev_mm = 0
            w = [(sV, ev_xsT[r][1]), (sG, e_wkv), (sV, e_kv_evac[0]),
                 (sV, e_kv_evac[1]), (sV, ev_av_evac[0]), (sC, ev_pair_rd[2])]
            if 'm' not in kvp:
                continue
            for c in range(HC):
                for half in range(2):
                    T1(lambda h, r=r, c=c, half=half: h.matmul(
                        ps01[:, half * 512:(half + 1) * 512],
                        xkT[:, r * 1024 + c * 128:r * 1024 + (c + 1) * 128],
                        wkv[:, c * 2048 + half * 512:c * 2048 + half * 512 + 512],
                        start=(c == 0), stop=(c == HC - 1)), w)
                    w = []
                    ev_mm = T1(lambda h, r=r, c=c, half=half: h.matmul(
                        ps23[:, half * 512:(half + 1) * 512],
                        xkT[:, r * 1024 + c * 128:r * 1024 + (c + 1) * 128],
                        wkv[:, c * 2048 + 1024 + half * 512:
                            c * 2048 + 1024 + half * 512 + 512],
                        start=(c == 0), stop=(c == HC - 1)))
            if 'v' in kvp:
              e_kv_evac[0] = V1(lambda h, sl=sl: h.tensor_copy(
                v65_4[:, sl, :, 0:64],
                ps23[:].rearrange("p (h o) -> p h o", o=64)),
                waits=[(sT, ev_mm), (sGc, ev_v65_ones)])
            ev_ro = rope_emit(ps01[:], ktab, sl * 128,
                              [(sT, ev_mm), (sV, ev_ktab_done)],
                              kro[:, r * 1024:(r + 1) * 1024])
            e_kv_evac[1] = ev_ro
            ev_kT_ev = transpose8(
                kro[:, r * 1024:(r + 1) * 1024], True, [(sV, ev_ro)],
                (ktq3[:, 0:4, sl * 128:(sl + 1) * 128],
                 ktq3[:, 4:8, sl * 128:(sl + 1) * 128]),
                (ev_kT_ev[0], ev_kT_ev[1]))
        e_ktq_done = max(ev_kT_ev)
        e_v65_done = e_kv_evac[0]
        if debug and qr == 0:
            dmaG(dbg["dbg_kt"], ktq[:], waits=[(sV, e_ktq_done)])
            dmaG(dbg["dbg_v65"], v65[:], waits=[(sV, e_v65_done)])

        for J in range(qr, 4):
            for hp in range(HP):
                diag = (J == qr)
                work = [(i, (i // 2 if diag else 0)) for i in range(8)]
                n_av = sum(4 - cu for _, cu in work)
                avp = ps01
                k_av = [0, 0]
                ev_S = [0] * 8
                ev_msk = [0] * 8
                ev_avmm = 0
                dcover = {}

                def emit_av(i, cu):
                    nonlocal ev_avmm
                    th = psv_of[i][0]
                    for sub in (0, 1):
                        hh = hp * 2 + sub
                        for c in range(cu, 4):
                            w2 = [(sC, dcover[i])]
                            if diag and c == cu:
                                w2.append((sV, ev_msk[i]))
                            if k_av[sub] == 0:
                                w2.append((sV, ev_av_evac[0]))
                            st = (k_av[sub] == 0)
                            sp = (k_av[sub] == n_av - 1)
                            ev_avmm = T1(lambda h, th=th, sub=sub, c=c,
                                         hh=hh, i=i, st=st, sp=sp, avp=avp:
                                         h.matmul(
                                avp[:, sub * 512 + c * 65:sub * 512 + c * 65 + 65],
                                ptb[:, th * 1024 + sub * 512 + c * 128:
                                    th * 1024 + sub * 512 + (c + 1) * 128],
                                v65[:, i * 1040 + hh * 65:i * 1040 + hh * 65 + 65],
                                start=st, stop=sp, skip_group_check=True), w2)
                            k_av[sub] += 1
                    ev_ptb_war[th] = ev_avmm

                def emit_mask(i, cu):
                    th = psv_of[i][0]
                    ptpair = ptb[:, th * 1024:th * 1024 + 1024].rearrange(
                        "p (s q) -> p s q", s=2)
                    dst = ptpair[:, :, cu * 128:cu * 128 + 128]
                    ev_msk[i] = V1(lambda h, dst=dst, i=i: h.tensor_tensor(
                        dst, dst,
                        masks[:, i * 128:(i + 1) * 128].unsqueeze(1)
                        .broadcast_to([128, 2, 128]), ALU.mult),
                        waits=[(sC, dcover[i])])
                    ev_ptb_msk[th] = ev_msk[i]

                psv_of = {}
                for idx, (i, cu) in enumerate(work):
                    pair = gpair[0]
                    gpair[0] = (pair + 1) % 3
                    psv_pair = st_pairs[pair]
                    psv_of[i] = (pair, psv_pair)
                    for sub in (0, 1):
                        w = [(sC, ev_pair_rd[pair])]
                        if idx == 0 and sub == 0:
                            w += [(sV, e_ktq_done), (sV, e_v65_done),
                                  (sV, e_qt_done), (sG, e_masks)]
                        ev_S[i] = T1(lambda h, i=i, cu=cu, sub=sub, J=J, hp=hp,
                                     psv_pair=psv_pair: h.matmul(
                            psv_pair[:, sub * 512 + cu * 128:(sub + 1) * 512],
                            ktq[64 * sub:64 * (sub + 1),
                                hp * 1024 + i * 128:hp * 1024 + (i + 1) * 128],
                            qt[64 * sub:64 * (sub + 1),
                               hp * 2048 + J * 512 + cu * 128:
                               hp * 2048 + J * 512 + 512],
                            start=True, stop=True, tile_position=(64 * sub, 0)), w)
                    if idx >= 2:
                        ip, cup = work[idx - 2]
                        if diag:
                            emit_mask(ip, cup)
                        emit_av(ip, cup)
                    ew = [(sT, ev_S[i]), (sT, ev_ptb_war[pair])]
                    if ev_ptb_msk[pair]:
                        ew.append((sV, ev_ptb_msk[pair]))
                    fn_exp = (lambda h, pair=pair, psv_pair=psv_pair:
                              h.activation(
                        ptb[:, pair * 1024:(pair + 1) * 1024], psv_pair[:],
                        AF.Exp, bias=cst[:, C_NEGC:C_NEGC + 1], scale=0.125))
                    pend_pairs.append(pair)
                    if i % 2 == 1 or idx == len(work) - 1:
                        d = S1(fn_exp, ew)
                        for p in pend_pairs:
                            ev_pair_rd[p] = d
                        pend_pairs.clear()
                        for ii in range(i + 1):
                            dcover.setdefault(ii, d)
                    else:
                        S0(fn_exp, ew)
                for it in (-2, -1):
                    it7, cu7 = work[it]
                    if diag:
                        emit_mask(it7, cu7)
                    emit_av(it7, cu7)
                for sub in (0, 1):
                    hh = hp * 2 + sub
                    av4 = avp[:, sub * 512:sub * 512 + 260].rearrange(
                        "p (c o) -> p c o", o=65)
                    avden = av4[:, :, 64:65].rearrange("p c o -> p (c o)")
                    avnum = av4[:, :, 0:64]
                    slot = (J - 1) * 16 + hh
                    ev_sl = ev_slot.get(slot, 0)
                    if qr > 0:
                        first = pn_first.setdefault(slot, True)
                        pn_first[slot] = False
                        pslot = pnum3[:, slot, :].rearrange("p (c o) -> p c o", o=64)
                        if first:
                            ev1 = V1(lambda h, pslot=pslot, avnum=avnum:
                                     h.tensor_copy(pslot, avnum),
                                     waits=[(sT, ev_avmm), (sV, ev_sl)])
                            ev2 = V1(lambda h, slot=slot, avden=avden:
                                     h.tensor_copy(pden3[:, slot, :], avden))
                        else:
                            ev1 = V1(lambda h, pslot=pslot, avnum=avnum:
                                     h.tensor_tensor(pslot, avnum, pslot, ALU.add),
                                     waits=[(sT, ev_avmm), (sV, ev_sl)])
                            ev2 = V1(lambda h, slot=slot, avden=avden:
                                     h.tensor_tensor(pden3[:, slot, :], avden,
                                                     pden3[:, slot, :], ALU.add))
                        ev_av_evac[0] = max(ev1, ev2)
                        ev_slot[slot] = max(ev1, ev2)
                    else:
                        dbase = sub * 16
                        if J > 0:
                            evd = V1(lambda h, slot=slot, avden=avden, dbase=dbase:
                                     h.tensor_tensor(dcol[:, dbase:dbase + 4],
                                                     avden, pden3[:, slot, :],
                                                     ALU.add),
                                     waits=[(sT, ev_avmm),
                                            (sV, ev_dcol_free[sub])])
                            evn = V1(lambda h, slot=slot, avnum=avnum, sub=sub:
                                     h.tensor_tensor(
                                fnum[:, sub * 256:(sub + 1) * 256].rearrange(
                                    "p (c o) -> p c o", o=64), avnum,
                                pnum3[:, slot, :].rearrange("p (c o) -> p c o",
                                                            o=64), ALU.add))
                        else:
                            evd = V1(lambda h, avden=avden, dbase=dbase:
                                     h.tensor_copy(dcol[:, dbase:dbase + 4], avden),
                                     waits=[(sT, ev_avmm),
                                            (sV, ev_dcol_free[sub])])
                            evn = V1(lambda h, avnum=avnum, sub=sub: h.tensor_copy(
                                fnum[:, sub * 256:(sub + 1) * 256].rearrange(
                                    "p (c o) -> p c o", o=64), avnum))
                        evr = V1(lambda h, dbase=dbase: h.reciprocal(
                            dcol[:, dbase + 8:dbase + 12],
                            dcol[:, dbase:dbase + 4]), waits=[(sV, evd)])
                        ev_av_evac[0] = max(evn, evd)
                        evdiv = V1(lambda h, J=J, hh=hh, sub=sub, dbase=dbase:
                                   h.tensor_tensor(
                            attn3[:, J * 4:J * 4 + 4, hh * 64:(hh + 1) * 64],
                            fnum[:, sub * 256:(sub + 1) * 256].rearrange(
                                "p (c o) -> p c o", o=64),
                            dcol[:, dbase + 8:dbase + 12].unsqueeze(2)
                            .broadcast_to([128, 4, 64]), ALU.mult),
                            waits=[(sV, evr), (sV, evn)])
                        for c in range(4):
                            e_attn_done[J * 4 + c] = evdiv
                        ev_dcol_free[sub] = evdiv
        last_att_T = sT.n
    ev_pnum_done = sV.n
    e_att_exp_done = ev_pair_rd[0]
    e_attn_all = max(e_attn_done)
    if debug:
        dmaG(dbg["dbg_attn"], attn[:], waits=[(sV, e_attn_all)])

    if phases < 4:
        emit("sync", lambda h: h.nop(), waits=[(sS, sS.n)])
        emit("gpsimd", lambda h: h.nop(), waits=[(sG, sG.n)])
        emit("scalar", lambda h: h.nop(), waits=[(sC, sC.n)])
        emit("vector", lambda h: h.nop(), waits=[(sV, sV.n)])
        with nc.Block() as block:
            for name in ("sync", "gpsimd", "tensor", "vector", "scalar"):
                thunks = stream[name]
                def runner(h, thunks=thunks):
                    for t in thunks:
                        t(h)
                getattr(block, name)(runner)
        es.close()
        return nc
    e_w2t = dmaG(w2t_bf.rearrange("p (c n) -> p c n", c=HC),
                 d_w2t[:].rearrange("(c p) n -> p c n", p=128),
                 waits=[(sV, ev_pnum_done), (sV, e_attn_all)])
    e_wgt = dmaY(wgt_f[:].rearrange("p (c n) -> p c n", c=HC),
                 d_wgt[:].rearrange("(c p) n -> p c n", p=128))

    x2t3 = x2t.rearrange("p (c t) -> p c t", c=HC)
    gu_bufs = (attn[:, 0:4096], attn[:, 4096:8192])

    ev_ln_free = [0, 0]
    ev_yb_free = [0, 0]

    def ln_emit(r1v, ev_r1_v, dst, cbase, waits_dst=()):
        evs1 = V1(lambda h: h.reduce_sum(lncol[:, cbase:cbase + 1], r1v,
                                         mybir.AxisListType.X),
                  waits=[(sV, ev_r1_v), (sC, ev_ln_free[cbase // 8])])
        evs2 = S1(lambda h: h.activation(sq[:], r1v, AF.Square,
                                         accum_out=lncol[:, cbase + 1:cbase + 2]),
                  waits=[(sV, ev_r1_v), (sC, ev_ln_free[cbase // 8])])
        V0(lambda h: h.tensor_scalar(lncol[:, cbase + 2:cbase + 3],
                                     lncol[:, cbase:cbase + 1], I1024, None,
                                     ALU.mult), waits=[(sV, evs1), (sC, evs2)])
        ev_m = V1(lambda h: h.tensor_scalar(lncol[:, cbase + 3:cbase + 4],
                                            lncol[:, cbase + 1:cbase + 2], I1024,
                                            None, ALU.mult))
        V0(lambda h: h.tensor_mul(lncol[:, cbase + 4:cbase + 5],
                                  lncol[:, cbase + 2:cbase + 3],
                                  lncol[:, cbase + 2:cbase + 3]),
           waits=[(sV, ev_m)])
        ev_mq = V1(lambda h: h.tensor_scalar(lncol[:, cbase + 5:cbase + 6],
                                             lncol[:, cbase + 2:cbase + 3], -1.0,
                                             None, ALU.mult))
        ev_var = V1(lambda h: h.tensor_sub(lncol[:, cbase + 3:cbase + 4],
                                           lncol[:, cbase + 3:cbase + 4],
                                           lncol[:, cbase + 4:cbase + 5]),
                    waits=[(sV, ev_mq)])
        ev_std = S1(lambda h: h.activation(lncol[:, cbase + 4:cbase + 5],
                                           lncol[:, cbase + 3:cbase + 4], AF.Sqrt,
                                           bias=cst[:, C_EPS:C_EPS + 1]),
                    waits=[(sV, ev_var)])
        ev_rstd = V1(lambda h: h.reciprocal(lncol[:, cbase + 6:cbase + 7],
                                            lncol[:, cbase + 4:cbase + 5]),
                     waits=[(sC, ev_std)])
        ev_p1 = S1(lambda h: h.activation(sq[:], r1v, AF.Identity,
                                          bias=lncol[:, cbase + 5:cbase + 6]),
                   waits=[(sV, ev_rstd)])
        ev = S1(lambda h: h.activation(dst, sq[:], AF.Identity,
                                       scale=lncol[:, cbase + 6:cbase + 7]),
                waits=[(sC, ev_p1)] + list(waits_dst))
        ev_ln_free[cbase // 8] = ev
        return ev

    def gate_batch(half, ev_sig_last, prev_gate_ev):
        gv = [gsc[:, t * 16:(t + 1) * 16] for t in range(8)]
        wv = [gw[:, t * 128:(t + 1) * 128] for t in range(8)]
        P_, GS, GS2, CNT, ALW, MSK, RNK, SEL, WR, DEN = \
            0, 24, 28, 36, 40, 44, 60, 76, 92, 108
        pairs = [(0, 1), (0, 2), (0, 3), (1, 2), (1, 3), (2, 3)]

        def layer(fn, waits=()):
            for t in range(8):
                (V1 if t == 7 else V0)(fn(t), waits=list(waits) if t == 0 else ())
            return sV.n

        def l1(t):
            def f(h, t=t):
                for k, (a, b) in enumerate(pairs):
                    h.tensor_tensor(wv[t][:, P_ + k * 4:P_ + (k + 1) * 4],
                                    gv[t][:, a:16:4], gv[t][:, b:16:4], ALU.add)
            return f
        ev = layer(l1, [(sC, ev_sig_last), (sV, prev_gate_ev)])
        ev = layer(lambda t: (lambda h, t=t: h.tensor_copy(
            wv[t][:, GS:GS + 4], wv[t][:, P_:P_ + 4])), [(sV, ev)])
        for k in range(1, 6):
            ev = layer(lambda t, k=k: (lambda h, t=t, k=k: h.tensor_max(
                wv[t][:, GS:GS + 4], wv[t][:, GS:GS + 4],
                wv[t][:, P_ + k * 4:P_ + (k + 1) * 4])), [(sV, ev)])
        ev = layer(lambda t: (lambda h, t=t: h.tensor_copy(
            wv[t][:, GS2:GS2 + 4], wv[t][:, GS:GS + 4])), [(sV, ev)])
        ev = layer(lambda t: (lambda h, t=t: h.tensor_copy(
            wv[t][:, GS2 + 4:GS2 + 8], wv[t][:, GS:GS + 4])), [(sV, ev)])
        ev = layer(lambda t: (lambda h, t=t: h.tensor_tensor(
            wv[t][:, CNT:CNT + 4], wv[t][:, GS2 + 1:GS2 + 5],
            wv[t][:, GS:GS + 4], ALU.is_gt)), [(sV, ev)])
        for k in (2, 3):
            ev = layer(lambda t, k=k: (lambda h, t=t, k=k: h.tensor_tensor(
                wv[t][:, ALW:ALW + 4], wv[t][:, GS2 + k:GS2 + k + 4],
                wv[t][:, GS:GS + 4], ALU.is_gt)), [(sV, ev)])
            ev = layer(lambda t: (lambda h, t=t: h.tensor_add(
                wv[t][:, CNT:CNT + 4], wv[t][:, CNT:CNT + 4],
                wv[t][:, ALW:ALW + 4])), [(sV, ev)])
        ev = layer(lambda t: (lambda h, t=t: h.tensor_scalar(
            wv[t][:, ALW:ALW + 4], wv[t][:, CNT:CNT + 4], 1.5, None, ALU.is_le)),
            [(sV, ev)])
        ev = layer(lambda t: (lambda h, t=t: h.scalar_tensor_tensor(
            wv[t][:, MSK:MSK + 16].rearrange("p (a b) -> p a b", a=4),
            wv[t][:, ALW:ALW + 4].unsqueeze(2).broadcast_to([128, 4, 4]),
            4.0, gv[t][:].rearrange("p (a b) -> p a b", a=4),
            ALU.mult, ALU.add)), [(sV, ev)])
        ev = layer(lambda t: (lambda h, t=t: h.tensor_scalar(
            wv[t][:, MSK:MSK + 16], wv[t][:, MSK:MSK + 16], -4.0, None, ALU.add)),
            [(sV, ev)])
        for t in range(8):
            ev_c = V1(lambda h, t=t: h.tensor_tensor(
                gcmp[:].rearrange("p (a b) -> p a b", a=16),
                wv[t][:, MSK:MSK + 16].unsqueeze(1).broadcast_to([128, 16, 16]),
                wv[t][:, MSK:MSK + 16].unsqueeze(2).broadcast_to([128, 16, 16]),
                ALU.is_gt), waits=[(sV, ev)])
            ev = V1(lambda h, t=t: h.reduce_sum(
                wv[t][:, RNK:RNK + 16],
                gcmp[:].rearrange("p (a b) -> p a b", a=16),
                mybir.AxisListType.X), waits=[(sV, ev_c)])
        ev = layer(lambda t: (lambda h, t=t: h.tensor_scalar(
            wv[t][:, SEL:SEL + 16], wv[t][:, RNK:RNK + 16], 3.5, None, ALU.is_le)),
            [(sV, ev)])
        ev = layer(lambda t: (lambda h, t=t: h.tensor_mul(
            wv[t][:, WR:WR + 16], wv[t][:, SEL:SEL + 16], gv[t][:])), [(sV, ev)])
        ev = layer(lambda t: (lambda h, t=t: h.reduce_sum(
            wv[t][:, DEN:DEN + 1], wv[t][:, WR:WR + 16],
            mybir.AxisListType.X)), [(sV, ev)])
        ev = layer(lambda t: (lambda h, t=t: h.tensor_scalar(
            wv[t][:, DEN + 1:DEN + 2], wv[t][:, DEN:DEN + 1], 1e-6, None,
            ALU.add)), [(sV, ev)])
        ev_rc = layer(lambda t: (lambda h, t=t: h.reciprocal(
            wv[t][:, DEN + 2:DEN + 3], wv[t][:, DEN + 1:DEN + 2])), [(sV, ev)])
        ev = layer(lambda t: (lambda h, t=t, half=half: h.tensor_copy(
            comb[:, (half * 8 + t) * NE:(half * 8 + t) * NE + 16],
            wv[t][:, WR:WR + 16])), [(sV, ev_rc)])
        ev = layer(lambda t: (lambda h, t=t, half=half: h.tensor_scalar(
            comb[:, (half * 8 + t) * NE:(half * 8 + t) * NE + 16],
            comb[:, (half * 8 + t) * NE:(half * 8 + t) * NE + 16],
            wv[t][:, DEN + 2:DEN + 3], None, ALU.mult)), [(sV, ev)])
        return ev

    ev_xow = [0, 0]
    ev_xow = [0, 0]
    ev_r1_used = [0, 0]
    ev_r1_scal = [0, 0]
    ev_gate_all = 0
    e_ffn_prev_half_v = 0
    e_ffn_prev_half_T = last_att_T
    ev_wuse = [last_att_T, last_att_T]
    ev_gu_used = [0, 0]

    for half in range(2):
        ev_atT = [[0, 0], [0, 0]]
        ev_op_evac = [0, 0]
        e_x2t_done = [0, 0]
        e_x1_done = [0] * 8
        e_sig = [0] * 8
        e_sig_last = 0
        for tl in range(8):
            tt = half * 8 + tl
            r = tl % 2
            ev_xow[r] = dmaY(xow[:, r * 1024:(r + 1) * 1024],
                             d_xown[tt * 128:(tt + 1) * 128, :],
                             waits=[(sV, ev_r1_used[r])] +
                                   ([(sT, last_att_T)] if (half == 0 and tl < 2)
                                    else []))
            ev_atT[r] = transpose8(
                attn[:, tt * 1024:(tt + 1) * 1024], True,
                [(sV, e_attn_done[tt]), (sV, ev_pnum_done), (sV, ev_atT[r][1])] +
                ([(sC, e_att_exp_done)] if (half == 0 and tl == 0) else []),
                (atT[:, r * 1024:r * 1024 + 512],
                 atT[:, r * 1024 + 512:r * 1024 + 1024]),
                (e_x2t_done[0], e_x2t_done[1]))
            opp = ps01 if r == 0 else ps23
            ev_mm = 0
            w = [(sV, ev_atT[r][1]), (sG, e_w2t), (sV, ev_op_evac[r])]
            for c in range(HC):
                for hf in range(2):
                    ev_mm = T1(lambda h, r=r, c=c, hf=hf, opp=opp: h.matmul(
                        opp[:, hf * 512:(hf + 1) * 512],
                        atT[:, r * 1024 + c * 128:r * 1024 + (c + 1) * 128],
                        w2t_bf[:, c * 1024 + hf * 512:c * 1024 + hf * 512 + 512],
                        start=(c == 0), stop=(c == HC - 1)), w)
                    w = []
            ev_r1 = V1(lambda h, r=r, opp=opp: h.tensor_add(
                r1b[:, r * 1024:(r + 1) * 1024], opp[:],
                xow[:, r * 1024:(r + 1) * 1024]),
                waits=[(sT, ev_mm), (sS, ev_xow[r]), (sC, ev_r1_scal[r]),
                       (sOut, ev_yb_free[r])])
            ev_op_evac[r] = ev_r1
            wdst = []
            if tl == 0:
                wdst = [(sT, e_ffn_prev_half_T), (sV, e_ffn_prev_half_v)]
            ev_x1 = ln_emit(r1b[:, r * 1024:(r + 1) * 1024], ev_r1,
                            x1res[:, tl * 1024:(tl + 1) * 1024], r * 8,
                            waits_dst=wdst)
            e_x1_done[tl] = ev_x1
            ev_r1_scal[r] = ev_x1
            ev_r1_used[r] = ev_r1
            ev_tx = transpose8(
                x1res[:, tl * 1024:(tl + 1) * 1024], False,
                [(sC, ev_x1)],
                (x2t3[:, 0:4, tl * 128:(tl + 1) * 128],
                 x2t3[:, 4:8, tl * 128:(tl + 1) * 128]),
                (e_x2t_done[0], e_x2t_done[1]),
                extra_evac=(x1t_f[:, 0:512], x1t_f[:, 512:1024]))
            e_x2t_done = ev_tx
            ev_gmm = 0
            w = [(sV, ev_tx[1]), (sS, e_wgt), (sC, e_sig_last)]
            for c in range(HC):
                ev_gmm = T1(lambda h, c=c: h.matmul(
                    ps6[:, 0:16], x1t_f[:, c * 128:(c + 1) * 128],
                    wgt_f[:, c * 16:(c + 1) * 16], start=(c == 0),
                    stop=(c == HC - 1)), w)
                w = []
            e_sig[tl] = S1(lambda h, tl=tl: h.activation(
                gsc[:, tl * 16:(tl + 1) * 16], ps6[:, 0:16], AF.Sigmoid),
                waits=[(sT, ev_gmm), (sV, ev_gate_all)])
            e_sig_last = e_sig[tl]
        ev_gate_all = gate_batch(half, e_sig_last, ev_gate_all)
        e_comb_all = ev_gate_all
        if debug and half == 0:
            dmaG(dbg["dbg_x1"], x1res[:, 0:8192], waits=[(sC, max(e_x1_done))])
            dmaG(dbg["dbg_x2t"], x2t[:, 0:8192], waits=[(sV, max(e_x2t_done))])

        e_wld = [0, 0]
        ev_dn_evac = [ev_op_evac[0], ev_op_evac[1]]
        ev_x1_acc = list(e_x1_done)
        ev_guw = 0
        for e in range(min(2, NE)):
            pe = e % 2
            dmaG(w13[:, pe * 4096:(pe + 1) * 4096]
                 .rearrange("p (c n) -> p c n", c=HC),
                 d_w1a[e].rearrange("(c p) n -> p c n", p=128),
                 waits=[(sT, ev_wuse[pe])])
            dmaG(w13[:, 8192 + pe * 4096:8192 + (pe + 1) * 4096]
                 .rearrange("p (c n) -> p c n", c=HC),
                 d_w3a[e].rearrange("(c p) n -> p c n", p=128))
            e_wld[pe] = dmaG(
                w2bf[:, pe * 4096:(pe + 1) * 4096]
                .rearrange("p (c n) -> p c n", c=4),
                d_w2a[e].rearrange("(c p) n -> p c n", p=128))
        for e in range(NE):
            pe = e % 2
            guv = gu_bufs[pe]
            ev_sil = 0
            for fc in range(4):
                for tc in range(2):
                    w = [(sG, e_wld[pe]), (sC, ev_sil), (sV, ev_guw)]
                    if fc == 0 and tc == 0 and e < 2:
                        w += [(sV, max(e_x2t_done)), (sC, e_sig_last)]
                    ev_g = 0
                    for c in range(HC):
                        ev_g = T1(lambda h, pe=pe, fc=fc, tc=tc, c=c: h.matmul(
                            ps6, w13[:, pe * 4096 + c * 512 + fc * 128:
                                        pe * 4096 + c * 512 + (fc + 1) * 128],
                            x2t[:, c * 1024 + tc * 512:c * 1024 + (tc + 1) * 512],
                            start=(c == 0), stop=(c == HC - 1)), w)
                        w = []
                    ev_u = 0
                    for c in range(HC):
                        ev_u = T1(lambda h, pe=pe, fc=fc, tc=tc, c=c: h.matmul(
                            ps7, w13[:, 8192 + pe * 4096 + c * 512 + fc * 128:
                                        8192 + pe * 4096 + c * 512 + (fc + 1) * 128],
                            x2t[:, c * 1024 + tc * 512:c * 1024 + (tc + 1) * 512],
                            start=(c == 0), stop=(c == HC - 1)))
                    ev_sil = S1(lambda h, tc=tc: h.activation(
                        silu_t[:, tc * 512:(tc + 1) * 512], ps6, AF.Silu),
                        waits=[(sT, ev_g), (sV, ev_guw)])
                    ev_guw = V1(lambda h, guv=guv, fc=fc, tc=tc: h.tensor_tensor(
                        guv[:, fc * 1024 + tc * 512:fc * 1024 + (tc + 1) * 512],
                        silu_t[:, tc * 512:(tc + 1) * 512], ps7, ALU.mult),
                        waits=[(sC, ev_sil), (sT, ev_u)])
            for tl in range(8):
                tt = half * 8 + tl
                dnp = ps01 if tl % 2 == 0 else ps23
                ev_mm = 0
                w = [(sV, ev_guw), (sV, ev_dn_evac[tl % 2]), (sG, e_wld[pe])]
                for fc in range(4):
                    for hf in range(2):
                        ev_mm = T1(lambda h, pe=pe, fc=fc, hf=hf, tl=tl, dnp=dnp:
                                   h.matmul(
                            dnp[:, hf * 512:(hf + 1) * 512],
                            gu_bufs[pe][:, fc * 1024 + tl * 128:
                                        fc * 1024 + (tl + 1) * 128],
                            w2bf[:, pe * 4096 + fc * 1024 + hf * 512:
                                 pe * 4096 + fc * 1024 + hf * 512 + 512],
                            start=(fc == 0), stop=(fc == 3)), w)
                        w = []
                ev_acc = V1(lambda h, tl=tl, tt=tt, e=e, dnp=dnp:
                            h.scalar_tensor_tensor(
                    x1res[:, tl * 1024:(tl + 1) * 1024], dnp[:],
                    comb[:, tt * NE + e:tt * NE + e + 1],
                    x1res[:, tl * 1024:(tl + 1) * 1024], ALU.mult, ALU.add),
                    waits=[(sT, ev_mm), (sC, e_x1_done[tl]),
                           (sV, e_comb_all),
                           (sV, ev_x1_acc[tl])] +
                          ([(sGc, e_comb1)] if e >= 16 else []))
                ev_dn_evac[tl % 2] = ev_acc
                ev_x1_acc[tl] = ev_acc
            ev_wuse[pe] = sT.n
            ev_gu_used[pe] = sT.n
            if e + 2 < NE:
                dmaG(w13[:, pe * 4096:(pe + 1) * 4096]
                     .rearrange("p (c n) -> p c n", c=HC),
                     d_w1a[e + 2].rearrange("(c p) n -> p c n", p=128),
                     waits=[(sT, ev_wuse[pe])])
                dmaG(w13[:, 8192 + pe * 4096:8192 + (pe + 1) * 4096]
                     .rearrange("p (c n) -> p c n", c=HC),
                     d_w3a[e + 2].rearrange("(c p) n -> p c n", p=128))
                e_wld[pe] = dmaG(
                    w2bf[:, pe * 4096:(pe + 1) * 4096]
                    .rearrange("p (c n) -> p c n", c=4),
                    d_w2a[e + 2].rearrange("(c p) n -> p c n", p=128))
        for tl in range(8):
            tt = half * 8 + tl
            r = tl % 2
            ev_y = ln_emit(x1res[:, tl * 1024:(tl + 1) * 1024], ev_x1_acc[tl],
                           yb[:, r * 1024:(r + 1) * 1024], r * 8,
                           waits_dst=[(sOut, ev_yb_free[r])])
            ev_yb_free[r] = dmaY(d_y[tt * 128:(tt + 1) * 128, :],
                                 yb[:, r * 1024:(r + 1) * 1024],
                                 waits=[(sC, ev_y)], store=True)
        if debug and half == 0:
            dmaG(dbg["dbg_comb"], comb[:, 0:8 * NE], waits=[(sV, e_comb_all)])
        e_ffn_prev_half_v = sV.n
        e_ffn_prev_half_T = sT.n

    emit("sync", lambda h: h.nop(), waits=[(sOut, sOut.n)])

    with nc.Block() as block:
        for name in ("sync", "gpsimd", "tensor", "vector", "scalar"):
            thunks = stream[name]

            def runner(h, thunks=thunks):
                for t in thunks:
                    t(h)
            getattr(block, name)(runner)
    es.close()
    return nc


_CACHE = {}


def _prep_host(inputs):
    f32 = np.float32
    x = np.ascontiguousarray(inputs["L0_attn_x"], dtype=f32)
    w1t = np.ascontiguousarray(np.asarray(inputs["L0_attn_W1"], f32).T)
    w2t = np.ascontiguousarray(np.asarray(inputs["L0_attn_W2"], f32).T)
    wgt = np.ascontiguousarray(np.asarray(inputs["L0_ffn_w_gate"], f32).T)
    w1e = np.asarray(inputs["L0_ffn_w1_e"], f32)
    w3e = np.asarray(inputs["L0_ffn_w3_e"], f32)
    w2e = np.asarray(inputs["L0_ffn_w2_e"], f32)
    w1s = np.asarray(inputs["L0_ffn_w1_s"], f32)
    w3s = np.asarray(inputs["L0_ffn_w3_s"], f32)
    w2s = np.asarray(inputs["L0_ffn_w2_s"], f32)
    w1a = np.ascontiguousarray(np.concatenate(
        [w1e.transpose(0, 2, 1), w1s.T.reshape(E, 2, FF).transpose(1, 0, 2)], 0))
    w3a = np.ascontiguousarray(np.concatenate(
        [w3e.transpose(0, 2, 1), w3s.T.reshape(E, 2, FF).transpose(1, 0, 2)], 0))
    w2a = np.ascontiguousarray(np.concatenate(
        [w2e.transpose(0, 2, 1), w2s.T.reshape(2, FF, E)], 0))
    tri = np.triu(np.ones((128, 128), f32))
    ones = np.ones((128, 128), f32)
    zeros = np.zeros((128, 128), f32)
    masks_p = [
        np.ascontiguousarray(np.stack([tri if i % 2 == 0 else zeros
                                       for i in range(8)])),
        np.ascontiguousarray(np.stack([ones if i % 2 == 0 else tri
                                       for i in range(8)])),
    ]
    in_maps, metas = [], []
    for core in range(8):
        b, p = core // 2, core % 2
        blocks = np.arange(NTO) * 2 + p
        rows = (blocks[:, None] * 128 + np.arange(128)[None, :]).reshape(-1)
        in_maps.append({
            "xseq": np.ascontiguousarray(x[b]),
            "xown": np.ascontiguousarray(x[b][rows]),
            "posq": np.ascontiguousarray(rows.astype(f32).reshape(NTO, 128).T),
            "masks": masks_p[p],
            "w1t": w1t, "w2t": w2t, "wgt": wgt,
            "w1a": w1a, "w3a": w3a, "w2a": w2a,
        })
        metas.append((b, blocks))
    return in_maps, metas


def kernel(L0_attn_x, L0_attn_W1, L0_attn_W2, L0_ffn_w_gate, L0_ffn_w1_e,
           L0_ffn_w3_e, L0_ffn_w2_e, L0_ffn_w1_s, L0_ffn_w3_s, L0_ffn_w2_s):
    inputs = dict(L0_attn_x=L0_attn_x, L0_attn_W1=L0_attn_W1,
                  L0_attn_W2=L0_attn_W2, L0_ffn_w_gate=L0_ffn_w_gate,
                  L0_ffn_w1_e=L0_ffn_w1_e, L0_ffn_w3_e=L0_ffn_w3_e,
                  L0_ffn_w2_e=L0_ffn_w2_e, L0_ffn_w1_s=L0_ffn_w1_s,
                  L0_ffn_w3_s=L0_ffn_w3_s, L0_ffn_w2_s=L0_ffn_w2_s)
    if "nc" not in _CACHE:
        _CACHE["nc"] = build_nc(debug=False)
    nc = _CACHE["nc"]
    in_maps, metas = _prep_host(inputs)
    res = run_bass_kernel_spmd(nc, in_maps, core_ids=list(range(8)))
    out = np.empty((B, T, E), dtype=np.float32)
    for core in range(8):
        b, blocks = metas[core]
        y = np.asarray(res.results[core]["y"]).reshape(NTO, 128, E)
        out.reshape(B, 32, 128, E)[b, blocks] = y
    return out


# version 10
# speedup vs baseline: 1.4798x; 1.0673x over previous
import numpy as np
import ml_dtypes
from contextlib import ExitStack

import concourse.bass as bass
import concourse.mybir as mybir
from concourse.bass_utils import run_bass_kernel_spmd

F32 = mybir.dt.float32
BF16 = mybir.dt.bfloat16
I32 = mybir.dt.int32
AF = mybir.ActivationFunctionType
ALU = mybir.AluOpType

B, T, E = 4, 4096, 1024
NH, D, HP, HC = 16, 64, 8, 8
NTO = 16
NQ = NTO * 128
NE = 18
FF = 512
CSHIFT = -20.0
I1024 = 1.0 / 1024.0


class Sem:
    def __init__(self, es, nc, name):
        self.h = es.enter_context(nc.semaphore(name))
        self.n = 0


def build_nc(debug=False, phases=5, qtiles=NTO, quarters=4, do_att=True, kvp='tmve'):
    nc = bass.Bass("TRN2", target_bir_lowering=False, detect_race_conditions=False)
    es = ExitStack()

    d_xseq = nc.dram_tensor("xseq", [T, E], F32, kind="ExternalInput").ap()
    d_xown = nc.dram_tensor("xown", [NQ, E], F32, kind="ExternalInput").ap()
    d_posq = nc.dram_tensor("posq", [128, NTO], F32, kind="ExternalInput").ap()
    d_masks = nc.dram_tensor("masks", [8, 128, 128], F32, kind="ExternalInput").ap()
    d_w1t = nc.dram_tensor("w1t", [E, 3 * E], F32, kind="ExternalInput").ap()
    d_w2t = nc.dram_tensor("w2t", [E, E], F32, kind="ExternalInput").ap()
    d_wgt = nc.dram_tensor("wgt", [E, 16], F32, kind="ExternalInput").ap()
    d_w1a = nc.dram_tensor("w1a", [NE, E, FF], F32, kind="ExternalInput").ap()
    d_w3a = nc.dram_tensor("w3a", [NE, E, FF], F32, kind="ExternalInput").ap()
    d_w2a = nc.dram_tensor("w2a", [NE, FF, E], F32, kind="ExternalInput").ap()
    d_y = nc.dram_tensor("y", [NQ, E], F32, kind="ExternalOutput").ap()
    dbg = {}
    if debug:
        for name, shape in [
            ("dbg_qt", [128, HP * 2048]), ("dbg_kt", [128, HP * 1024]),
            ("dbg_v65", [128, 8 * 1040]), ("dbg_attn", [128, NTO * 1024]),
            ("dbg_x1", [128, 8 * 1024]), ("dbg_comb", [128, 8 * NE]),
            ("dbg_qtab", [128, NTO * 128]), ("dbg_x2t", [128, 8 * 1024]),
            ("dbg_cst", [128, 8]), ("dbg_invf", [128, 64]),
            ("dbg_invf2", [128, 64]), ("dbg_tgi", [128, 128]),
        ]:
            dbg[name] = nc.dram_tensor(name, shape, F32, kind="ExternalOutput").ap()

    def sb(name, shape, dt):
        return es.enter_context(nc.sbuf_tensor(name, shape, dt))

    cst = sb("cst", [128, 8], F32)
    C_ONE, C_2PI, C_I2PI, C_LN1E4, C_NEGC, C_E4, C_EPS, C_PI = 0, 1, 2, 3, 4, 5, 6, 7
    invf = sb("invf", [128, 64], F32)
    id_f32 = sb("id_f32", [128, 128], F32)
    id_bf = sb("id_bf", [128, 128], BF16)
    masks = sb("masksb", [128, 8 * 128], BF16)
    qtab = sb("qtab", [128, NTO * 128], F32)
    ktab = qtab[:, 0:1024]
    comb = sb("comb", [128, NTO * NE], F32)
    pden = sb("pden", [128, 3 * 16 * 4], F32)
    xbt = sb("xbt", [128, 4 * 1024], BF16)
    xk_bf = xbt[:, 0:2048]
    xkT = xbt[:, 2048:4096]
    kro = sb("kro", [128, 2 * 1024], BF16)
    ptb = sb("ptb", [128, 6 * 512], BF16)
    rope_t1 = sb("rope_t1", [128, 1024], BF16)
    rope_t2 = sb("rope_t2", [128, 1024], BF16)
    tgen = sb("tgen", [128, 8 * 64], F32)
    tgi = sb("tgi", [128, 4 * 64], I32)
    idsc = tgi[:, 0:128]
    tposi = sb("tposi", [128, 8], I32)
    poscol = sb("poscol", [128, 16], F32)
    fnum = sb("fnum", [128, 2 * 256], F32)
    dcol = sb("dcol", [128, 32], F32)
    sq = sb("sq", [128, 1024], F32)
    lncol = sb("lncol", [128, 16], F32)
    gsc = sb("gsc", [128, 8 * 16], F32)
    gw = None
    G_PS, G_GS, G_GS2, G_CNT, G_ALW, G_MSK, G_RANK, G_SEL, G_WR, G_DEN = \
        0, 24, 28, 36, 40, 44, 60, 76, 92, 108
    gw = ptb[:].bitcast(F32)
    gcmp = rope_t1[:].bitcast(F32)[:, 0:256]
    silu_t = kro[:, 0:1024]
    xow = xbt[:].bitcast(F32)
    r1b = sb("r1b", [128, 2 * 1024], F32)
    yb = r1b
    wgt_f = sb("wgt_f", [128, HC * 16], F32)

    wkv = sb("wkv", [128, HC * 2048], BF16)
    ktq = sb("ktq", [128, HP * 1024], BF16)
    v65 = sb("v65", [128, 8 * 1040], BF16)
    qt = sb("qt", [128, HP * 2048], BF16)
    pnum = sb("pnum", [128, 3 * 16 * 256], BF16)
    attn = sb("attn", [128, NTO * 1024], BF16)

    x1res = wkv[:].bitcast(F32)
    x2t = ktq[:]
    w2bf = v65[:, 0:8192]
    w13 = qt[:]
    wq_bf = pnum[:, 0:8192]
    w2t_bf = pnum[:, 0:8192]
    atT = pnum[:, 8192:10240]
    x1t_f = pnum[:, 10240:12288].bitcast(F32)

    ps01 = es.enter_context(nc.psum_tensor("ps01", [128, 1024], F32))
    ps23 = es.enter_context(nc.psum_tensor("ps23", [128, 1024], F32))
    ps45 = es.enter_context(nc.psum_tensor("ps45", [128, 1024], F32))
    psS = es.enter_context(nc.psum_tensor("psS", [128, 1024], F32))
    ps6 = psS[:, 0:512]
    ps7 = psS[:, 512:1024]
    ps45b = ps45[:].bitcast(BF16)

    sS = Sem(es, nc, "sS")
    sG = Sem(es, nc, "sG")
    sT = Sem(es, nc, "sT")
    sV = Sem(es, nc, "sV")
    sC = Sem(es, nc, "sC")
    sGc = Sem(es, nc, "sGc")
    sOut = Sem(es, nc, "sOut")

    stream = {e: [] for e in ("sync", "gpsimd", "tensor", "vector", "scalar")}

    def emit(eng, fn, waits=(), incs=(), dma=False):
        for s, v in waits:
            assert v <= s.n, f"wait on future event: {v} > {s.n} ({eng})"
        for s, k in incs:
            s.n += k
        w = tuple((s.h, v) for s, v in waits if v > 0)
        i = tuple((s.h, k) for s, k in incs)
        drain_incs = (not dma) and eng in ("vector", "scalar", "gpsimd") and i

        def thunk(h, w=w, i=i, fn=fn, drain_incs=drain_incs):
            for sem, v in w:
                h.wait_ge(sem, v)
            inst = fn(h)
            if drain_incs:
                d = h.drain(fusable=False)
                for sem, k in i:
                    d.then_inc(sem, k)
            else:
                for sem, k in i:
                    inst.then_inc(sem, k)
        stream[eng].append(thunk)

    def T1(fn, waits=()):
        emit("tensor", fn, waits, [(sT, 1)])
        return sT.n

    def V1(fn, waits=()):
        emit("vector", fn, waits, [(sV, 1)])
        return sV.n

    def V0(fn, waits=()):
        emit("vector", fn, waits)

    def S1(fn, waits=()):
        emit("scalar", fn, waits, [(sC, 1)])
        return sC.n

    def S0(fn, waits=()):
        emit("scalar", fn, waits)

    def G1(fn, waits=()):
        emit("gpsimd", fn, waits, [(sGc, 1)])
        return sGc.n

    def G0(fn, waits=()):
        emit("gpsimd", fn, waits)

    def dmaY(out, in_, waits=(), store=False):
        emit("sync", lambda h: h.dma_start(out=out, in_=in_), waits,
             [(sOut if store else sS, 16)], dma=True)
        return (sOut if store else sS).n

    def dmaG(out, in_, waits=()):
        emit("gpsimd", lambda h: h.dma_start(out=out, in_=in_), waits, [(sG, 16)],
             dma=True)
        return sG.n

    G0(lambda h: h.memset(cst[:, C_ONE:C_ONE + 1], 1.0))
    G0(lambda h: h.memset(cst[:, C_NEGC:C_NEGC + 1], CSHIFT))
    G0(lambda h: h.memset(cst[:, C_EPS:C_EPS + 1], 1e-5))
    e0 = G1(lambda h: h.memset(cst[:, C_E4:C_E4 + 1], 10000.0))
    S0(lambda h: h.activation(cst[:, C_2PI:C_2PI + 1], cst[:, C_ONE:C_ONE + 1],
                              AF.Arctan), waits=[(sGc, e0)])
    e1 = S1(lambda h: h.activation(cst[:, C_LN1E4:C_LN1E4 + 1],
                                   cst[:, C_E4:C_E4 + 1], AF.Ln))
    e_rc0 = V1(lambda h: h.reciprocal(cst[:, C_I2PI:C_I2PI + 1],
                                      cst[:, C_2PI:C_2PI + 1]),
               waits=[(sC, e1)])
    V0(lambda h: h.tensor_scalar(cst[:, C_2PI:C_2PI + 1], cst[:, C_2PI:C_2PI + 1],
                                 8.0, None, ALU.mult), waits=[(sV, e_rc0)])
    V0(lambda h: h.tensor_scalar(cst[:, C_I2PI:C_I2PI + 1],
                                 cst[:, C_I2PI:C_I2PI + 1], 0.125, None,
                                 ALU.mult), waits=[(sV, e_rc0)])
    e_cst = V1(lambda h: h.tensor_scalar(cst[:, C_PI:C_PI + 1],
                                         cst[:, C_2PI:C_2PI + 1], 0.5, None,
                                         ALU.mult))

    e2 = G1(lambda h: h.iota(tgi[:, 0:32], pattern=[[1, 32]], base=0,
                             channel_multiplier=0))
    V0(lambda h: h.tensor_copy(invf[:, 0:32], tgi[:, 0:32]), waits=[(sGc, e2)])
    V0(lambda h: h.tensor_scalar(invf[:, 0:32], invf[:, 0:32], -0.03125, None,
                                 ALU.mult))
    e3 = V1(lambda h: h.tensor_scalar(invf[:, 0:32], invf[:, 0:32],
                                      cst[:, C_LN1E4:C_LN1E4 + 1], None, ALU.mult),
            waits=[(sC, e1)])
    e4 = S1(lambda h: h.activation(invf[:, 0:32], invf[:, 0:32], AF.Exp),
            waits=[(sV, e3)])
    e_invf = V1(lambda h: h.tensor_copy(invf[:, 32:64], invf[:, 0:32]),
                waits=[(sC, e4)])

    if debug:
        dmaG(dbg["dbg_invf2"], invf[:], waits=[(sV, e_invf)])
    e5 = G1(lambda h: h.iota(idsc[:], pattern=[[1, 128]], base=0,
                             channel_multiplier=-1), waits=[(sV, e3)])
    V0(lambda h: h.tensor_copy(id_f32[:], idsc[:]), waits=[(sGc, e5)])
    e_idf = V1(lambda h: h.tensor_scalar(id_f32[:], id_f32[:], 0.0, None,
                                         ALU.is_equal))
    e_id = V1(lambda h: h.tensor_copy(id_bf[:], id_f32[:]), waits=[(sV, e_idf)])

    e_masks = dmaG(masks[:].rearrange("p (i c) -> p i c", i=8),
                   d_masks[:, :, :].transpose([1, 0, 2]))
    e_comb1 = G1(lambda h: h.memset(
        comb[:].rearrange("p (t e) -> p t e", e=NE)[:, :, 16:18], 1.0))

    if phases < 1:
        if debug:
            G0(lambda h: h.tensor_copy(tgen[:, 0:128].bitcast(I32) if False else tgen[:, 0:128], tgi[:].bitcast(F32)) if False else h.nop())
            dmaG(dbg["dbg_tgi"], tgi[:].bitcast(F32), waits=[(sV, e_invf)])
            dmaG(dbg["dbg_invf"], invf[:], waits=[(sV, e_invf)])
            dmaG(dbg["dbg_cst"], cst[:], waits=[(sV, e_invf)])
        emit("sync", lambda h: h.nop(), waits=[(sS, sS.n)])
        emit("gpsimd", lambda h: h.nop(), waits=[(sG, sG.n)])
        with nc.Block() as block:
            for name in ("sync", "gpsimd", "tensor", "vector", "scalar"):
                thunks = stream[name]
                def runner(h, thunks=thunks):
                    for t in thunks:
                        t(h)
                getattr(block, name)(runner)
        es.close()
        return nc

    def gen_tables_batch(tab, tiles, pcols, extra_waits, prev_events):
        n = len(tiles)
        ang = [tab[:, j * 128:j * 128 + 64] for j in tiles]
        scr = [tab[:, j * 128 + 64:j * 128 + 128] for j in tiles]
        rr = [tgen[:, i * 128:i * 128 + 64] for i in range(n)]
        sh = [tgen[:, i * 128 + 64:i * 128 + 128] for i in range(n)]
        ii = [tgi[:, i * 64:(i + 1) * 64] for i in range(n)]
        ev = 0

        def layer(fn_per_tile, waits=()):
            for i in range(n):
                (V1 if i == n - 1 else V0)(fn_per_tile(i),
                                           waits=list(waits) if i == 0 else ())
            return sV.n

        w0 = [(sV, e_invf), (sV, e_cst)] + list(extra_waits) + list(prev_events)
        ev = layer(lambda i: (lambda h, i=i: h.tensor_scalar(
            ang[i], invf[:, 0:64], pcols[i], None, ALU.mult)), w0)
        ev = layer(lambda i: (lambda h, i=i: h.tensor_scalar(
            scr[i], ang[i], cst[:, C_I2PI:C_I2PI + 1], None, ALU.mult)),
            [(sV, ev)])
        ev = layer(lambda i: (lambda h, i=i: h.tensor_copy(ii[i], scr[i])),
                   [(sV, ev)])
        ev = layer(lambda i: (lambda h, i=i: h.tensor_copy(scr[i], ii[i])),
                   [(sV, ev)])
        ev = layer(lambda i: (lambda h, i=i: h.tensor_scalar(
            scr[i], scr[i], cst[:, C_2PI:C_2PI + 1], None, ALU.mult)), [(sV, ev)])
        ev = layer(lambda i: (lambda h, i=i: h.tensor_sub(rr[i], ang[i], scr[i])),
                   [(sV, ev)])
        ev = layer(lambda i: (lambda h, i=i: h.tensor_scalar(
            sh[i], rr[i], cst[:, C_PI:C_PI + 1], None, ALU.is_gt)), [(sV, ev)])
        ev = layer(lambda i: (lambda h, i=i: h.tensor_scalar(
            sh[i], sh[i], cst[:, C_2PI:C_2PI + 1], None, ALU.mult)), [(sV, ev)])
        ev = layer(lambda i: (lambda h, i=i: h.tensor_sub(rr[i], rr[i], sh[i])),
                   [(sV, ev)])
        ev = layer(lambda i: (lambda h, i=i: h.scalar_tensor_tensor(
            sh[i], rr[i], -1.0,
            cst[:, C_PI:C_PI + 1].broadcast_to([128, 64]), ALU.mult, ALU.is_gt)),
            [(sV, ev)])
        ev = layer(lambda i: (lambda h, i=i: h.tensor_scalar(
            sh[i], sh[i], cst[:, C_2PI:C_2PI + 1], None, ALU.mult)), [(sV, ev)])
        ev_r = layer(lambda i: (lambda h, i=i: h.tensor_add(rr[i], rr[i], sh[i])),
                     [(sV, ev)])
        for i in range(n):
            S0(lambda h, i=i: h.activation(scr[i], rr[i], AF.Sin),
               waits=[(sV, ev_r)] if i == 0 else ())
        for i in range(n):
            (S1 if i == n - 1 else S0)(lambda h, i=i: h.activation(
                sh[i], rr[i], AF.Sin, scale=0.5))
        ev_s = sC.n
        ev = layer(lambda i: (lambda h, i=i: h.tensor_mul(sh[i], sh[i], sh[i])),
                   [(sC, ev_s)])
        ev = layer(lambda i: (lambda h, i=i: h.tensor_scalar(
            ang[i], sh[i], -2.0, 1.0, ALU.mult, ALU.add)), [(sV, ev)])
        ev = layer(lambda i: (lambda h, i=i: h.tensor_scalar(
            tab[:, tiles[i] * 128 + 64:tiles[i] * 128 + 96],
            tab[:, tiles[i] * 128 + 64:tiles[i] * 128 + 96],
            -1.0, None, ALU.mult)), [(sV, ev)])
        return ev, ev_s

    def rope_emit(psrc, tab, off, waits, out_bf):
        ps3 = psrc.rearrange("p (h d) -> p h d", d=64)
        t13 = rope_t1[:, 0:1024].rearrange("p (h d) -> p h d", d=64)
        V0(lambda h: h.tensor_tensor(
            t13[:, :, 0:32], ps3[:, :, 32:64],
            tab[:, off + 64:off + 96].unsqueeze(1).broadcast_to([128, 16, 32]),
            ALU.mult), waits=waits)
        V0(lambda h: h.tensor_tensor(
            t13[:, :, 32:64], ps3[:, :, 0:32],
            tab[:, off + 96:off + 128].unsqueeze(1).broadcast_to([128, 16, 32]),
            ALU.mult))
        ev_m = V1(lambda h: h.tensor_tensor(
            rope_t2[:, 0:1024].rearrange("p (h d) -> p h d", d=64), ps3,
            tab[:, off:off + 64].unsqueeze(1).broadcast_to([128, 16, 64]),
            ALU.mult))
        return V1(lambda h: h.tensor_add(out_bf, rope_t1[:, 0:1024],
                                         rope_t2[:, 0:1024]),
                  waits=[(sV, ev_m)])

    def transpose8(src, isbf, waits_first, evac_aps, ev_prev_reads,
                   extra_evac=None):
        ident = id_bf if isbf else id_f32
        psv = ps45b if isbf else ps45[:]
        bank_off = 1024 if isbf else 512
        evts = [0, 0]
        for c in range(8):
            off = (c % 4) * 128 + (0 if c < 4 else bank_off)
            w = list(waits_first) if c == 0 else []
            if c == 0 and ev_prev_reads[0]:
                w.append((sV, ev_prev_reads[0]))
            if c == 4 and ev_prev_reads[1]:
                w.append((sV, ev_prev_reads[1]))
            ev = T1(lambda h, off=off, c=c: h.transpose(
                psv[:, off:off + 128], src[:, c * 128:(c + 1) * 128], ident[:]), w)
            if c == 3:
                evts[0] = ev
            if c == 7:
                evts[1] = ev
        out = []
        for half in range(2):
            pslice = psv[:, half * bank_off:half * bank_off + 512]
            if extra_evac is None:
                out.append(V1(lambda h, pslice=pslice, half=half: h.tensor_copy(
                    evac_aps[half], pslice), waits=[(sT, evts[half])]))
            else:
                V0(lambda h, pslice=pslice, half=half: h.tensor_copy(
                    evac_aps[half], pslice), waits=[(sT, evts[half])])
                out.append(V1(lambda h, pslice=pslice, half=half: h.tensor_copy(
                    extra_evac[half], pslice)))
        return out

    e_wq = dmaG(wq_bf.rearrange("p (c n) -> p c n", c=HC),
                d_w1t[:, 0:E].rearrange("(c p) n -> p c n", p=128))

    qt3 = qt[:].rearrange("p (hp q) -> p hp q", hp=HP)
    e_posq = dmaY(poscol[:, 0:16], d_posq[:])
    gen_ev, gen_sc = 0, 0
    for bb in range(4):
        tiles = list(range(bb * 4, (bb + 1) * 4))
        pcols = [poscol[:, j:j + 1] for j in tiles]
        gen_ev, gen_sc = gen_tables_batch(
            qtab, tiles, pcols, [(sS, e_posq)],
            [(sV, gen_ev), (sC, gen_sc)] if bb else [(sV, e_id)])
    e_qtab = gen_ev
    e_qtab_t = [e_qtab] * NTO

    ev_xqT = [[0, 0], [0, 0]]
    ev_qt_ev = [0, 0]
    ev_rope_prev = 0
    for j in range(qtiles):
        r = j % 2
        ev_x = dmaG(xk_bf[:, r * 1024:(r + 1) * 1024],
                    d_xown[j * 128:(j + 1) * 128, :],
                    waits=[(sV, ev_xqT[r][1])])
        ev_xqT[r] = transpose8(
            xk_bf[:, r * 1024:(r + 1) * 1024], True,
            [(sG, ev_x), (sV, e_id)],
            (xkT[:, r * 1024:r * 1024 + 512], xkT[:, r * 1024 + 512:r * 1024 + 1024]),
            (ev_qt_ev[0], ev_qt_ev[1]))
        ev_mm = 0
        w = [(sV, ev_xqT[r][1]), (sG, e_wq), (sV, ev_rope_prev)]
        for c in range(HC):
            for half in range(2):
                ev_mm = T1(lambda h, r=r, c=c, half=half: h.matmul(
                    ps01[:, half * 512:(half + 1) * 512],
                    xkT[:, r * 1024 + c * 128:r * 1024 + (c + 1) * 128],
                    wq_bf[:, c * 1024 + half * 512:c * 1024 + half * 512 + 512],
                    start=(c == 0), stop=(c == HC - 1)), w)
                w = []
        ev_rope_prev = rope_emit(ps01[:], qtab, j * 128,
                                 [(sT, ev_mm), (sV, e_qtab_t[j])],
                                 kro[:, r * 1024:(r + 1) * 1024])
        ev_qt_ev = transpose8(
            kro[:, r * 1024:(r + 1) * 1024], True, [(sV, ev_rope_prev)],
            (qt3[:, 0:4, j * 128:(j + 1) * 128], qt3[:, 4:8, j * 128:(j + 1) * 128]),
            (ev_qt_ev[0], ev_qt_ev[1]))
    e_qt_done = max(ev_qt_ev)
    e_q_rope_last = sV.n
    if debug:
        dmaG(dbg["dbg_qt"], qt[:], waits=[(sV, e_qt_done)])
        dmaG(dbg["dbg_qtab"], qtab[:], waits=[(sV, e_qt_done)])
        dmaG(dbg["dbg_cst"], cst[:], waits=[(sV, e_qt_done)])
        dmaG(dbg["dbg_invf"], invf[:], waits=[(sV, e_qt_done)])

    if phases < 3:
        emit("sync", lambda h: h.nop(), waits=[(sS, sS.n)])
        emit("gpsimd", lambda h: h.nop(), waits=[(sG, sG.n)])
        with nc.Block() as block:
            for name in ("sync", "gpsimd", "tensor", "vector", "scalar"):
                thunks = stream[name]
                def runner(h, thunks=thunks):
                    for t in thunks:
                        t(h)
                getattr(block, name)(runner)
        es.close()
        return nc
    import os
    e_wkv = (dmaG(wkv[:].rearrange("p (c n) -> p c n", c=HC),
                  d_w1t[:, E:3 * E].rearrange("(c p) n -> p c n", p=128))
             if os.environ.get("K_NOWKV") != "1" else 0)

    pnum3 = pnum[:].rearrange("p (s n) -> p s n", n=256)
    pden3 = pden[:].rearrange("p (s n) -> p s n", n=4)
    pn_first = {}
    ev_slot = {}
    v65_4 = v65[:].rearrange("p (t h o) -> p t h o", t=8, o=65)
    ktq3 = ktq[:].rearrange("p (hp q) -> p hp q", hp=HP)

    e_attn_done = [0] * NTO
    ev_av_evac = [0]
    ev_dcol_free = [0, 0]
    e_kv_evac = [e_q_rope_last, e_q_rope_last]
    ev_kT_ev = [0, 0]
    ev_xsT = [[0, 0], [0, 0]]
    ev_pair_rd = [0, 0, 0]
    ev_ptb_war = [0, 0, 0]
    ev_ptb_msk = [0, 0, 0]
    last_att_T = 0
    attn3 = attn[:].rearrange("p (t o) -> p t o", o=1024)
    st_pairs = (psS, ps45, ps23)
    gpair = [0]
    pend_pairs = []

    for qi, qr in enumerate([3, 2, 1, 0][:quarters]):
        ev_v65_ones_v = V1(lambda h: h.memset(v65_4[:, :, :, 64:65], 1.0),
                           waits=[(sT, last_att_T)])
        ev_v65_ones = G1(lambda h: h.nop(), waits=[(sV, ev_v65_ones_v)])
        if 't' in kvp:
            for sl in range(8):
                gt = qr * 8 + sl
                (G1 if sl == 7 else G0)(lambda h, gt=gt, sl=sl: h.iota(
                    tposi[:, sl:sl + 1], pattern=[[1, 1]], base=gt * 128,
                    channel_multiplier=1), waits=[(sV, gen_ev)] if sl == 0 else ())
            e6 = sGc.n
            evp = V1(lambda h: h.tensor_copy(poscol[:, 0:8], tposi[:, 0:8]),
                     waits=[(sGc, e6), (sV, gen_ev)])
            for bb in range(2):
                tiles = list(range(bb * 4, (bb + 1) * 4))
                pcols = [poscol[:, i:i + 1] for i in tiles]
                gen_ev, gen_sc = gen_tables_batch(
                    ktab, tiles, pcols, [(sV, evp)],
                    [(sV, gen_ev), (sC, gen_sc)])
            ev_ktab_done = gen_ev

        for sl in range(8):
            gt = qr * 8 + sl
            r = sl % 2
            ev_x = dmaG(xk_bf[:, r * 1024:(r + 1) * 1024],
                        d_xseq[gt * 128:(gt + 1) * 128, :],
                        waits=[(sV, ev_xsT[r][1]), (sV, ev_xqT[r][1])])
            ev_xsT[r] = transpose8(
                xk_bf[:, r * 1024:(r + 1) * 1024], True,
                [(sG, ev_x)] + ([(sC, ev_pair_rd[1])] if sl == 0 else []),
                (xkT[:, r * 1024:r * 1024 + 512],
                 xkT[:, r * 1024 + 512:r * 1024 + 1024]),
                (ev_kT_ev[0], ev_kT_ev[1]))
            ev_mm = 0
            w = [(sV, ev_xsT[r][1]), (sG, e_wkv), (sV, e_kv_evac[0]),
                 (sV, e_kv_evac[1]), (sV, ev_av_evac[0]), (sC, ev_pair_rd[2])]
            if 'm' not in kvp:
                continue
            for c in range(HC):
                for half in range(2):
                    T1(lambda h, r=r, c=c, half=half: h.matmul(
                        ps01[:, half * 512:(half + 1) * 512],
                        xkT[:, r * 1024 + c * 128:r * 1024 + (c + 1) * 128],
                        wkv[:, c * 2048 + half * 512:c * 2048 + half * 512 + 512],
                        start=(c == 0), stop=(c == HC - 1)), w)
                    w = []
                    ev_mm = T1(lambda h, r=r, c=c, half=half: h.matmul(
                        ps23[:, half * 512:(half + 1) * 512],
                        xkT[:, r * 1024 + c * 128:r * 1024 + (c + 1) * 128],
                        wkv[:, c * 2048 + 1024 + half * 512:
                            c * 2048 + 1024 + half * 512 + 512],
                        start=(c == 0), stop=(c == HC - 1)))
            if 'v' in kvp:
              e_kv_evac[0] = V1(lambda h, sl=sl: h.tensor_copy(
                v65_4[:, sl, :, 0:64],
                ps23[:].rearrange("p (h o) -> p h o", o=64)),
                waits=[(sT, ev_mm), (sGc, ev_v65_ones)])
            ev_ro = rope_emit(ps01[:], ktab, sl * 128,
                              [(sT, ev_mm), (sV, ev_ktab_done)],
                              kro[:, r * 1024:(r + 1) * 1024])
            e_kv_evac[1] = ev_ro
            ev_kT_ev = transpose8(
                kro[:, r * 1024:(r + 1) * 1024], True, [(sV, ev_ro)],
                (ktq3[:, 0:4, sl * 128:(sl + 1) * 128],
                 ktq3[:, 4:8, sl * 128:(sl + 1) * 128]),
                (ev_kT_ev[0], ev_kT_ev[1]))
        e_ktq_done = max(ev_kT_ev)
        e_v65_done = e_kv_evac[0]
        if debug and qr == 0:
            dmaG(dbg["dbg_kt"], ktq[:], waits=[(sV, e_ktq_done)])
            dmaG(dbg["dbg_v65"], v65[:], waits=[(sV, e_v65_done)])

        for J in range(qr, 4):
            for hp in range(HP):
                diag = (J == qr)
                work = [(i, (i // 2 if diag else 0)) for i in range(8)]
                n_av = sum(4 - cu for _, cu in work)
                avp = ps01
                k_av = [0, 0]
                ev_S = [0] * 8
                ev_msk = [0] * 8
                ev_avmm = 0
                dcover = {}

                def emit_av(i, cu):
                    nonlocal ev_avmm
                    th = psv_of[i][0]
                    for sub in (0, 1):
                        hh = hp * 2 + sub
                        for c in range(cu, 4):
                            w2 = ([(sC, dcover[i])]
                                  if (sub == 0 and c == cu) else [])
                            if diag and sub == 0 and c == cu:
                                w2.append((sV, ev_msk[i]))
                            if k_av[sub] == 0:
                                w2.append((sV, ev_av_evac[0]))
                            st = (k_av[sub] == 0)
                            sp = (k_av[sub] == n_av - 1)
                            ev_avmm = T1(lambda h, th=th, sub=sub, c=c,
                                         hh=hh, i=i, st=st, sp=sp, avp=avp:
                                         h.matmul(
                                avp[:, sub * 512 + c * 65:sub * 512 + c * 65 + 65],
                                ptb[:, th * 1024 + sub * 512 + c * 128:
                                    th * 1024 + sub * 512 + (c + 1) * 128],
                                v65[:, i * 1040 + hh * 65:i * 1040 + hh * 65 + 65],
                                start=st, stop=sp, skip_group_check=True), w2)
                            k_av[sub] += 1
                    ev_ptb_war[th] = ev_avmm

                def emit_mask(i, cu):
                    th = psv_of[i][0]
                    ptpair = ptb[:, th * 1024:th * 1024 + 1024].rearrange(
                        "p (s q) -> p s q", s=2)
                    dst = ptpair[:, :, cu * 128:cu * 128 + 128]
                    ev_msk[i] = V1(lambda h, dst=dst, i=i: h.tensor_tensor(
                        dst, dst,
                        masks[:, i * 128:(i + 1) * 128].unsqueeze(1)
                        .broadcast_to([128, 2, 128]), ALU.mult),
                        waits=[(sC, dcover[i])])
                    ev_ptb_msk[th] = ev_msk[i]

                psv_of = {}
                for idx, (i, cu) in enumerate(work):
                    pair = gpair[0]
                    gpair[0] = (pair + 1) % 3
                    psv_pair = st_pairs[pair]
                    psv_of[i] = (pair, psv_pair)
                    for sub in (0, 1):
                        w = [(sC, ev_pair_rd[pair])] if sub == 0 else []
                        if idx == 0 and sub == 0:
                            w += [(sV, e_ktq_done), (sV, e_v65_done),
                                  (sV, e_qt_done), (sG, e_masks)]
                        ev_S[i] = T1(lambda h, i=i, cu=cu, sub=sub, J=J, hp=hp,
                                     psv_pair=psv_pair: h.matmul(
                            psv_pair[:, sub * 512 + cu * 128:(sub + 1) * 512],
                            ktq[64 * sub:64 * (sub + 1),
                                hp * 1024 + i * 128:hp * 1024 + (i + 1) * 128],
                            qt[64 * sub:64 * (sub + 1),
                               hp * 2048 + J * 512 + cu * 128:
                               hp * 2048 + J * 512 + 512],
                            start=True, stop=True, tile_position=(64 * sub, 0)), w)
                    if idx >= 2:
                        ip, cup = work[idx - 2]
                        if diag:
                            emit_mask(ip, cup)
                        emit_av(ip, cup)
                    ew = [(sT, ev_S[i]), (sT, ev_ptb_war[pair])]
                    if ev_ptb_msk[pair]:
                        ew.append((sV, ev_ptb_msk[pair]))
                    fn_exp = (lambda h, pair=pair, psv_pair=psv_pair:
                              h.activation(
                        ptb[:, pair * 1024:(pair + 1) * 1024], psv_pair[:],
                        AF.Exp, bias=cst[:, C_NEGC:C_NEGC + 1], scale=0.125))
                    pend_pairs.append(pair)
                    if i % 2 == 1 or idx == len(work) - 1:
                        d = S1(fn_exp, ew)
                        for p in pend_pairs:
                            ev_pair_rd[p] = d
                        pend_pairs.clear()
                        for ii in range(i + 1):
                            dcover.setdefault(ii, d)
                    else:
                        S0(fn_exp, ew)
                for it in (-2, -1):
                    it7, cu7 = work[it]
                    if diag:
                        emit_mask(it7, cu7)
                    emit_av(it7, cu7)
                for sub in (0, 1):
                    hh = hp * 2 + sub
                    av4 = avp[:, sub * 512:sub * 512 + 260].rearrange(
                        "p (c o) -> p c o", o=65)
                    avden = av4[:, :, 64:65].rearrange("p c o -> p (c o)")
                    avnum = av4[:, :, 0:64]
                    slot = (J - 1) * 16 + hh
                    ev_sl = ev_slot.get(slot, 0)
                    if qr > 0:
                        first = pn_first.setdefault(slot, True)
                        pn_first[slot] = False
                        pslot = pnum3[:, slot, :].rearrange("p (c o) -> p c o", o=64)
                        if first:
                            ev1 = V1(lambda h, pslot=pslot, avnum=avnum:
                                     h.tensor_copy(pslot, avnum),
                                     waits=[(sT, ev_avmm), (sV, ev_sl)])
                            ev2 = V1(lambda h, slot=slot, avden=avden:
                                     h.tensor_copy(pden3[:, slot, :], avden))
                        else:
                            ev1 = V1(lambda h, pslot=pslot, avnum=avnum:
                                     h.tensor_tensor(pslot, avnum, pslot, ALU.add),
                                     waits=[(sT, ev_avmm), (sV, ev_sl)])
                            ev2 = V1(lambda h, slot=slot, avden=avden:
                                     h.tensor_tensor(pden3[:, slot, :], avden,
                                                     pden3[:, slot, :], ALU.add))
                        ev_av_evac[0] = max(ev1, ev2)
                        ev_slot[slot] = max(ev1, ev2)
                    else:
                        dbase = sub * 16
                        if J > 0:
                            evd = V1(lambda h, slot=slot, avden=avden, dbase=dbase:
                                     h.tensor_tensor(dcol[:, dbase:dbase + 4],
                                                     avden, pden3[:, slot, :],
                                                     ALU.add),
                                     waits=[(sT, ev_avmm),
                                            (sV, ev_dcol_free[sub])])
                            evn = V1(lambda h, slot=slot, avnum=avnum, sub=sub:
                                     h.tensor_tensor(
                                fnum[:, sub * 256:(sub + 1) * 256].rearrange(
                                    "p (c o) -> p c o", o=64), avnum,
                                pnum3[:, slot, :].rearrange("p (c o) -> p c o",
                                                            o=64), ALU.add))
                        else:
                            evd = V1(lambda h, avden=avden, dbase=dbase:
                                     h.tensor_copy(dcol[:, dbase:dbase + 4], avden),
                                     waits=[(sT, ev_avmm),
                                            (sV, ev_dcol_free[sub])])
                            evn = V1(lambda h, avnum=avnum, sub=sub: h.tensor_copy(
                                fnum[:, sub * 256:(sub + 1) * 256].rearrange(
                                    "p (c o) -> p c o", o=64), avnum))
                        evr = V1(lambda h, dbase=dbase: h.reciprocal(
                            dcol[:, dbase + 8:dbase + 12],
                            dcol[:, dbase:dbase + 4]), waits=[(sV, evd)])
                        ev_av_evac[0] = max(evn, evd)
                        evdiv = V1(lambda h, J=J, hh=hh, sub=sub, dbase=dbase:
                                   h.tensor_tensor(
                            attn3[:, J * 4:J * 4 + 4, hh * 64:(hh + 1) * 64],
                            fnum[:, sub * 256:(sub + 1) * 256].rearrange(
                                "p (c o) -> p c o", o=64),
                            dcol[:, dbase + 8:dbase + 12].unsqueeze(2)
                            .broadcast_to([128, 4, 64]), ALU.mult),
                            waits=[(sV, evr), (sV, evn)])
                        for c in range(4):
                            e_attn_done[J * 4 + c] = evdiv
                        ev_dcol_free[sub] = evdiv
        last_att_T = sT.n
    ev_pnum_done = sV.n
    e_att_exp_done = ev_pair_rd[0]
    e_attn_all = max(e_attn_done)
    if debug:
        dmaG(dbg["dbg_attn"], attn[:], waits=[(sV, e_attn_all)])

    if phases < 4:
        emit("sync", lambda h: h.nop(), waits=[(sS, sS.n)])
        emit("gpsimd", lambda h: h.nop(), waits=[(sG, sG.n)])
        emit("scalar", lambda h: h.nop(), waits=[(sC, sC.n)])
        emit("vector", lambda h: h.nop(), waits=[(sV, sV.n)])
        with nc.Block() as block:
            for name in ("sync", "gpsimd", "tensor", "vector", "scalar"):
                thunks = stream[name]
                def runner(h, thunks=thunks):
                    for t in thunks:
                        t(h)
                getattr(block, name)(runner)
        es.close()
        return nc
    e_w2t = dmaG(w2t_bf.rearrange("p (c n) -> p c n", c=HC),
                 d_w2t[:].rearrange("(c p) n -> p c n", p=128),
                 waits=[(sV, ev_pnum_done), (sV, e_attn_all)])
    e_wgt = dmaY(wgt_f[:].rearrange("p (c n) -> p c n", c=HC),
                 d_wgt[:].rearrange("(c p) n -> p c n", p=128))

    x2t3 = x2t.rearrange("p (c t) -> p c t", c=HC)
    gu_bufs = (attn[:, 0:4096], attn[:, 4096:8192])

    ev_ln_free = [0, 0]
    ev_yb_free = [0, 0]

    def ln_emit(r1v, ev_r1_v, dst, cbase, waits_dst=()):
        evs1 = V1(lambda h: h.reduce_sum(lncol[:, cbase:cbase + 1], r1v,
                                         mybir.AxisListType.X),
                  waits=[(sV, ev_r1_v), (sC, ev_ln_free[cbase // 8])])
        evs2 = S1(lambda h: h.activation(sq[:], r1v, AF.Square,
                                         accum_out=lncol[:, cbase + 1:cbase + 2]),
                  waits=[(sV, ev_r1_v), (sC, ev_ln_free[cbase // 8])])
        V0(lambda h: h.tensor_scalar(lncol[:, cbase + 2:cbase + 3],
                                     lncol[:, cbase:cbase + 1], I1024, None,
                                     ALU.mult), waits=[(sV, evs1), (sC, evs2)])
        ev_m = V1(lambda h: h.tensor_scalar(lncol[:, cbase + 3:cbase + 4],
                                            lncol[:, cbase + 1:cbase + 2], I1024,
                                            None, ALU.mult))
        V0(lambda h: h.tensor_mul(lncol[:, cbase + 4:cbase + 5],
                                  lncol[:, cbase + 2:cbase + 3],
                                  lncol[:, cbase + 2:cbase + 3]),
           waits=[(sV, ev_m)])
        ev_mq = V1(lambda h: h.tensor_scalar(lncol[:, cbase + 5:cbase + 6],
                                             lncol[:, cbase + 2:cbase + 3], -1.0,
                                             None, ALU.mult))
        ev_var = V1(lambda h: h.tensor_sub(lncol[:, cbase + 3:cbase + 4],
                                           lncol[:, cbase + 3:cbase + 4],
                                           lncol[:, cbase + 4:cbase + 5]),
                    waits=[(sV, ev_mq)])
        ev_std = S1(lambda h: h.activation(lncol[:, cbase + 4:cbase + 5],
                                           lncol[:, cbase + 3:cbase + 4], AF.Sqrt,
                                           bias=cst[:, C_EPS:C_EPS + 1]),
                    waits=[(sV, ev_var)])
        ev_rstd = V1(lambda h: h.reciprocal(lncol[:, cbase + 6:cbase + 7],
                                            lncol[:, cbase + 4:cbase + 5]),
                     waits=[(sC, ev_std)])
        ev_p1 = S1(lambda h: h.activation(sq[:], r1v, AF.Identity,
                                          bias=lncol[:, cbase + 5:cbase + 6]),
                   waits=[(sV, ev_rstd)])
        ev = S1(lambda h: h.activation(dst, sq[:], AF.Identity,
                                       scale=lncol[:, cbase + 6:cbase + 7]),
                waits=[(sC, ev_p1)] + list(waits_dst))
        ev_ln_free[cbase // 8] = ev
        return ev

    def gate_batch(half, ev_sig_last, prev_gate_ev):
        gv = [gsc[:, t * 16:(t + 1) * 16] for t in range(8)]
        wv = [gw[:, t * 128:(t + 1) * 128] for t in range(8)]
        P_, GS, GS2, CNT, ALW, MSK, RNK, SEL, WR, DEN = \
            0, 24, 28, 36, 40, 44, 60, 76, 92, 108
        pairs = [(0, 1), (0, 2), (0, 3), (1, 2), (1, 3), (2, 3)]

        def layer(fn, waits=()):
            for t in range(8):
                (V1 if t == 7 else V0)(fn(t), waits=list(waits) if t == 0 else ())
            return sV.n

        def l1(t):
            def f(h, t=t):
                for k, (a, b) in enumerate(pairs):
                    h.tensor_tensor(wv[t][:, P_ + k * 4:P_ + (k + 1) * 4],
                                    gv[t][:, a:16:4], gv[t][:, b:16:4], ALU.add)
            return f
        ev = layer(l1, [(sC, ev_sig_last), (sV, prev_gate_ev)])
        ev = layer(lambda t: (lambda h, t=t: h.tensor_copy(
            wv[t][:, GS:GS + 4], wv[t][:, P_:P_ + 4])), [(sV, ev)])
        for k in range(1, 6):
            ev = layer(lambda t, k=k: (lambda h, t=t, k=k: h.tensor_max(
                wv[t][:, GS:GS + 4], wv[t][:, GS:GS + 4],
                wv[t][:, P_ + k * 4:P_ + (k + 1) * 4])), [(sV, ev)])
        ev = layer(lambda t: (lambda h, t=t: h.tensor_copy(
            wv[t][:, GS2:GS2 + 4], wv[t][:, GS:GS + 4])), [(sV, ev)])
        ev = layer(lambda t: (lambda h, t=t: h.tensor_copy(
            wv[t][:, GS2 + 4:GS2 + 8], wv[t][:, GS:GS + 4])), [(sV, ev)])
        ev = layer(lambda t: (lambda h, t=t: h.tensor_tensor(
            wv[t][:, CNT:CNT + 4], wv[t][:, GS2 + 1:GS2 + 5],
            wv[t][:, GS:GS + 4], ALU.is_gt)), [(sV, ev)])
        for k in (2, 3):
            ev = layer(lambda t, k=k: (lambda h, t=t, k=k: h.tensor_tensor(
                wv[t][:, ALW:ALW + 4], wv[t][:, GS2 + k:GS2 + k + 4],
                wv[t][:, GS:GS + 4], ALU.is_gt)), [(sV, ev)])
            ev = layer(lambda t: (lambda h, t=t: h.tensor_add(
                wv[t][:, CNT:CNT + 4], wv[t][:, CNT:CNT + 4],
                wv[t][:, ALW:ALW + 4])), [(sV, ev)])
        ev = layer(lambda t: (lambda h, t=t: h.tensor_scalar(
            wv[t][:, ALW:ALW + 4], wv[t][:, CNT:CNT + 4], 1.5, None, ALU.is_le)),
            [(sV, ev)])
        ev = layer(lambda t: (lambda h, t=t: h.scalar_tensor_tensor(
            wv[t][:, MSK:MSK + 16].rearrange("p (a b) -> p a b", a=4),
            wv[t][:, ALW:ALW + 4].unsqueeze(2).broadcast_to([128, 4, 4]),
            4.0, gv[t][:].rearrange("p (a b) -> p a b", a=4),
            ALU.mult, ALU.add)), [(sV, ev)])
        ev = layer(lambda t: (lambda h, t=t: h.tensor_scalar(
            wv[t][:, MSK:MSK + 16], wv[t][:, MSK:MSK + 16], -4.0, None, ALU.add)),
            [(sV, ev)])
        for t in range(8):
            ev_c = V1(lambda h, t=t: h.tensor_tensor(
                gcmp[:].rearrange("p (a b) -> p a b", a=16),
                wv[t][:, MSK:MSK + 16].unsqueeze(1).broadcast_to([128, 16, 16]),
                wv[t][:, MSK:MSK + 16].unsqueeze(2).broadcast_to([128, 16, 16]),
                ALU.is_gt), waits=[(sV, ev)])
            ev = V1(lambda h, t=t: h.reduce_sum(
                wv[t][:, RNK:RNK + 16],
                gcmp[:].rearrange("p (a b) -> p a b", a=16),
                mybir.AxisListType.X), waits=[(sV, ev_c)])
        ev = layer(lambda t: (lambda h, t=t: h.tensor_scalar(
            wv[t][:, SEL:SEL + 16], wv[t][:, RNK:RNK + 16], 3.5, None, ALU.is_le)),
            [(sV, ev)])
        ev = layer(lambda t: (lambda h, t=t: h.tensor_mul(
            wv[t][:, WR:WR + 16], wv[t][:, SEL:SEL + 16], gv[t][:])), [(sV, ev)])
        ev = layer(lambda t: (lambda h, t=t: h.reduce_sum(
            wv[t][:, DEN:DEN + 1], wv[t][:, WR:WR + 16],
            mybir.AxisListType.X)), [(sV, ev)])
        ev = layer(lambda t: (lambda h, t=t: h.tensor_scalar(
            wv[t][:, DEN + 1:DEN + 2], wv[t][:, DEN:DEN + 1], 1e-6, None,
            ALU.add)), [(sV, ev)])
        ev_rc = layer(lambda t: (lambda h, t=t: h.reciprocal(
            wv[t][:, DEN + 2:DEN + 3], wv[t][:, DEN + 1:DEN + 2])), [(sV, ev)])
        ev = layer(lambda t: (lambda h, t=t, half=half: h.tensor_copy(
            comb[:, (half * 8 + t) * NE:(half * 8 + t) * NE + 16],
            wv[t][:, WR:WR + 16])), [(sV, ev_rc)])
        ev = layer(lambda t: (lambda h, t=t, half=half: h.tensor_scalar(
            comb[:, (half * 8 + t) * NE:(half * 8 + t) * NE + 16],
            comb[:, (half * 8 + t) * NE:(half * 8 + t) * NE + 16],
            wv[t][:, DEN + 2:DEN + 3], None, ALU.mult)), [(sV, ev)])
        return ev

    ev_xow = [0, 0]
    ev_xow = [0, 0]
    ev_r1_used = [0, 0]
    ev_r1_scal = [0, 0]
    ev_gate_all = 0
    e_ffn_prev_half_v = 0
    e_ffn_prev_half_T = last_att_T
    ev_wuse = [last_att_T, last_att_T]
    ev_gu_used = [0, 0]

    for half in range(2):
        ev_atT = [[0, 0], [0, 0]]
        ev_op_evac = [0, 0]
        e_x2t_done = [0, 0]
        e_x1_done = [0] * 8
        e_sig = [0] * 8
        e_sig_last = 0
        for tl in range(8):
            tt = half * 8 + tl
            r = tl % 2
            ev_xow[r] = dmaY(xow[:, r * 1024:(r + 1) * 1024],
                             d_xown[tt * 128:(tt + 1) * 128, :],
                             waits=[(sV, ev_r1_used[r])] +
                                   ([(sT, last_att_T)] if (half == 0 and tl < 2)
                                    else []))
            ev_atT[r] = transpose8(
                attn[:, tt * 1024:(tt + 1) * 1024], True,
                [(sV, e_attn_done[tt]), (sV, ev_pnum_done), (sV, ev_atT[r][1])] +
                ([(sC, e_att_exp_done)] if (half == 0 and tl == 0) else []),
                (atT[:, r * 1024:r * 1024 + 512],
                 atT[:, r * 1024 + 512:r * 1024 + 1024]),
                (e_x2t_done[0], e_x2t_done[1]))
            opp = ps01 if r == 0 else ps23
            ev_mm = 0
            w = [(sV, ev_atT[r][1]), (sG, e_w2t), (sV, ev_op_evac[r])]
            for c in range(HC):
                for hf in range(2):
                    ev_mm = T1(lambda h, r=r, c=c, hf=hf, opp=opp: h.matmul(
                        opp[:, hf * 512:(hf + 1) * 512],
                        atT[:, r * 1024 + c * 128:r * 1024 + (c + 1) * 128],
                        w2t_bf[:, c * 1024 + hf * 512:c * 1024 + hf * 512 + 512],
                        start=(c == 0), stop=(c == HC - 1)), w)
                    w = []
            ev_r1 = V1(lambda h, r=r, opp=opp: h.tensor_add(
                r1b[:, r * 1024:(r + 1) * 1024], opp[:],
                xow[:, r * 1024:(r + 1) * 1024]),
                waits=[(sT, ev_mm), (sS, ev_xow[r]), (sC, ev_r1_scal[r]),
                       (sOut, ev_yb_free[r])])
            ev_op_evac[r] = ev_r1
            wdst = []
            if tl == 0:
                wdst = [(sT, e_ffn_prev_half_T), (sV, e_ffn_prev_half_v)]
            ev_x1 = ln_emit(r1b[:, r * 1024:(r + 1) * 1024], ev_r1,
                            x1res[:, tl * 1024:(tl + 1) * 1024], r * 8,
                            waits_dst=wdst)
            e_x1_done[tl] = ev_x1
            ev_r1_scal[r] = ev_x1
            ev_r1_used[r] = ev_r1
            ev_tx = transpose8(
                x1res[:, tl * 1024:(tl + 1) * 1024], False,
                [(sC, ev_x1)],
                (x2t3[:, 0:4, tl * 128:(tl + 1) * 128],
                 x2t3[:, 4:8, tl * 128:(tl + 1) * 128]),
                (e_x2t_done[0], e_x2t_done[1]),
                extra_evac=(x1t_f[:, 0:512], x1t_f[:, 512:1024]))
            e_x2t_done = ev_tx
            ev_gmm = 0
            w = [(sV, ev_tx[1]), (sS, e_wgt), (sC, e_sig_last)]
            for c in range(HC):
                ev_gmm = T1(lambda h, c=c: h.matmul(
                    ps6[:, 0:16], x1t_f[:, c * 128:(c + 1) * 128],
                    wgt_f[:, c * 16:(c + 1) * 16], start=(c == 0),
                    stop=(c == HC - 1)), w)
                w = []
            e_sig[tl] = S1(lambda h, tl=tl: h.activation(
                gsc[:, tl * 16:(tl + 1) * 16], ps6[:, 0:16], AF.Sigmoid),
                waits=[(sT, ev_gmm), (sV, ev_gate_all)])
            e_sig_last = e_sig[tl]
        ev_gate_all = gate_batch(half, e_sig_last, ev_gate_all)
        e_comb_all = ev_gate_all
        if debug and half == 0:
            dmaG(dbg["dbg_x1"], x1res[:, 0:8192], waits=[(sC, max(e_x1_done))])
            dmaG(dbg["dbg_x2t"], x2t[:, 0:8192], waits=[(sV, max(e_x2t_done))])

        e_wld = [0, 0]
        ev_dn_evac = [ev_op_evac[0], ev_op_evac[1]]
        ev_x1_acc = list(e_x1_done)
        ev_guw = 0
        for e in range(min(2, NE)):
            pe = e % 2
            dmaG(w13[:, pe * 4096:(pe + 1) * 4096]
                 .rearrange("p (c n) -> p c n", c=HC),
                 d_w1a[e].rearrange("(c p) n -> p c n", p=128),
                 waits=[(sT, ev_wuse[pe])])
            dmaG(w13[:, 8192 + pe * 4096:8192 + (pe + 1) * 4096]
                 .rearrange("p (c n) -> p c n", c=HC),
                 d_w3a[e].rearrange("(c p) n -> p c n", p=128))
            e_wld[pe] = dmaG(
                w2bf[:, pe * 4096:(pe + 1) * 4096]
                .rearrange("p (c n) -> p c n", c=4),
                d_w2a[e].rearrange("(c p) n -> p c n", p=128))
        for e in range(NE):
            pe = e % 2
            guv = gu_bufs[pe]
            ev_sil = 0
            for fc in range(4):
                for tc in range(2):
                    w = [(sG, e_wld[pe]), (sC, ev_sil), (sV, ev_guw)]
                    if fc == 0 and tc == 0 and e < 2:
                        w += [(sV, max(e_x2t_done)), (sC, e_sig_last)]
                    ev_g = 0
                    for c in range(HC):
                        ev_g = T1(lambda h, pe=pe, fc=fc, tc=tc, c=c: h.matmul(
                            ps6, w13[:, pe * 4096 + c * 512 + fc * 128:
                                        pe * 4096 + c * 512 + (fc + 1) * 128],
                            x2t[:, c * 1024 + tc * 512:c * 1024 + (tc + 1) * 512],
                            start=(c == 0), stop=(c == HC - 1)), w)
                        w = []
                    ev_u = 0
                    for c in range(HC):
                        ev_u = T1(lambda h, pe=pe, fc=fc, tc=tc, c=c: h.matmul(
                            ps7, w13[:, 8192 + pe * 4096 + c * 512 + fc * 128:
                                        8192 + pe * 4096 + c * 512 + (fc + 1) * 128],
                            x2t[:, c * 1024 + tc * 512:c * 1024 + (tc + 1) * 512],
                            start=(c == 0), stop=(c == HC - 1)))
                    ev_sil = S1(lambda h, tc=tc: h.activation(
                        silu_t[:, tc * 512:(tc + 1) * 512], ps6, AF.Silu),
                        waits=[(sT, ev_g), (sV, ev_guw)])
                    ev_guw = V1(lambda h, guv=guv, fc=fc, tc=tc: h.tensor_tensor(
                        guv[:, fc * 1024 + tc * 512:fc * 1024 + (tc + 1) * 512],
                        silu_t[:, tc * 512:(tc + 1) * 512], ps7, ALU.mult),
                        waits=[(sC, ev_sil), (sT, ev_u)])
            for tl in range(8):
                tt = half * 8 + tl
                dnp = ps01 if tl % 2 == 0 else ps23
                ev_mm = 0
                w = [(sV, ev_guw), (sV, ev_dn_evac[tl % 2]), (sG, e_wld[pe])]
                for fc in range(4):
                    for hf in range(2):
                        ev_mm = T1(lambda h, pe=pe, fc=fc, hf=hf, tl=tl, dnp=dnp:
                                   h.matmul(
                            dnp[:, hf * 512:(hf + 1) * 512],
                            gu_bufs[pe][:, fc * 1024 + tl * 128:
                                        fc * 1024 + (tl + 1) * 128],
                            w2bf[:, pe * 4096 + fc * 1024 + hf * 512:
                                 pe * 4096 + fc * 1024 + hf * 512 + 512],
                            start=(fc == 0), stop=(fc == 3)), w)
                        w = []
                ev_acc = V1(lambda h, tl=tl, tt=tt, e=e, dnp=dnp:
                            h.scalar_tensor_tensor(
                    x1res[:, tl * 1024:(tl + 1) * 1024], dnp[:],
                    comb[:, tt * NE + e:tt * NE + e + 1],
                    x1res[:, tl * 1024:(tl + 1) * 1024], ALU.mult, ALU.add),
                    waits=[(sT, ev_mm), (sC, e_x1_done[tl]),
                           (sV, e_comb_all),
                           (sV, ev_x1_acc[tl])] +
                          ([(sGc, e_comb1)] if e >= 16 else []))
                ev_dn_evac[tl % 2] = ev_acc
                ev_x1_acc[tl] = ev_acc
            ev_wuse[pe] = sT.n
            ev_gu_used[pe] = sT.n
            if e + 2 < NE:
                dmaG(w13[:, pe * 4096:(pe + 1) * 4096]
                     .rearrange("p (c n) -> p c n", c=HC),
                     d_w1a[e + 2].rearrange("(c p) n -> p c n", p=128),
                     waits=[(sT, ev_wuse[pe])])
                dmaG(w13[:, 8192 + pe * 4096:8192 + (pe + 1) * 4096]
                     .rearrange("p (c n) -> p c n", c=HC),
                     d_w3a[e + 2].rearrange("(c p) n -> p c n", p=128))
                e_wld[pe] = dmaG(
                    w2bf[:, pe * 4096:(pe + 1) * 4096]
                    .rearrange("p (c n) -> p c n", c=4),
                    d_w2a[e + 2].rearrange("(c p) n -> p c n", p=128))
        for tl in range(8):
            tt = half * 8 + tl
            r = tl % 2
            ev_y = ln_emit(x1res[:, tl * 1024:(tl + 1) * 1024], ev_x1_acc[tl],
                           yb[:, r * 1024:(r + 1) * 1024], r * 8,
                           waits_dst=[(sOut, ev_yb_free[r])])
            ev_yb_free[r] = dmaY(d_y[tt * 128:(tt + 1) * 128, :],
                                 yb[:, r * 1024:(r + 1) * 1024],
                                 waits=[(sC, ev_y)], store=True)
        if debug and half == 0:
            dmaG(dbg["dbg_comb"], comb[:, 0:8 * NE], waits=[(sV, e_comb_all)])
        e_ffn_prev_half_v = sV.n
        e_ffn_prev_half_T = sT.n

    emit("sync", lambda h: h.nop(), waits=[(sOut, sOut.n)])

    with nc.Block() as block:
        for name in ("sync", "gpsimd", "tensor", "vector", "scalar"):
            thunks = stream[name]

            def runner(h, thunks=thunks):
                for t in thunks:
                    t(h)
            getattr(block, name)(runner)
    es.close()
    return nc


_CACHE = {}


def _prep_host(inputs):
    f32 = np.float32
    x = np.ascontiguousarray(inputs["L0_attn_x"], dtype=f32)
    w1t = np.ascontiguousarray(np.asarray(inputs["L0_attn_W1"], f32).T)
    w2t = np.ascontiguousarray(np.asarray(inputs["L0_attn_W2"], f32).T)
    wgt = np.ascontiguousarray(np.asarray(inputs["L0_ffn_w_gate"], f32).T)
    w1e = np.asarray(inputs["L0_ffn_w1_e"], f32)
    w3e = np.asarray(inputs["L0_ffn_w3_e"], f32)
    w2e = np.asarray(inputs["L0_ffn_w2_e"], f32)
    w1s = np.asarray(inputs["L0_ffn_w1_s"], f32)
    w3s = np.asarray(inputs["L0_ffn_w3_s"], f32)
    w2s = np.asarray(inputs["L0_ffn_w2_s"], f32)
    w1a = np.ascontiguousarray(np.concatenate(
        [w1e.transpose(0, 2, 1), w1s.T.reshape(E, 2, FF).transpose(1, 0, 2)], 0))
    w3a = np.ascontiguousarray(np.concatenate(
        [w3e.transpose(0, 2, 1), w3s.T.reshape(E, 2, FF).transpose(1, 0, 2)], 0))
    w2a = np.ascontiguousarray(np.concatenate(
        [w2e.transpose(0, 2, 1), w2s.T.reshape(2, FF, E)], 0))
    tri = np.triu(np.ones((128, 128), f32))
    ones = np.ones((128, 128), f32)
    zeros = np.zeros((128, 128), f32)
    masks_p = [
        np.ascontiguousarray(np.stack([tri if i % 2 == 0 else zeros
                                       for i in range(8)])),
        np.ascontiguousarray(np.stack([ones if i % 2 == 0 else tri
                                       for i in range(8)])),
    ]
    in_maps, metas = [], []
    for core in range(8):
        b, p = core // 2, core % 2
        blocks = np.arange(NTO) * 2 + p
        rows = (blocks[:, None] * 128 + np.arange(128)[None, :]).reshape(-1)
        in_maps.append({
            "xseq": np.ascontiguousarray(x[b]),
            "xown": np.ascontiguousarray(x[b][rows]),
            "posq": np.ascontiguousarray(rows.astype(f32).reshape(NTO, 128).T),
            "masks": masks_p[p],
            "w1t": w1t, "w2t": w2t, "wgt": wgt,
            "w1a": w1a, "w3a": w3a, "w2a": w2a,
        })
        metas.append((b, blocks))
    return in_maps, metas


def kernel(L0_attn_x, L0_attn_W1, L0_attn_W2, L0_ffn_w_gate, L0_ffn_w1_e,
           L0_ffn_w3_e, L0_ffn_w2_e, L0_ffn_w1_s, L0_ffn_w3_s, L0_ffn_w2_s):
    inputs = dict(L0_attn_x=L0_attn_x, L0_attn_W1=L0_attn_W1,
                  L0_attn_W2=L0_attn_W2, L0_ffn_w_gate=L0_ffn_w_gate,
                  L0_ffn_w1_e=L0_ffn_w1_e, L0_ffn_w3_e=L0_ffn_w3_e,
                  L0_ffn_w2_e=L0_ffn_w2_e, L0_ffn_w1_s=L0_ffn_w1_s,
                  L0_ffn_w3_s=L0_ffn_w3_s, L0_ffn_w2_s=L0_ffn_w2_s)
    if "nc" not in _CACHE:
        _CACHE["nc"] = build_nc(debug=False)
    nc = _CACHE["nc"]
    in_maps, metas = _prep_host(inputs)
    res = run_bass_kernel_spmd(nc, in_maps, core_ids=list(range(8)))
    out = np.empty((B, T, E), dtype=np.float32)
    for core in range(8):
        b, blocks = metas[core]
        y = np.asarray(res.results[core]["y"]).reshape(NTO, 128, E)
        out.reshape(B, 32, 128, E)[b, blocks] = y
    return out


# version 14
# speedup vs baseline: 1.5642x; 1.0570x over previous
import numpy as np
import ml_dtypes
from contextlib import ExitStack

import concourse.bass as bass
import concourse.mybir as mybir
from concourse.bass_utils import run_bass_kernel_spmd

F32 = mybir.dt.float32
BF16 = mybir.dt.bfloat16
I32 = mybir.dt.int32
AF = mybir.ActivationFunctionType
ALU = mybir.AluOpType

B, T, E = 4, 4096, 1024
NH, D, HP, HC = 16, 64, 8, 8
NTO = 16
NQ = NTO * 128
NE = 18
FF = 512
CSHIFT = -20.0
I1024 = 1.0 / 1024.0


class Sem:
    def __init__(self, es, nc, name):
        self.h = es.enter_context(nc.semaphore(name))
        self.n = 0


def build_nc(debug=False, phases=5, qtiles=NTO, quarters=4, do_att=True, kvp='tmve'):
    nc = bass.Bass("TRN2", target_bir_lowering=False, detect_race_conditions=False)
    es = ExitStack()

    d_xseq = nc.dram_tensor("xseq", [T, E], F32, kind="ExternalInput").ap()
    d_xown = nc.dram_tensor("xown", [NQ, E], F32, kind="ExternalInput").ap()
    d_ropeq = nc.dram_tensor("ropeq", [NQ, 128], F32, kind="ExternalInput").ap()
    d_ropek = nc.dram_tensor("ropek", [4096, 128], F32, kind="ExternalInput").ap()
    d_masks = nc.dram_tensor("masks", [8, 128, 128], F32, kind="ExternalInput").ap()
    d_w1t = nc.dram_tensor("w1t", [E, 3 * E], F32, kind="ExternalInput").ap()
    d_w2t = nc.dram_tensor("w2t", [E, E], F32, kind="ExternalInput").ap()
    d_wgt = nc.dram_tensor("wgt", [E, 16], F32, kind="ExternalInput").ap()
    d_w1a = nc.dram_tensor("w1a", [NE, E, FF], F32, kind="ExternalInput").ap()
    d_w3a = nc.dram_tensor("w3a", [NE, E, FF], F32, kind="ExternalInput").ap()
    d_w2a = nc.dram_tensor("w2a", [NE, FF, E], F32, kind="ExternalInput").ap()
    d_y = nc.dram_tensor("y", [NQ, E], F32, kind="ExternalOutput").ap()
    dbg = {}
    if debug:
        for name, shape in [
            ("dbg_qt", [128, HP * 2048]), ("dbg_kt", [128, HP * 1024]),
            ("dbg_v65", [128, 8 * 1040]), ("dbg_attn", [128, NTO * 1024]),
            ("dbg_x1", [128, 8 * 1024]), ("dbg_comb", [128, 8 * NE]),
            ("dbg_qtab", [128, NTO * 128]), ("dbg_x2t", [128, 8 * 1024]),
            ("dbg_cst", [128, 8]), ("dbg_invf", [128, 64]),
            ("dbg_invf2", [128, 64]), ("dbg_tgi", [128, 128]),
        ]:
            dbg[name] = nc.dram_tensor(name, shape, F32, kind="ExternalOutput").ap()

    def sb(name, shape, dt):
        return es.enter_context(nc.sbuf_tensor(name, shape, dt))

    cst = sb("cst", [128, 8], F32)
    C_ONE, C_2PI, C_I2PI, C_LN1E4, C_NEGC, C_E4, C_EPS, C_PI = 0, 1, 2, 3, 4, 5, 6, 7
    id_f32 = sb("id_f32", [128, 128], F32)
    id_bf = sb("id_bf", [128, 128], BF16)
    masks = sb("masksb", [128, 8 * 128], BF16)
    qtab = sb("qtab", [128, NTO * 128], F32)
    ktab = qtab[:, 0:1024]
    comb = sb("comb", [128, NTO * NE], F32)
    pden = sb("pden", [128, 3 * 16 * 4], F32)
    xbt = sb("xbt", [128, 4 * 1024], BF16)
    xk_bf = xbt[:, 0:2048]
    xkT = xbt[:, 2048:4096]
    kro = sb("kro", [128, 2 * 1024], BF16)
    ptb = sb("ptb", [128, 6 * 512], BF16)
    rope_t1 = sb("rope_t1", [128, 1024], BF16)
    rope_t2 = sb("rope_t2", [128, 1024], BF16)
    tgi = sb("tgi", [128, 4 * 64], I32)
    idsc = tgi[:, 0:128]
    fnum = sb("fnum", [128, 2 * 256], F32)
    dcol = sb("dcol", [128, 32], F32)
    sq = sb("sq", [128, 1024], F32)
    lncol = sb("lncol", [128, 16], F32)
    gsc = sb("gsc", [128, 8 * 16], F32)
    G_PS, G_GS, G_GS2, G_CNT, G_ALW, G_MSK, G_RANK, G_SEL, G_WR, G_DEN = \
        0, 24, 28, 36, 40, 44, 60, 76, 92, 108
    gw = qtab[:, 1024:2048]
    gcmp = sq[:, 0:256]
    silu_t = kro[:, 0:1024]
    r1b = sb("r1b", [128, 2 * 1024], F32)
    yb = r1b
    wgt_f = sb("wgt_f", [128, HC * 16], F32)

    x1res = sb("x1res", [128, 8192], F32)
    ktq = sb("ktq", [128, HP * 1024], BF16)
    v65 = sb("v65", [128, 8 * 1040], BF16)
    qt = sb("qt", [128, HP * 2048], BF16)
    pnum = sb("pnum", [128, 3 * 16 * 256], BF16)
    attn = sb("attn", [128, NTO * 1024], BF16)

    wkv = attn[:, 0:16384]
    x2t = ktq[:]
    w2bf = v65[:, 0:8192]
    w13 = qt[:]
    wq_bf = pnum[:, 0:8192]
    w2t_bf = pnum[:, 0:8192]
    atT = pnum[:, 8192:10240]
    x1t_f = qtab[:, 0:1024]

    ps01 = es.enter_context(nc.psum_tensor("ps01", [128, 1024], F32))
    ps23 = es.enter_context(nc.psum_tensor("ps23", [128, 1024], F32))
    psTb = es.enter_context(nc.psum_tensor("psTb", [128, 2048], BF16))
    psS = es.enter_context(nc.psum_tensor("psS", [128, 1024], F32))
    ps6 = psS[:, 0:512]
    ps7 = psS[:, 512:1024]

    sS = Sem(es, nc, "sS")
    sG = Sem(es, nc, "sG")
    sT = Sem(es, nc, "sT")
    sV = Sem(es, nc, "sV")
    sC = Sem(es, nc, "sC")
    sGc = Sem(es, nc, "sGc")
    sOut = Sem(es, nc, "sOut")

    stream = {e: [] for e in ("sync", "gpsimd", "tensor", "vector", "scalar")}

    def emit(eng, fn, waits=(), incs=(), dma=False):
        for s, v in waits:
            assert v <= s.n, f"wait on future event: {v} > {s.n} ({eng})"
        for s, k in incs:
            s.n += k
        w = tuple((s.h, v) for s, v in waits if v > 0)
        i = tuple((s.h, k) for s, k in incs)
        drain_incs = (not dma) and eng in ("vector", "scalar", "gpsimd") and i

        def thunk(h, w=w, i=i, fn=fn, drain_incs=drain_incs):
            for sem, v in w:
                h.wait_ge(sem, v)
            inst = fn(h)
            if drain_incs:
                d = h.drain(fusable=False)
                for sem, k in i:
                    d.then_inc(sem, k)
            else:
                for sem, k in i:
                    inst.then_inc(sem, k)
        stream[eng].append(thunk)

    def T1(fn, waits=()):
        emit("tensor", fn, waits, [(sT, 1)])
        return sT.n

    def V1(fn, waits=()):
        emit("vector", fn, waits, [(sV, 1)])
        return sV.n

    def V0(fn, waits=()):
        emit("vector", fn, waits)

    def S1(fn, waits=()):
        emit("scalar", fn, waits, [(sC, 1)])
        return sC.n

    def S0(fn, waits=()):
        emit("scalar", fn, waits)

    def G1(fn, waits=()):
        emit("gpsimd", fn, waits, [(sGc, 1)])
        return sGc.n

    def G0(fn, waits=()):
        emit("gpsimd", fn, waits)

    def dmaY(out, in_, waits=(), store=False):
        emit("sync", lambda h: h.dma_start(out=out, in_=in_), waits,
             [(sOut if store else sS, 16)], dma=True)
        return (sOut if store else sS).n

    def dmaG(out, in_, waits=()):
        emit("gpsimd", lambda h: h.dma_start(out=out, in_=in_), waits, [(sG, 16)],
             dma=True)
        return sG.n

    G0(lambda h: h.memset(cst[:, C_ONE:C_ONE + 1], 1.0))
    G0(lambda h: h.memset(cst[:, C_NEGC:C_NEGC + 1], CSHIFT))
    G0(lambda h: h.memset(cst[:, C_EPS:C_EPS + 1], 1e-5))

    e5 = G1(lambda h: h.iota(idsc[:], pattern=[[1, 128]], base=0,
                             channel_multiplier=-1))
    V0(lambda h: h.tensor_copy(id_f32[:], idsc[:]), waits=[(sGc, e5)])
    e_idf = V1(lambda h: h.tensor_scalar(id_f32[:], id_f32[:], 0.0, None,
                                         ALU.is_equal))
    e_id = V1(lambda h: h.tensor_copy(id_bf[:], id_f32[:]), waits=[(sV, e_idf)])

    e_masks = dmaG(masks[:].rearrange("p (i c) -> p i c", i=8),
                   d_masks[:, :, :].transpose([1, 0, 2]))
    e_comb1 = G1(lambda h: h.memset(
        comb[:].rearrange("p (t e) -> p t e", e=NE)[:, :, 16:18], 1.0))

    if phases < 1:
        emit("sync", lambda h: h.nop(), waits=[(sS, sS.n)])
        emit("gpsimd", lambda h: h.nop(), waits=[(sG, sG.n)])
        with nc.Block() as block:
            for name in ("sync", "gpsimd", "tensor", "vector", "scalar"):
                thunks = stream[name]
                def runner(h, thunks=thunks):
                    for t in thunks:
                        t(h)
                getattr(block, name)(runner)
        es.close()
        return nc


    def rope_emit(psrc, tab, off, waits, out_bf):
        ps3 = psrc.rearrange("p (h d) -> p h d", d=64)
        t13 = rope_t1[:, 0:1024].rearrange("p (h d) -> p h d", d=64)
        V0(lambda h: h.tensor_tensor(
            t13[:, :, 0:32], ps3[:, :, 32:64],
            tab[:, off + 64:off + 96].unsqueeze(1).broadcast_to([128, 16, 32]),
            ALU.mult), waits=waits)
        V0(lambda h: h.tensor_tensor(
            t13[:, :, 32:64], ps3[:, :, 0:32],
            tab[:, off + 96:off + 128].unsqueeze(1).broadcast_to([128, 16, 32]),
            ALU.mult))
        ev_m = V1(lambda h: h.tensor_tensor(
            rope_t2[:, 0:1024].rearrange("p (h d) -> p h d", d=64), ps3,
            tab[:, off:off + 64].unsqueeze(1).broadcast_to([128, 16, 64]),
            ALU.mult))
        return V1(lambda h: h.tensor_add(out_bf, rope_t1[:, 0:1024],
                                         rope_t2[:, 0:1024]),
                  waits=[(sV, ev_m)])

    def transpose8(src, isbf, waits_first, evac_aps, ev_prev_reads,
                   extra_evac=None):
        ident = id_bf if isbf else id_f32
        psv = psTb[:] if isbf else psS[:]
        bank_off = 1024 if isbf else 512
        evts = [0, 0]
        for c in range(8):
            off = (c % 4) * 128 + (0 if c < 4 else bank_off)
            w = list(waits_first) if c == 0 else []
            if c == 0 and ev_prev_reads[0]:
                w.append((sV, ev_prev_reads[0]))
            if c == 4 and ev_prev_reads[1]:
                w.append((sV, ev_prev_reads[1]))
            ev = T1(lambda h, off=off, c=c: h.transpose(
                psv[:, off:off + 128], src[:, c * 128:(c + 1) * 128], ident[:]), w)
            if c == 3:
                evts[0] = ev
            if c == 7:
                evts[1] = ev
        out = []
        for half in range(2):
            pslice = psv[:, half * bank_off:half * bank_off + 512]
            if extra_evac is None:
                out.append(V1(lambda h, pslice=pslice, half=half: h.tensor_copy(
                    evac_aps[half], pslice), waits=[(sT, evts[half])]))
            else:
                V0(lambda h, pslice=pslice, half=half: h.tensor_copy(
                    evac_aps[half], pslice), waits=[(sT, evts[half])])
                out.append(V1(lambda h, pslice=pslice, half=half: h.tensor_copy(
                    extra_evac[half], pslice)))
        return out

    e_wq = dmaG(wq_bf.rearrange("p (c n) -> p c n", c=HC),
                d_w1t[:, 0:E].rearrange("(c p) n -> p c n", p=128))

    qt3 = qt[:].rearrange("p (hp q) -> p hp q", hp=HP)
    e_qtab = dmaY(qtab[:].rearrange("p (j c) -> p j c", c=128),
                  d_ropeq[:].rearrange("(j p) c -> p j c", p=128))

    ev_xqT = [[0, 0], [0, 0]]
    ev_qt_ev = [0, 0]
    ev_rope_prev = 0
    for j in range(qtiles):
        r = j % 2
        ev_x = dmaG(xk_bf[:, r * 1024:(r + 1) * 1024],
                    d_xown[j * 128:(j + 1) * 128, :],
                    waits=[(sV, ev_xqT[r][1])])
        ev_xqT[r] = transpose8(
            xk_bf[:, r * 1024:(r + 1) * 1024], True,
            [(sG, ev_x), (sV, e_id)],
            (xkT[:, r * 1024:r * 1024 + 512], xkT[:, r * 1024 + 512:r * 1024 + 1024]),
            (ev_qt_ev[0], ev_qt_ev[1]))
        ev_mm = 0
        w = [(sV, ev_xqT[r][1]), (sG, e_wq), (sV, ev_rope_prev)]
        for c in range(HC):
            for half in range(2):
                ev_mm = T1(lambda h, r=r, c=c, half=half: h.matmul(
                    ps01[:, half * 512:(half + 1) * 512],
                    xkT[:, r * 1024 + c * 128:r * 1024 + (c + 1) * 128],
                    wq_bf[:, c * 1024 + half * 512:c * 1024 + half * 512 + 512],
                    start=(c == 0), stop=(c == HC - 1)), w)
                w = []
        ev_rope_prev = rope_emit(ps01[:], qtab, j * 128,
                                 [(sT, ev_mm), (sS, e_qtab)],
                                 kro[:, r * 1024:(r + 1) * 1024])
        ev_qt_ev = transpose8(
            kro[:, r * 1024:(r + 1) * 1024], True, [(sV, ev_rope_prev)],
            (qt3[:, 0:4, j * 128:(j + 1) * 128], qt3[:, 4:8, j * 128:(j + 1) * 128]),
            (ev_qt_ev[0], ev_qt_ev[1]))
    e_qt_done = max(ev_qt_ev)
    e_q_rope_last = sV.n
    if debug:
        dmaG(dbg["dbg_qt"], qt[:], waits=[(sV, e_qt_done)])
        dmaG(dbg["dbg_qtab"], qtab[:], waits=[(sV, e_qt_done)])
        dmaG(dbg["dbg_cst"], cst[:], waits=[(sV, e_qt_done)])

    if phases < 3:
        emit("sync", lambda h: h.nop(), waits=[(sS, sS.n)])
        emit("gpsimd", lambda h: h.nop(), waits=[(sG, sG.n)])
        with nc.Block() as block:
            for name in ("sync", "gpsimd", "tensor", "vector", "scalar"):
                thunks = stream[name]
                def runner(h, thunks=thunks):
                    for t in thunks:
                        t(h)
                getattr(block, name)(runner)
        es.close()
        return nc
    import os
    e_wkv = (dmaG(wkv.rearrange("p (c n) -> p c n", c=HC),
                  d_w1t[:, E:3 * E].rearrange("(c p) n -> p c n", p=128))
             if os.environ.get("K_NOWKV") != "1" else 0)

    pnum3 = pnum[:].rearrange("p (s n) -> p s n", n=256)
    pden3 = pden[:].rearrange("p (s n) -> p s n", n=4)
    pn_first = {}
    ev_slot = {}
    v65_4 = v65[:].rearrange("p (t h o) -> p t h o", t=8, o=65)
    ktq3 = ktq[:].rearrange("p (hp q) -> p hp q", hp=HP)

    e_attn_done = [0] * NTO
    ev_av_evac = [0]
    ev_dcol_free = [0, 0]
    e_kv_evac = [e_q_rope_last, e_q_rope_last]
    ev_kT_ev = [0, 0]
    ev_xsT = [[0, 0], [0, 0]]
    ev_pair_rd = [0, 0]
    ev_ptb_war = [0, 0, 0]
    ev_ptb_msk = [0, 0, 0]
    last_att_T = 0
    attn3 = attn[:].rearrange("p (t o) -> p t o", o=1024)
    st_pairs = (psS, ps23)
    gpair = [0]
    gthird = [0]
    pre_x = {}
    ev_ktab_rd = e_q_rope_last

    for qi, qr in enumerate([3, 2, 1, 0][:quarters]):
        ev_v65_ones_v = V1(lambda h: h.memset(v65_4[:, :, :, 64:65], 1.0),
                           waits=[(sT, last_att_T)])
        ev_v65_ones = G1(lambda h: h.nop(), waits=[(sV, ev_v65_ones_v)])
        if 't' in kvp:
            ev_ktab_done = dmaY(
                ktab.rearrange("p (s c) -> p s c", c=128),
                d_ropek[qr * 1024:(qr + 1) * 1024]
                .rearrange("(s p) c -> p s c", p=128),
                waits=[(sV, ev_ktab_rd)])

        for sl in range(8):
            gt = qr * 8 + sl
            r = sl % 2
            if sl in pre_x:
                ev_x = pre_x.pop(sl)
            else:
                ev_x = dmaG(xk_bf[:, r * 1024:(r + 1) * 1024],
                            d_xseq[gt * 128:(gt + 1) * 128, :],
                            waits=[(sV, ev_xsT[r][1]), (sV, ev_xqT[r][1])])
            ev_xsT[r] = transpose8(
                xk_bf[:, r * 1024:(r + 1) * 1024], True,
                [(sG, ev_x)] + ([(sC, ev_pair_rd[1])] if sl == 0 else []),
                (xkT[:, r * 1024:r * 1024 + 512],
                 xkT[:, r * 1024 + 512:r * 1024 + 1024]),
                (ev_kT_ev[0], ev_kT_ev[1]))
            ev_mm = 0
            w = [(sV, ev_xsT[r][1]), (sG, e_wkv), (sV, e_kv_evac[0]),
                 (sV, e_kv_evac[1]), (sV, ev_av_evac[0]), (sC, ev_pair_rd[1])]
            if 'm' not in kvp:
                continue
            for c in range(HC):
                for half in range(2):
                    T1(lambda h, r=r, c=c, half=half: h.matmul(
                        ps01[:, half * 512:(half + 1) * 512],
                        xkT[:, r * 1024 + c * 128:r * 1024 + (c + 1) * 128],
                        wkv[:, c * 2048 + half * 512:c * 2048 + half * 512 + 512],
                        start=(c == 0), stop=(c == HC - 1)), w)
                    w = []
                    ev_mm = T1(lambda h, r=r, c=c, half=half: h.matmul(
                        ps23[:, half * 512:(half + 1) * 512],
                        xkT[:, r * 1024 + c * 128:r * 1024 + (c + 1) * 128],
                        wkv[:, c * 2048 + 1024 + half * 512:
                            c * 2048 + 1024 + half * 512 + 512],
                        start=(c == 0), stop=(c == HC - 1)))
            if 'v' in kvp:
              e_kv_evac[0] = V1(lambda h, sl=sl: h.tensor_copy(
                v65_4[:, sl, :, 0:64],
                ps23[:].rearrange("p (h o) -> p h o", o=64)),
                waits=[(sT, ev_mm), (sGc, ev_v65_ones)])
            ev_ro = rope_emit(ps01[:], ktab, sl * 128,
                              [(sT, ev_mm), (sS, ev_ktab_done)],
                              kro[:, r * 1024:(r + 1) * 1024])
            e_kv_evac[1] = ev_ro
            ev_kT_ev = transpose8(
                kro[:, r * 1024:(r + 1) * 1024], True, [(sV, ev_ro)],
                (ktq3[:, 0:4, sl * 128:(sl + 1) * 128],
                 ktq3[:, 4:8, sl * 128:(sl + 1) * 128]),
                (ev_kT_ev[0], ev_kT_ev[1]))
        e_ktq_done = max(ev_kT_ev)
        e_v65_done = e_kv_evac[0]
        ev_ktab_rd = e_kv_evac[1]
        if qi + 1 < quarters:
            nqr = [3, 2, 1, 0][qi + 1]
            for sl2 in (0, 1):
                pre_x[sl2] = dmaG(
                    xk_bf[:, sl2 * 1024:(sl2 + 1) * 1024],
                    d_xseq[(nqr * 8 + sl2) * 128:(nqr * 8 + sl2 + 1) * 128, :],
                    waits=[(sV, ev_xsT[sl2][1])])
        if debug and qr == 0:
            dmaG(dbg["dbg_kt"], ktq[:], waits=[(sV, e_ktq_done)])
            dmaG(dbg["dbg_v65"], v65[:], waits=[(sV, e_v65_done)])

        for J in range(qr, 4):
            for hp in range(HP):
                diag = (J == qr)
                work = [(i, (i // 2 if diag else 0)) for i in range(8)]
                n_av = sum(4 - cu for _, cu in work)
                avp = ps01
                k_av = [0, 0]
                ev_S = [0] * 8
                ev_msk = [0] * 8
                ev_avmm = 0
                dcover = {}

                def emit_av(i, cu):
                    nonlocal ev_avmm
                    th = psv_of[i][0]
                    for sub in (0, 1):
                        hh = hp * 2 + sub
                        for c in range(cu, 4):
                            w2 = ([(sC, dcover[i])]
                                  if (sub == 0 and c == cu) else [])
                            if diag and sub == 0 and c == cu:
                                w2.append((sV, ev_msk[i]))
                            if k_av[sub] == 0:
                                w2.append((sV, ev_av_evac[0]))
                            st = (k_av[sub] == 0)
                            sp = (k_av[sub] == n_av - 1)
                            ev_avmm = T1(lambda h, th=th, sub=sub, c=c,
                                         hh=hh, i=i, st=st, sp=sp, avp=avp:
                                         h.matmul(
                                avp[:, sub * 512 + c * 65:sub * 512 + c * 65 + 65],
                                ptb[:, th * 1024 + sub * 512 + c * 128:
                                    th * 1024 + sub * 512 + (c + 1) * 128],
                                v65[:, i * 1040 + hh * 65:i * 1040 + hh * 65 + 65],
                                start=st, stop=sp, skip_group_check=True), w2)
                            k_av[sub] += 1
                    ev_ptb_war[th] = ev_avmm

                def emit_mask(i, cu):
                    th = psv_of[i][0]
                    ptpair = ptb[:, th * 1024:th * 1024 + 1024].rearrange(
                        "p (s q) -> p s q", s=2)
                    dst = ptpair[:, :, cu * 128:cu * 128 + 128]
                    ev_msk[i] = V1(lambda h, dst=dst, i=i: h.tensor_tensor(
                        dst, dst,
                        masks[:, i * 128:(i + 1) * 128].unsqueeze(1)
                        .broadcast_to([128, 2, 128]), ALU.mult),
                        waits=[(sC, dcover[i])])
                    ev_ptb_msk[th] = ev_msk[i]

                psv_of = {}
                for idx, (i, cu) in enumerate(work):
                    pair = gpair[0]
                    gpair[0] = (pair + 1) % 2
                    th = gthird[0]
                    gthird[0] = (th + 1) % 3
                    psv_pair = st_pairs[pair]
                    psv_of[i] = (th, psv_pair)
                    for sub in (0, 1):
                        w = [(sC, ev_pair_rd[pair])] if sub == 0 else []
                        if idx == 0 and sub == 0:
                            w += [(sV, e_ktq_done), (sV, e_v65_done),
                                  (sV, e_qt_done), (sG, e_masks)]
                        ev_S[i] = T1(lambda h, i=i, cu=cu, sub=sub, J=J, hp=hp,
                                     psv_pair=psv_pair: h.matmul(
                            psv_pair[:, sub * 512 + cu * 128:(sub + 1) * 512],
                            ktq[64 * sub:64 * (sub + 1),
                                hp * 1024 + i * 128:hp * 1024 + (i + 1) * 128],
                            qt[64 * sub:64 * (sub + 1),
                               hp * 2048 + J * 512 + cu * 128:
                               hp * 2048 + J * 512 + 512],
                            start=True, stop=True, tile_position=(64 * sub, 0)), w)
                    if idx >= 2:
                        ip, cup = work[idx - 2]
                        if diag:
                            emit_mask(ip, cup)
                        emit_av(ip, cup)
                    ew = [(sT, ev_S[i]), (sT, ev_ptb_war[th])]
                    if ev_ptb_msk[th]:
                        ew.append((sV, ev_ptb_msk[th]))
                    d = S1(lambda h, th=th, psv_pair=psv_pair: h.activation(
                        ptb[:, th * 1024:(th + 1) * 1024], psv_pair[:],
                        AF.Exp, bias=cst[:, C_NEGC:C_NEGC + 1], scale=0.125), ew)
                    ev_pair_rd[pair] = d
                    dcover[i] = d
                for it in (-2, -1):
                    it7, cu7 = work[it]
                    if diag:
                        emit_mask(it7, cu7)
                    emit_av(it7, cu7)
                for sub in (0, 1):
                    hh = hp * 2 + sub
                    av4 = avp[:, sub * 512:sub * 512 + 260].rearrange(
                        "p (c o) -> p c o", o=65)
                    avden = av4[:, :, 64:65].rearrange("p c o -> p (c o)")
                    avnum = av4[:, :, 0:64]
                    slot = (J - 1) * 16 + hh
                    ev_sl = ev_slot.get(slot, 0)
                    if qr > 0:
                        first = pn_first.setdefault(slot, True)
                        pn_first[slot] = False
                        pslot = pnum3[:, slot, :].rearrange("p (c o) -> p c o", o=64)
                        if first:
                            ev1 = V1(lambda h, pslot=pslot, avnum=avnum:
                                     h.tensor_copy(pslot, avnum),
                                     waits=[(sT, ev_avmm), (sV, ev_sl)])
                            ev2 = V1(lambda h, slot=slot, avden=avden:
                                     h.tensor_copy(pden3[:, slot, :], avden))
                        else:
                            ev1 = V1(lambda h, pslot=pslot, avnum=avnum:
                                     h.tensor_tensor(pslot, avnum, pslot, ALU.add),
                                     waits=[(sT, ev_avmm), (sV, ev_sl)])
                            ev2 = V1(lambda h, slot=slot, avden=avden:
                                     h.tensor_tensor(pden3[:, slot, :], avden,
                                                     pden3[:, slot, :], ALU.add))
                        ev_av_evac[0] = max(ev1, ev2)
                        ev_slot[slot] = max(ev1, ev2)
                    else:
                        dbase = sub * 16
                        if J > 0:
                            evd = V1(lambda h, slot=slot, avden=avden, dbase=dbase:
                                     h.tensor_tensor(dcol[:, dbase:dbase + 4],
                                                     avden, pden3[:, slot, :],
                                                     ALU.add),
                                     waits=[(sT, ev_avmm),
                                            (sV, ev_dcol_free[sub])])
                            evn = V1(lambda h, slot=slot, avnum=avnum, sub=sub:
                                     h.tensor_tensor(
                                fnum[:, sub * 256:(sub + 1) * 256].rearrange(
                                    "p (c o) -> p c o", o=64), avnum,
                                pnum3[:, slot, :].rearrange("p (c o) -> p c o",
                                                            o=64), ALU.add))
                        else:
                            evd = V1(lambda h, avden=avden, dbase=dbase:
                                     h.tensor_copy(dcol[:, dbase:dbase + 4], avden),
                                     waits=[(sT, ev_avmm),
                                            (sV, ev_dcol_free[sub])])
                            evn = V1(lambda h, avnum=avnum, sub=sub: h.tensor_copy(
                                fnum[:, sub * 256:(sub + 1) * 256].rearrange(
                                    "p (c o) -> p c o", o=64), avnum))
                        evr = V1(lambda h, dbase=dbase: h.reciprocal(
                            dcol[:, dbase + 8:dbase + 12],
                            dcol[:, dbase:dbase + 4]), waits=[(sV, evd)])
                        ev_av_evac[0] = max(evn, evd)
                        evdiv = V1(lambda h, J=J, hh=hh, sub=sub, dbase=dbase:
                                   h.tensor_tensor(
                            attn3[:, J * 4:J * 4 + 4, hh * 64:(hh + 1) * 64],
                            fnum[:, sub * 256:(sub + 1) * 256].rearrange(
                                "p (c o) -> p c o", o=64),
                            dcol[:, dbase + 8:dbase + 12].unsqueeze(2)
                            .broadcast_to([128, 4, 64]), ALU.mult),
                            waits=[(sV, evr), (sV, evn)])
                        for c in range(4):
                            e_attn_done[J * 4 + c] = evdiv
                        ev_dcol_free[sub] = evdiv
        last_att_T = sT.n
    ev_pnum_done = sV.n
    e_att_exp_done = max(ev_pair_rd)
    e_attn_all = max(e_attn_done)
    if debug:
        dmaG(dbg["dbg_attn"], attn[:], waits=[(sV, e_attn_all)])

    if phases < 4:
        emit("sync", lambda h: h.nop(), waits=[(sS, sS.n)])
        emit("gpsimd", lambda h: h.nop(), waits=[(sG, sG.n)])
        emit("scalar", lambda h: h.nop(), waits=[(sC, sC.n)])
        emit("vector", lambda h: h.nop(), waits=[(sV, sV.n)])
        with nc.Block() as block:
            for name in ("sync", "gpsimd", "tensor", "vector", "scalar"):
                thunks = stream[name]
                def runner(h, thunks=thunks):
                    for t in thunks:
                        t(h)
                getattr(block, name)(runner)
        es.close()
        return nc
    e_w2t = dmaG(w2t_bf.rearrange("p (c n) -> p c n", c=HC),
                 d_w2t[:].rearrange("(c p) n -> p c n", p=128),
                 waits=[(sV, ev_pnum_done), (sV, e_attn_all)])
    e_wgt = dmaY(wgt_f[:].rearrange("p (c n) -> p c n", c=HC),
                 d_wgt[:].rearrange("(c p) n -> p c n", p=128))

    x2t3 = x2t.rearrange("p (c t) -> p c t", c=HC)
    gu_bufs = (attn[:, 0:4096], attn[:, 4096:8192])

    ev_ln_free = [0, 0]
    ev_yb_free = [0, 0]

    def ln_emit(r1v, ev_r1_v, dst, cbase, waits_dst=()):
        evs1 = V1(lambda h: h.reduce_sum(lncol[:, cbase:cbase + 1], r1v,
                                         mybir.AxisListType.X),
                  waits=[(sV, ev_r1_v), (sC, ev_ln_free[cbase // 8])])
        evs2 = S1(lambda h: h.activation(sq[:], r1v, AF.Square,
                                         accum_out=lncol[:, cbase + 1:cbase + 2]),
                  waits=[(sV, ev_r1_v), (sC, ev_ln_free[cbase // 8])])
        V0(lambda h: h.tensor_scalar(lncol[:, cbase + 2:cbase + 3],
                                     lncol[:, cbase:cbase + 1], I1024, None,
                                     ALU.mult), waits=[(sV, evs1), (sC, evs2)])
        ev_m = V1(lambda h: h.tensor_scalar(lncol[:, cbase + 3:cbase + 4],
                                            lncol[:, cbase + 1:cbase + 2], I1024,
                                            None, ALU.mult))
        V0(lambda h: h.tensor_mul(lncol[:, cbase + 4:cbase + 5],
                                  lncol[:, cbase + 2:cbase + 3],
                                  lncol[:, cbase + 2:cbase + 3]),
           waits=[(sV, ev_m)])
        ev_mq = V1(lambda h: h.tensor_scalar(lncol[:, cbase + 5:cbase + 6],
                                             lncol[:, cbase + 2:cbase + 3], -1.0,
                                             None, ALU.mult))
        ev_var = V1(lambda h: h.tensor_sub(lncol[:, cbase + 3:cbase + 4],
                                           lncol[:, cbase + 3:cbase + 4],
                                           lncol[:, cbase + 4:cbase + 5]),
                    waits=[(sV, ev_mq)])
        ev_std = S1(lambda h: h.activation(lncol[:, cbase + 4:cbase + 5],
                                           lncol[:, cbase + 3:cbase + 4], AF.Sqrt,
                                           bias=cst[:, C_EPS:C_EPS + 1]),
                    waits=[(sV, ev_var)])
        ev_rstd = V1(lambda h: h.reciprocal(lncol[:, cbase + 6:cbase + 7],
                                            lncol[:, cbase + 4:cbase + 5]),
                     waits=[(sC, ev_std)])
        ev_p1 = S1(lambda h: h.activation(sq[:], r1v, AF.Identity,
                                          bias=lncol[:, cbase + 5:cbase + 6]),
                   waits=[(sV, ev_rstd)])
        ev = S1(lambda h: h.activation(dst, sq[:], AF.Identity,
                                       scale=lncol[:, cbase + 6:cbase + 7]),
                waits=[(sC, ev_p1)] + list(waits_dst))
        ev_ln_free[cbase // 8] = ev
        return ev, ev_rstd

    def gate_batch(half, ev_sig_last, prev_gate_ev):
        gv = [gsc[:, t * 16:(t + 1) * 16] for t in range(8)]
        wv = [gw[:, t * 128:(t + 1) * 128] for t in range(8)]
        P_, GS, GS2, CNT, ALW, MSK, RNK, SEL, WR, DEN = \
            0, 24, 28, 36, 40, 44, 60, 76, 92, 108
        pairs = [(0, 1), (0, 2), (0, 3), (1, 2), (1, 3), (2, 3)]

        def layer(fn, waits=()):
            for t in range(8):
                (V1 if t == 7 else V0)(fn(t), waits=list(waits) if t == 0 else ())
            return sV.n

        def l1(t):
            def f(h, t=t):
                for k, (a, b) in enumerate(pairs):
                    h.tensor_tensor(wv[t][:, P_ + k * 4:P_ + (k + 1) * 4],
                                    gv[t][:, a:16:4], gv[t][:, b:16:4], ALU.add)
            return f
        ev = layer(l1, [(sC, ev_sig_last), (sV, prev_gate_ev)])
        ev = layer(lambda t: (lambda h, t=t: h.tensor_copy(
            wv[t][:, GS:GS + 4], wv[t][:, P_:P_ + 4])), [(sV, ev)])
        for k in range(1, 6):
            ev = layer(lambda t, k=k: (lambda h, t=t, k=k: h.tensor_max(
                wv[t][:, GS:GS + 4], wv[t][:, GS:GS + 4],
                wv[t][:, P_ + k * 4:P_ + (k + 1) * 4])), [(sV, ev)])
        ev = layer(lambda t: (lambda h, t=t: h.tensor_copy(
            wv[t][:, GS2:GS2 + 4], wv[t][:, GS:GS + 4])), [(sV, ev)])
        ev = layer(lambda t: (lambda h, t=t: h.tensor_copy(
            wv[t][:, GS2 + 4:GS2 + 8], wv[t][:, GS:GS + 4])), [(sV, ev)])
        ev = layer(lambda t: (lambda h, t=t: h.tensor_tensor(
            wv[t][:, CNT:CNT + 4], wv[t][:, GS2 + 1:GS2 + 5],
            wv[t][:, GS:GS + 4], ALU.is_gt)), [(sV, ev)])
        for k in (2, 3):
            ev = layer(lambda t, k=k: (lambda h, t=t, k=k: h.tensor_tensor(
                wv[t][:, ALW:ALW + 4], wv[t][:, GS2 + k:GS2 + k + 4],
                wv[t][:, GS:GS + 4], ALU.is_gt)), [(sV, ev)])
            ev = layer(lambda t: (lambda h, t=t: h.tensor_add(
                wv[t][:, CNT:CNT + 4], wv[t][:, CNT:CNT + 4],
                wv[t][:, ALW:ALW + 4])), [(sV, ev)])
        ev = layer(lambda t: (lambda h, t=t: h.tensor_scalar(
            wv[t][:, ALW:ALW + 4], wv[t][:, CNT:CNT + 4], 1.5, None, ALU.is_le)),
            [(sV, ev)])
        ev = layer(lambda t: (lambda h, t=t: h.scalar_tensor_tensor(
            wv[t][:, MSK:MSK + 16].rearrange("p (a b) -> p a b", a=4),
            wv[t][:, ALW:ALW + 4].unsqueeze(2).broadcast_to([128, 4, 4]),
            4.0, gv[t][:].rearrange("p (a b) -> p a b", a=4),
            ALU.mult, ALU.add)), [(sV, ev)])
        ev = layer(lambda t: (lambda h, t=t: h.tensor_scalar(
            wv[t][:, MSK:MSK + 16], wv[t][:, MSK:MSK + 16], -4.0, None, ALU.add)),
            [(sV, ev)])
        for t in range(8):
            ev_c = V1(lambda h, t=t: h.tensor_tensor(
                gcmp[:].rearrange("p (a b) -> p a b", a=16),
                wv[t][:, MSK:MSK + 16].unsqueeze(1).broadcast_to([128, 16, 16]),
                wv[t][:, MSK:MSK + 16].unsqueeze(2).broadcast_to([128, 16, 16]),
                ALU.is_gt), waits=[(sV, ev)])
            ev = V1(lambda h, t=t: h.reduce_sum(
                wv[t][:, RNK:RNK + 16],
                gcmp[:].rearrange("p (a b) -> p a b", a=16),
                mybir.AxisListType.X), waits=[(sV, ev_c)])
        ev = layer(lambda t: (lambda h, t=t: h.tensor_scalar(
            wv[t][:, SEL:SEL + 16], wv[t][:, RNK:RNK + 16], 3.5, None, ALU.is_le)),
            [(sV, ev)])
        ev = layer(lambda t: (lambda h, t=t: h.tensor_mul(
            wv[t][:, WR:WR + 16], wv[t][:, SEL:SEL + 16], gv[t][:])), [(sV, ev)])
        ev = layer(lambda t: (lambda h, t=t: h.reduce_sum(
            wv[t][:, DEN:DEN + 1], wv[t][:, WR:WR + 16],
            mybir.AxisListType.X)), [(sV, ev)])
        ev = layer(lambda t: (lambda h, t=t: h.tensor_scalar(
            wv[t][:, DEN + 1:DEN + 2], wv[t][:, DEN:DEN + 1], 1e-6, None,
            ALU.add)), [(sV, ev)])
        ev_rc = layer(lambda t: (lambda h, t=t: h.reciprocal(
            wv[t][:, DEN + 2:DEN + 3], wv[t][:, DEN + 1:DEN + 2])), [(sV, ev)])
        ev = layer(lambda t: (lambda h, t=t, half=half: h.tensor_copy(
            comb[:, (half * 8 + t) * NE:(half * 8 + t) * NE + 16],
            wv[t][:, WR:WR + 16])), [(sV, ev_rc)])
        ev = layer(lambda t: (lambda h, t=t, half=half: h.tensor_scalar(
            comb[:, (half * 8 + t) * NE:(half * 8 + t) * NE + 16],
            comb[:, (half * 8 + t) * NE:(half * 8 + t) * NE + 16],
            wv[t][:, DEN + 2:DEN + 3], None, ALU.mult)), [(sV, ev)])
        return ev

    ev_xow = [0, 0]
    ev_xow = [0, 0]
    ev_r1_used = [0, 0]
    ev_r1_scal = [0, 0]
    ev_gate_all = 0
    e_ffn_prev_half_v = 0
    e_ffn_prev_half_T = last_att_T
    ev_w13_free = [last_att_T, last_att_T]
    ev_dn_last = [last_att_T, last_att_T]
    ev_gu_done = [0, 0]
    e_w13 = [0, 0]
    e_wld2 = [0, 0]

    for half in range(2):
        ev_atT = [[0, 0], [0, 0]]
        ev_op_evac = [0, 0]
        e_x2t_done = [0, 0]
        e_x1_done = [0] * 8
        e_sig = [0] * 8
        e_sig_last = 0
        for tl in range(8):
            tt = half * 8 + tl
            r = tl % 2
            ev_xow[r] = dmaY(r1b[:, r * 1024:(r + 1) * 1024],
                             d_xown[tt * 128:(tt + 1) * 128, :],
                             waits=[(sV, ev_r1_used[r]), (sC, ev_r1_scal[r]),
                                    (sOut, ev_yb_free[r])])
            ev_atT[r] = transpose8(
                attn[:, tt * 1024:(tt + 1) * 1024], True,
                [(sV, e_attn_done[tt]), (sV, ev_pnum_done), (sV, ev_atT[r][1])] +
                ([(sC, e_att_exp_done)] if (half == 0 and tl == 0) else []),
                (atT[:, r * 1024:r * 1024 + 512],
                 atT[:, r * 1024 + 512:r * 1024 + 1024]),
                (e_x2t_done[0], e_x2t_done[1]))
            opp = ps01 if r == 0 else ps23
            ev_mm = 0
            w = [(sV, ev_atT[r][1]), (sG, e_w2t), (sV, ev_op_evac[r])]
            for c in range(HC):
                for hf in range(2):
                    ev_mm = T1(lambda h, r=r, c=c, hf=hf, opp=opp: h.matmul(
                        opp[:, hf * 512:(hf + 1) * 512],
                        atT[:, r * 1024 + c * 128:r * 1024 + (c + 1) * 128],
                        w2t_bf[:, c * 1024 + hf * 512:c * 1024 + hf * 512 + 512],
                        start=(c == 0), stop=(c == HC - 1)), w)
                    w = []
            ev_r1 = V1(lambda h, r=r, opp=opp: h.tensor_add(
                r1b[:, r * 1024:(r + 1) * 1024],
                r1b[:, r * 1024:(r + 1) * 1024], opp[:]),
                waits=[(sT, ev_mm), (sS, ev_xow[r])])
            ev_op_evac[r] = ev_r1
            wdst = []
            if tl == 0:
                wdst = [(sT, e_ffn_prev_half_T), (sV, e_ffn_prev_half_v)]
            ev_x1, ev_r1v_rd = ln_emit(r1b[:, r * 1024:(r + 1) * 1024], ev_r1,
                                       x1res[:, tl * 1024:(tl + 1) * 1024],
                                       r * 8, waits_dst=wdst)
            e_x1_done[tl] = ev_x1
            ev_r1_scal[r] = ev_x1
            ev_r1_used[r] = ev_r1v_rd
            ev_tx = transpose8(
                x1res[:, tl * 1024:(tl + 1) * 1024], False,
                [(sC, ev_x1)],
                (x2t3[:, 0:4, tl * 128:(tl + 1) * 128],
                 x2t3[:, 4:8, tl * 128:(tl + 1) * 128]),
                (e_x2t_done[0], e_x2t_done[1]),
                extra_evac=(x1t_f[:, 0:512], x1t_f[:, 512:1024]))
            e_x2t_done = ev_tx
            ev_gmm = 0
            w = [(sV, ev_tx[1]), (sS, e_wgt), (sC, e_sig_last)]
            for c in range(HC):
                ev_gmm = T1(lambda h, c=c: h.matmul(
                    ps6[:, 0:16], x1t_f[:, c * 128:(c + 1) * 128],
                    wgt_f[:, c * 16:(c + 1) * 16], start=(c == 0),
                    stop=(c == HC - 1)), w)
                w = []
            e_sig[tl] = S1(lambda h, tl=tl: h.activation(
                gsc[:, tl * 16:(tl + 1) * 16], ps6[:, 0:16], AF.Sigmoid),
                waits=[(sT, ev_gmm), (sV, ev_gate_all)])
            e_sig_last = e_sig[tl]
        ev_gate_all = gate_batch(half, e_sig_last, ev_gate_all)
        e_comb_all = ev_gate_all
        if debug and half == 0:
            dmaG(dbg["dbg_x1"], x1res[:, 0:8192], waits=[(sC, max(e_x1_done))])
            dmaG(dbg["dbg_x2t"], x2t[:, 0:8192], waits=[(sV, max(e_x2t_done))])

        ev_dn_evac = [ev_op_evac[0], ev_op_evac[1]]
        ev_x1_acc = list(e_x1_done)
        ev_guw = 0
        for e in range(min(2, NE)):
            pe = e % 2
            dmaG(w13[:, pe * 4096:(pe + 1) * 4096]
                 .rearrange("p (c n) -> p c n", c=HC),
                 d_w1a[e].rearrange("(c p) n -> p c n", p=128),
                 waits=[(sT, ev_w13_free[pe])])
            e_w13[pe] = dmaG(
                w13[:, 8192 + pe * 4096:8192 + (pe + 1) * 4096]
                .rearrange("p (c n) -> p c n", c=HC),
                d_w3a[e].rearrange("(c p) n -> p c n", p=128))
            e_wld2[pe] = dmaG(
                w2bf[:, pe * 4096:(pe + 1) * 4096]
                .rearrange("p (c n) -> p c n", c=4),
                d_w2a[e].rearrange("(c p) n -> p c n", p=128),
                waits=[(sT, ev_dn_last[pe])])

        def emit_down(e):
            pe = e % 2
            final = (e == NE - 1)
            for tl in range(8):
                tt = half * 8 + tl
                dnp = ps01 if tl % 2 == 0 else ps23
                ev_mm = 0
                w = [(sV, ev_dn_evac[tl % 2])]
                if tl == 0:
                    w += [(sV, ev_gu_done[pe]), (sG, e_wld2[pe])]
                for fc in range(4):
                    for hf in range(2):
                        ev_mm = T1(lambda h, pe=pe, fc=fc, hf=hf, tl=tl, dnp=dnp:
                                   h.matmul(
                            dnp[:, hf * 512:(hf + 1) * 512],
                            gu_bufs[pe][:, fc * 1024 + tl * 128:
                                        fc * 1024 + (tl + 1) * 128],
                            w2bf[:, pe * 4096 + fc * 1024 + hf * 512:
                                 pe * 4096 + fc * 1024 + hf * 512 + 512],
                            start=(fc == 0), stop=(fc == 3)), w)
                        w = []
                ev_acc = V1(lambda h, tl=tl, tt=tt, e=e, dnp=dnp:
                            h.scalar_tensor_tensor(
                    x1res[:, tl * 1024:(tl + 1) * 1024], dnp[:],
                    comb[:, tt * NE + e:tt * NE + e + 1],
                    x1res[:, tl * 1024:(tl + 1) * 1024], ALU.mult, ALU.add),
                    waits=[(sT, ev_mm), (sC, e_x1_done[tl]),
                           (sV, e_comb_all),
                           (sV, ev_x1_acc[tl])] +
                          ([(sGc, e_comb1)] if e >= 16 else []))
                ev_dn_evac[tl % 2] = ev_acc
                ev_x1_acc[tl] = ev_acc
                if final:
                    r = tl % 2
                    ev_y, _ = ln_emit(x1res[:, tl * 1024:(tl + 1) * 1024],
                                      ev_x1_acc[tl],
                                      yb[:, r * 1024:(r + 1) * 1024],
                                      r * 8, waits_dst=[(sOut, ev_yb_free[r])])
                    ev_yb_free[r] = dmaY(d_y[tt * 128:(tt + 1) * 128, :],
                                         yb[:, r * 1024:(r + 1) * 1024],
                                         waits=[(sC, ev_y)], store=True)
            ev_dn_last[pe] = sT.n
            if e + 2 < NE:
                e_wld2[pe] = dmaG(
                    w2bf[:, pe * 4096:(pe + 1) * 4096]
                    .rearrange("p (c n) -> p c n", c=4),
                    d_w2a[e + 2].rearrange("(c p) n -> p c n", p=128),
                    waits=[(sT, ev_dn_last[pe])])

        for e in range(NE):
            pe = e % 2
            guv = gu_bufs[pe]
            ev_sil = 0
            for fc in range(4):
                for tc in range(2):
                    w = [(sG, e_w13[pe]), (sC, ev_sil), (sV, ev_guw)]
                    if fc == 0 and tc == 0 and e < 2:
                        w += [(sV, max(e_x2t_done)), (sC, e_sig_last)]
                    ev_g = 0
                    for c in range(HC):
                        ev_g = T1(lambda h, pe=pe, fc=fc, tc=tc, c=c: h.matmul(
                            ps6, w13[:, pe * 4096 + c * 512 + fc * 128:
                                        pe * 4096 + c * 512 + (fc + 1) * 128],
                            x2t[:, c * 1024 + tc * 512:c * 1024 + (tc + 1) * 512],
                            start=(c == 0), stop=(c == HC - 1)), w)
                        w = []
                    ev_u = 0
                    for c in range(HC):
                        ev_u = T1(lambda h, pe=pe, fc=fc, tc=tc, c=c: h.matmul(
                            ps7, w13[:, 8192 + pe * 4096 + c * 512 + fc * 128:
                                        8192 + pe * 4096 + c * 512 + (fc + 1) * 128],
                            x2t[:, c * 1024 + tc * 512:c * 1024 + (tc + 1) * 512],
                            start=(c == 0), stop=(c == HC - 1)))
                    ev_sil = S1(lambda h, tc=tc: h.activation(
                        silu_t[:, tc * 512:(tc + 1) * 512], ps6, AF.Silu),
                        waits=[(sT, ev_g), (sV, ev_guw)])
                    wgu = [(sC, ev_sil), (sT, ev_u)]
                    if fc == 0 and tc == 0:
                        wgu.append((sT, ev_dn_last[pe]))
                    ev_guw = V1(lambda h, guv=guv, fc=fc, tc=tc: h.tensor_tensor(
                        guv[:, fc * 1024 + tc * 512:fc * 1024 + (tc + 1) * 512],
                        silu_t[:, tc * 512:(tc + 1) * 512], ps7, ALU.mult),
                        waits=wgu)
            ev_gu_done[pe] = ev_guw
            ev_w13_free[pe] = sT.n
            if e >= 1:
                emit_down(e - 1)
            if e + 2 < NE:
                dmaG(w13[:, pe * 4096:(pe + 1) * 4096]
                     .rearrange("p (c n) -> p c n", c=HC),
                     d_w1a[e + 2].rearrange("(c p) n -> p c n", p=128),
                     waits=[(sT, ev_w13_free[pe])])
                e_w13[pe] = dmaG(
                    w13[:, 8192 + pe * 4096:8192 + (pe + 1) * 4096]
                    .rearrange("p (c n) -> p c n", c=HC),
                    d_w3a[e + 2].rearrange("(c p) n -> p c n", p=128))
        emit_down(NE - 1)
        if debug and half == 0:
            dmaG(dbg["dbg_comb"], comb[:, 0:8 * NE], waits=[(sV, e_comb_all)])
        e_ffn_prev_half_v = sV.n
        e_ffn_prev_half_T = sT.n

    emit("sync", lambda h: h.nop(), waits=[(sOut, sOut.n)])

    with nc.Block() as block:
        for name in ("sync", "gpsimd", "tensor", "vector", "scalar"):
            thunks = stream[name]

            def runner(h, thunks=thunks):
                for t in thunks:
                    t(h)
            getattr(block, name)(runner)
    es.close()
    return nc


_CACHE = {}


def _rope_table():
    if "rope" in _CACHE:
        return _CACHE["rope"]
    import jax
    import jax.numpy as jnp
    with jax.default_device(jax.devices("cpu")[0]):
        inv_freq = 1.0 / 10000.0 ** (jnp.arange(0, D, 2, dtype=jnp.float32) / D)
        ang = jnp.arange(T, dtype=jnp.float32)[:, None] * inv_freq[None, :]
        cosv = np.asarray(jax.device_get(jnp.cos(ang)), np.float32)
        sinv = np.asarray(jax.device_get(jnp.sin(ang)), np.float32)
    tab = np.concatenate([cosv, cosv, -sinv, sinv], axis=1)
    _CACHE["rope"] = np.ascontiguousarray(tab, dtype=np.float32)
    return _CACHE["rope"]


def _prep_host(inputs):
    f32 = np.float32
    x = np.ascontiguousarray(inputs["L0_attn_x"], dtype=f32)
    w1t = np.ascontiguousarray(np.asarray(inputs["L0_attn_W1"], f32).T)
    w2t = np.ascontiguousarray(np.asarray(inputs["L0_attn_W2"], f32).T)
    wgt = np.ascontiguousarray(np.asarray(inputs["L0_ffn_w_gate"], f32).T)
    w1e = np.asarray(inputs["L0_ffn_w1_e"], f32)
    w3e = np.asarray(inputs["L0_ffn_w3_e"], f32)
    w2e = np.asarray(inputs["L0_ffn_w2_e"], f32)
    w1s = np.asarray(inputs["L0_ffn_w1_s"], f32)
    w3s = np.asarray(inputs["L0_ffn_w3_s"], f32)
    w2s = np.asarray(inputs["L0_ffn_w2_s"], f32)
    w1a = np.ascontiguousarray(np.concatenate(
        [w1e.transpose(0, 2, 1), w1s.T.reshape(E, 2, FF).transpose(1, 0, 2)], 0))
    w3a = np.ascontiguousarray(np.concatenate(
        [w3e.transpose(0, 2, 1), w3s.T.reshape(E, 2, FF).transpose(1, 0, 2)], 0))
    w2a = np.ascontiguousarray(np.concatenate(
        [w2e.transpose(0, 2, 1), w2s.T.reshape(2, FF, E)], 0))
    tri = np.triu(np.ones((128, 128), f32))
    ones = np.ones((128, 128), f32)
    zeros = np.zeros((128, 128), f32)
    masks_p = [
        np.ascontiguousarray(np.stack([tri if i % 2 == 0 else zeros
                                       for i in range(8)])),
        np.ascontiguousarray(np.stack([ones if i % 2 == 0 else tri
                                       for i in range(8)])),
    ]
    rope = _rope_table()
    in_maps, metas = [], []
    for core in range(8):
        b, p = core // 2, core % 2
        blocks = np.arange(NTO) * 2 + p
        rows = (blocks[:, None] * 128 + np.arange(128)[None, :]).reshape(-1)
        in_maps.append({
            "xseq": np.ascontiguousarray(x[b]),
            "xown": np.ascontiguousarray(x[b][rows]),
            "ropeq": np.ascontiguousarray(rope[rows]),
            "ropek": rope,
            "masks": masks_p[p],
            "w1t": w1t, "w2t": w2t, "wgt": wgt,
            "w1a": w1a, "w3a": w3a, "w2a": w2a,
        })
        metas.append((b, blocks))
    return in_maps, metas


def kernel(L0_attn_x, L0_attn_W1, L0_attn_W2, L0_ffn_w_gate, L0_ffn_w1_e,
           L0_ffn_w3_e, L0_ffn_w2_e, L0_ffn_w1_s, L0_ffn_w3_s, L0_ffn_w2_s):
    inputs = dict(L0_attn_x=L0_attn_x, L0_attn_W1=L0_attn_W1,
                  L0_attn_W2=L0_attn_W2, L0_ffn_w_gate=L0_ffn_w_gate,
                  L0_ffn_w1_e=L0_ffn_w1_e, L0_ffn_w3_e=L0_ffn_w3_e,
                  L0_ffn_w2_e=L0_ffn_w2_e, L0_ffn_w1_s=L0_ffn_w1_s,
                  L0_ffn_w3_s=L0_ffn_w3_s, L0_ffn_w2_s=L0_ffn_w2_s)
    if "nc" not in _CACHE:
        _CACHE["nc"] = build_nc(debug=False)
    nc = _CACHE["nc"]
    in_maps, metas = _prep_host(inputs)
    res = run_bass_kernel_spmd(nc, in_maps, core_ids=list(range(8)))
    out = np.empty((B, T, E), dtype=np.float32)
    for core in range(8):
        b, blocks = metas[core]
        y = np.asarray(res.results[core]["y"]).reshape(NTO, 128, E)
        out.reshape(B, 32, 128, E)[b, blocks] = y
    return out
